# version 12
# baseline (speedup 1.0000x reference)
import numpy as np
import ml_dtypes
import concourse.bass as bass
import concourse.mybir as mybir
from concourse.bass_utils import run_bass_kernel_spmd
from contextlib import ExitStack

F32 = mybir.dt.float32
BF16 = mybir.dt.bfloat16
AF = mybir.ActivationFunctionType
ALU = mybir.AluOpType
AX = mybir.AxisListType
BF = ml_dtypes.bfloat16

S, D, H = 4096, 1024, 4096
NT, ND, NH, QB = 32, 8, 32, 32
SCALE = 1.0 / 32.0
LN2 = float(np.log(2.0))
INVLN2 = float(1.0 / np.log(2.0))
NEG = -1e30


class Sem:
    def __init__(self, h):
        self.h = h
        self.n = 0

    def bump(self, k=1):
        self.n += k
        return self.n


class Em:

    def __init__(self, selfsems=None):
        self.ops = {k: [] for k in ("sync", "vector", "scalar", "tensor", "gpsimd")}
        self.selfsems = selfsems or {}

    SELF_SERIALIZE = False

    def emit(self, eng, fn, waits=(), inc=None, inc_by=1, ser=False):
        selfsem = self.selfsems.get(eng)
        if not (ser or self.SELF_SERIALIZE):
            selfsem = None
        pre = []
        if selfsem is not None:
            is_dma = inc is not None and inc is not selfsem
            pre = [(selfsem.h, selfsem.n)]
            if not is_dma:
                assert inc is None or inc is selfsem
                inc = selfsem
                inc_by = 1
        if inc is None:
            ms = self.selfsems.get(eng)
            if ms is not None:
                inc = ms
                inc_by = 1
        waits = pre + [(s.h, v) for (s, v) in waits if v > 0]
        best = {}
        for h, v in waits:
            if v > best.get(h, 0):
                best[h] = v
        waits = tuple(best.items())
        ih = inc.h if inc is not None else None
        if inc is not None:
            inc.bump(inc_by)

        def run(e, fn=fn, waits=waits, ih=ih, inc_by=inc_by):
            for h, v in waits:
                e.wait_ge(h, v)
            ins = fn(e)
            if ih is not None:
                ins.then_inc(ih, inc_by)
        self.ops[eng].append(run)


def build_nc(debug_taps=False):
    nc = bass.Bass("TRN2", target_bir_lowering=False)

    def din(name, shape, dtype=F32):
        return nc.dram_tensor(name, shape, dtype, kind="ExternalInput").ap()

    x_d = din("x", [S, D])
    Wq_d = din("Wq", [D, D]); Wk_d = din("Wk", [D, D]); Wv_d = din("Wv", [D, D])
    Wo_d = din("Wo", [D, D])
    bqt_d = din("bqt", [128, ND]); bkt_d = din("bkt", [128, ND]); bvt_d = din("bvt", [128, ND])
    bo_d = din("bo_r", [1, D])
    wfc_d = din("w_fc", [H, D]); bfc_d = din("b_fc_r", [1, H])
    wpr_d = din("w_proj", [D, H]); bpr_d = din("b_proj_r", [1, D])
    ident_d = din("ident", [128, 128], BF16)
    cmask_d = din("cmask", [128, 4, 512], BF16)
    onc_d = din("ones_col", [128, 1], BF16)
    onr_d = din("ones_row", [1, 128], BF16)

    y_d = nc.dram_tensor("y", [S, D], F32, kind="ExternalOutput").ap()

    def dsc(name, shape, dtype=BF16):
        return nc.dram_tensor(name, shape, dtype, kind="Internal").ap()

    qt_s = dsc("qt_s", [ND, QB, 128, 128])
    xqt_s = dsc("xqt_s", [ND, NT, 128, 128])
    at_s = dsc("at_s", [ND, QB, 128, 128])
    v_s = dsc("v_s", [NT, ND, 128, 128])
    vc_s = dsc("vc_s", [8, ND, 128, 128])
    wfct_s = dsc("wfct_s", [ND, NT, 128, 128])
    wprt_s = dsc("wprt_s", [NH, ND, 128, 128])
    hg_s = dsc("hg_s", [NT, 128, H])

    taps = {}
    if debug_taps:
        def mtap(name, shape, dtype=BF16):
            taps[name] = nc.dram_tensor(name, shape, dtype, kind="ExternalOutput").ap()
        mtap("t_xnT", [128, ND * S]); mtap("t_KT", [128, ND * S])
        mtap("t_KcT", [128, ND * 1024]); mtap("t_rawb", [128, S])
        mtap("t_vals", [128, 64]); mtap("t_attn", [128, D])
        mtap("t_wfcq", [128, D]); mtap("t_hg", [128, H])
        mtap("t_sc", [128, NT * 128]); mtap("t_xq", [128, D])
        mtap("t_vtb", [128, 512])
        mtap("t_vps", [128, 512], F32)
        mtap("t_tr0", [128, 128])
        mtap("t_wv0", [128, 1024]); mtap("t_wq0", [128, 1024])
        mtap("t_wv1", [128, 1024]); mtap("t_wv2", [128, 1024])
        mtap("t_sstat", [128, 16], F32); mtap("t_pcb", [128, 1024])
        mtap("t_csc", [128, 1024]); mtap("t_vsum", [1, D])


    em = Em()
    with ExitStack() as ctx:
        def sbt(name, shape, dtype=F32):
            return ctx.enter_context(nc.sbuf_tensor(name, shape, dtype))

        def pst(name, shape, dtype=F32):
            return ctx.enter_context(nc.psum_tensor(name, shape, dtype))

        def mksem(name):
            return Sem(ctx.enter_context(nc.semaphore(name)))

        sDS = mksem("sDS"); sDP = mksem("sDP")
        sT = mksem("sT"); sV = mksem("sV"); sC = mksem("sC"); sG = mksem("sG")
        sDX = mksem("sDX"); sDQ = mksem("sDQ"); sDV = mksem("sDV")
        sDC = [mksem("sDC0"), mksem("sDC1")]
        sDH = mksem("sDH"); sDAL = mksem("sDAL"); sDA = mksem("sDA")
        sDW = mksem("sDW"); sDY = mksem("sDY")
        sDWp = [mksem("sDWa"), mksem("sDWb")]
        sVS = mksem("sVS"); sVC = mksem("sVC")
        sDC2 = [mksem("sDC2a"), mksem("sDC2b")]
        sDQ2 = mksem("sDQ2")
        em.__init__(selfsems={"vector": sV, "scalar": sC, "gpsimd": sG})

        R1 = sbt("R1", [128, 32768], BF16)
        R2 = sbt("R2", [128, 32768], BF16)
        R3 = sbt("R3", [128, 32768], BF16)
        FA = sbt("FA", [128, 3072], F32)

        def r3(lo, n):
            return R3[:, lo:lo + n]

        def r3v(lo, nt, n):
            return R3[:, lo:lo + nt * n].rearrange("p (t n) -> p t n", n=n)

        xnT = R1[:].rearrange("p (t n) -> p t n", n=S)
        V_sb = R1[:].rearrange("p (t n) -> p t n", n=D)
        wfcT = xnT
        bo_rowb = R1[0:1, 0:D]
        bpr_rowb = R1[0:1, D:2 * D]
        KT = R2[:].rearrange("p (t n) -> p t n", n=S)
        wprT = R2[:].rearrange("p (t n) -> p t n", n=D)
        bfc_rowb = R2[0:1, 0:H]

        wq_bp = [r3(0, 1024), r3(1024, 1024)]
        wq_stp = [r3v(2048, 8, 128), r3v(3072, 8, 128)]
        rndW = r3(4096, 1024)
        xnb = r3(0, 1024); xqb = r3(1024, 1024); rx_b = r3(2048, 1024)
        ttile = r3v(3072, ND, 128)
        KcT = r3v(0, ND, 1024)
        rawb = r3(8192, 4096)
        t1a = r3(12288, 1024); t1b = r3(13312, 1024)
        g8 = [r3(14336, 512), r3(14848, 512)]
        wdiag = r3(12288, 512)
        Cb = [r3(12800, 512), r3(13312, 512)]
        vcs = [r3(13824, 512), r3(14336, 512)]
        coef_row = R3[0:1, 14848:14976]
        vals_b = r3(12288, 64)
        vals_dbg = r3(12352, 64)
        qsl = r3v(15360, ND, 128)
        vcs8 = [r3(16384 + i * 1024, 1024) for i in range(4)]
        ctp = [r3v(13824, 4, 128), r3v(14336, 4, 128)]
        qsls = [r3v(15360, 8, 128), r3v(24576, 8, 128)]
        vsum_rowb = R3[0:1, 25600:26624]
        cmask = R3[:, 26624:28672].rearrange("p (t n) -> p t n", n=512)
        c_sc = r3(28672, 1024)
        Pcb = r3(29696, 1024)
        PcT = r3v(30720, 8, 128)
        mselc = [r3(31744, 512), r3(32256, 512)]
        abuf = r3(31744, 1024)
        atile = r3v(12288, ND, 128)
        hgb = r3(8192, 4096)
        amax_all = r3(16384, 4096)
        amax_v = r3v(16384, NT, 128)
        rnd15 = r3(20480, 1024)
        sc_all = r3v(21504, NT, 128)
        rsc_all = r3v(25600, NT, 128)
        Wo_sb = r3v(0, ND, 1024)
        hq_b = r3(12288, 4096)
        hqT = r3v(16384, NH, 128)
        asl = r3v(20480, ND, 128)
        Wq_sb = r3v(8192, ND, 1024)
        Wk_sb = r3v(16384, ND, 1024)
        Wv_sb = r3v(24576, ND, 1024)

        FA0 = FA[:, 0:1024]; FA1 = FA[:, 1024:2048]; FA2 = FA[:, 2048:3072]
        wstage = FA[:, 0:1536]
        xtile = FA[:, 0:1024]
        xn_f = FA[:, 1024:2048]
        tx_f = FA[:, 2048:3072]
        Ebuf = [FA[:, 0:512], FA[:, 512:1024],
                FA[:, 1024:1536], FA[:, 1536:2048]]
        Ewb = [FA[:, 2048:2560], FA[:, 2560:3072]]
        th_c = FA[:, 0:1024]
        xtile2 = FA[:, 1024:2048]
        ybuf = FA[:, 2048:3072]
        brow_st = FA[0:1, 0:2048]
        f15 = (FA[:, 0:1024], FA[:, 1024:2048], FA[:, 2048:3072])

        ident = sbt("identsb", [128, 128], BF16)
        onc = sbt("onc", [128, 1], BF16)
        onr = sbt("onr", [1, 128], BF16)
        bq_sb = sbt("bq_sb", [128, ND]); bk_sb = sbt("bk_sb", [128, ND])
        bv_sb = sbt("bv_sb", [128, ND]); bqs_sb = sbt("bqs_sb", [128, ND])
        amaxw = sbt("amaxw", [128, 32]); scw = sbt("scw", [128, 32], BF16)
        rscw = sbt("rscw", [128, 32])
        tmpw = [sbt(f"tmpw{i}", [128, 32]) for i in range(3)]
        rndw = sbt("rndw", [128, 32], BF16)
        trev = [sbt(f"trev{i}", [128, 128], BF16) for i in range(2)]
        bnst = sbt("bnst", [128, 12]); bnag = sbt("bnag", [128, 2])
        vc_bf = sbt("vc_bf", [128, 128], BF16)
        vt_buf = sbt("vt_buf", [128, 512], BF16)
        sstat = sbt("sstat", [128, 16])
        tauf = sbt("tauf", [128, 2])
        swp = sbt("swp", [128, 8]); gsum = sbt("gsum", [128, 8])
        scp = sbt("scp", [128, 2])
        coefb = sbt("coefb", [128, 1], BF16)
        negc = sbt("negc", [128, 1])
        stat = sstat
        amaxx, scx, rscx, tmpx, rndx = amaxw, scw, rscw, tmpw, rndw
        vc_red = FA[:, 0:128]
        vpstap = FA[:, 1024:1536]

        pA = pst("pA", [128, 512]); pB = pst("pB", [128, 512])
        pC_ = pst("pC_", [128, 512]); pD = pst("pD", [128, 512])
        pACC = [pst("pACC0", [128, 512]), pst("pACC1", [128, 512])]
        pT1 = pst("pT1", [128, 512], BF16); pT2 = pst("pT2", [128, 512], BF16)

        def pts(s):
            bank = pT1 if (s % 8) < 4 else pT2
            q = s % 4
            return bank[:, q * 128:(q + 1) * 128]

        for dst, src in [(ident, ident_d), (onc, onc_d), (onr, onr_d),
                         (bq_sb, bqt_d), (bk_sb, bkt_d), (bv_sb, bvt_d)]:
            em.emit("sync", lambda e, d=dst, s=src: e.dma_start(out=d[:], in_=s),
                    inc=sDS, inc_by=16)
        v_consts = sDS.n
        em.emit("vector", lambda e: e.tensor_scalar(
            out=bqs_sb[:], in0=bq_sb[:], scalar1=SCALE, scalar2=None, op0=ALU.mult),
            waits=[(sDS, v_consts)], inc=sV)

        def scale_chain(G, amax_ap, sc_ap, rsc_ap, tmp, rnd_ap, wait_v, waits0=()):
            t0, t1, t2 = tmp
            em.emit("vector", lambda e: e.tensor_scalar(
                out=t0[:, :G], in0=amax_ap, scalar1=2.0 ** -40, scalar2=None,
                op0=ALU.max), waits=list(waits0) + [(sV, wait_v)], inc=sV, ser=True)
            em.emit("scalar", lambda e: e.activation(
                out=t1[:, :G], in_=t0[:, :G], func=AF.Ln), waits=[(sV, sV.n)], inc=sC)
            cc = sC.n
            em.emit("vector", lambda e: e.tensor_scalar(
                out=rnd_ap, in0=t1[:, :G], scalar1=INVLN2, scalar2=191.5,
                op0=ALU.mult, op1=ALU.add), waits=[(sC, cc)], ser=True)
            em.emit("vector", lambda e: e.tensor_scalar(
                out=t1[:, :G], in0=rnd_ap, scalar1=192.0, scalar2=None,
                op0=ALU.subtract), inc=sV, ser=True)
            em.emit("scalar", lambda e: e.activation(
                out=t2[:, :G], in_=t1[:, :G], func=AF.Exp, scale=LN2),
                waits=[(sV, sV.n)], inc=sC)
            cc = sC.n
            em.emit("vector", lambda e: e.tensor_scalar(
                out=t1[:, :G], in0=t2[:, :G], scalar1=2.0, scalar2=None, op0=ALU.mult),
                waits=[(sC, cc)], ser=True)
            em.emit("vector", lambda e: e.tensor_tensor(
                out=t1[:, :G], in0=t0[:, :G], in1=t1[:, :G], op=ALU.is_ge), ser=True)
            em.emit("vector", lambda e: e.tensor_scalar(
                out=t1[:, :G], in0=t1[:, :G], scalar1=1.0, scalar2=None, op0=ALU.add), ser=True)
            em.emit("vector", lambda e: e.tensor_tensor(
                out=t2[:, :G], in0=t2[:, :G], in1=t1[:, :G], op=ALU.mult), ser=True)
            em.emit("vector", lambda e: e.tensor_tensor(
                out=t1[:, :G], in0=t0[:, :G], in1=t2[:, :G], op=ALU.is_lt), ser=True)
            em.emit("vector", lambda e: e.tensor_scalar(
                out=t1[:, :G], in0=t1[:, :G], scalar1=-0.5, scalar2=1.0,
                op0=ALU.mult, op1=ALU.add), ser=True)
            em.emit("vector", lambda e: e.tensor_tensor(
                out=t2[:, :G], in0=t2[:, :G], in1=t1[:, :G], op=ALU.mult), ser=True)
            em.emit("vector", lambda e: e.tensor_scalar(
                out=sc_ap, in0=t2[:, :G], scalar1=0.125, scalar2=None,
                op0=ALU.mult), inc=sV, ser=True)
            def _recip(e):
                with nc.allow_low_precision(reason="reciprocal of exact powers of two is exact in bf16"):
                    return e.reciprocal(out=rsc_ap, in_=sc_ap)
            em.emit("vector", _recip, waits=[(sV, sV.n)], inc=sV, ser=True)
            return sV.n

        em.emit("vector", lambda e: e.memset(negc[:], -192.0 * LN2), inc=sV)

        def chainT(G, amax_ap, T_ap, t0, t2, t3, rnd_ap, wait_v):
            em.emit("vector", lambda e: e.tensor_scalar(
                out=t0, in0=amax_ap, scalar1=2.0 ** -40, scalar2=None,
                op0=ALU.max), waits=[(sV, wait_v)], inc=sV, ser=True)
            em.emit("scalar", lambda e: e.activation(
                out=t3, in_=t0, func=AF.Ln), waits=[(sV, sV.n)], inc=sC)
            em.emit("vector", lambda e: e.tensor_scalar(
                out=rnd_ap, in0=t3, scalar1=INVLN2, scalar2=191.5,
                op0=ALU.mult, op1=ALU.add), waits=[(sC, sC.n)], ser=True)
            em.emit("scalar", lambda e: e.activation(
                out=t2, in_=rnd_ap, func=AF.Exp, scale=LN2, bias=negc[:, 0:1]),
                waits=[(sV, sV.n)], inc=sC)
            em.emit("vector", lambda e: e.tensor_scalar(
                out=t3, in0=t2, scalar1=2.0, scalar2=None, op0=ALU.mult),
                waits=[(sC, sC.n)], ser=True)
            em.emit("vector", lambda e: e.tensor_tensor(
                out=t3, in0=t0, in1=t3, op=ALU.is_ge), ser=True)
            em.emit("vector", lambda e: e.tensor_scalar(
                out=t3, in0=t3, scalar1=1.0, scalar2=None, op0=ALU.add), ser=True)
            em.emit("vector", lambda e: e.tensor_tensor(
                out=t2, in0=t2, in1=t3, op=ALU.mult), ser=True)
            em.emit("vector", lambda e: e.tensor_tensor(
                out=t3, in0=t0, in1=t2, op=ALU.is_lt), ser=True)
            em.emit("vector", lambda e: e.tensor_scalar(
                out=t3, in0=t3, scalar1=-0.5, scalar2=1.0,
                op0=ALU.mult, op1=ALU.add), ser=True)
            em.emit("vector", lambda e: e.tensor_tensor(
                out=t2, in0=t2, in1=t3, op=ALU.mult), ser=True)
            em.emit("vector", lambda e: e.tensor_copy(out=rnd_ap, in_=t2), ser=True)
            em.emit("vector", lambda e: e.tensor_scalar(
                out=T_ap, in0=rnd_ap, scalar1=24.0, scalar2=None,
                op0=ALU.mult), inc=sV, ser=True)
            return sV.n

        tr_bank = [0]
        v_evac_bank = [None, None]
        v_store_last = [0, 0]

        def tr_batch(n, in_fn, out_fn, tr_waits=(), ev_waits=(), rows=128,
                     ev_eng="vector"):
            b = tr_bank[0]
            tr_bank[0] ^= 1
            bank = pT1 if b == 0 else pT2
            for k in range(n):
                w = list(tr_waits) if k == 0 else []
                if k == 0 and v_evac_bank[b] is not None:
                    w.append(v_evac_bank[b])
                em.emit("tensor", lambda e, k=k, bank=bank: e.transpose(
                    out=bank[0:rows, k * 128:(k + 1) * 128], in_=in_fn(k),
                    identity=ident[:]), waits=w, inc=(sT if k == n - 1 else None))
            v_end = sT.n
            esem = sV if ev_eng == "vector" else sC
            for k in range(n):
                w = [(sT, v_end)] + (list(ev_waits) if k == 0 else [])
                if ev_eng == "vector":
                    em.emit("vector", lambda e, k=k, bank=bank: e.tensor_copy(
                        out=out_fn(k), in_=bank[0:rows, k * 128:(k + 1) * 128]),
                        waits=w, inc=sV, ser=True)
                else:
                    em.emit("scalar", lambda e, k=k, bank=bank: e.copy(
                        out=out_fn(k), in_=bank[0:rows, k * 128:(k + 1) * 128]),
                        waits=w, inc=sC)
            v_evac_bank[b] = (esem, esem.n)
            return v_end, (esem, esem.n)

        def w_round(srcs, store_fns, tap0=None):
            n = len(srcs)
            wsA = [FA0, FA1]
            v_amax = [0] * n
            v_dma = [0] * n
            for i in range(n):
                p = i % 2
                w = [(sV, v_amax[i - 2])] if i >= 2 else [(sV, sV.n)]
                em.emit("sync", lambda e, i=i, p=p: e.dma_start(
                    out=wsA[p], in_=srcs[i]), waits=w, inc=sDC[p], inc_by=16)
                v_dma[i] = sDC[p].n
                em.emit("vector", lambda e, i=i, p=p: e.reduce_max(
                    out=FA2[:, i * 32:(i + 1) * 32],
                    in_=wsA[p].rearrange("p (g k) -> p g k", k=32),
                    axis=AX.X, apply_absolute_value=True),
                    waits=[(sDC[p], v_dma[i])], inc=sV)
                v_amax[i] = sV.n
            v_chain = chainT(n * 32, FA2[:, 0:n * 32], FA0[:, 0:n * 32],
                             FA0[:, 0:n * 32] if False else FA0[:, 0:n * 32],
                             FA1[:, 0:n * 32], FA2[:, 0:n * 32],
                             rndW[:, 0:n * 32], sV.n)
            wsB = [FA1, FA2]
            v_add = [0] * n
            v_sub = [0] * n
            v_tr_last = [0] * n
            for i in range(n):
                p = i % 2
                w = [(sV, v_add[i - 2])] if i >= 2 else [(sV, v_chain)]
                em.emit("sync", lambda e, i=i, p=p: e.dma_start(
                    out=wsB[p], in_=srcs[i]), waits=w, inc=sDC[p], inc_by=16)
                v_dmab = sDC[p].n
                wadd = [(sDC[p], v_dmab)]
                if i >= 2:
                    wadd.append((sT, v_tr_last[i - 2]))
                em.emit("vector", lambda e, i=i, p=p: e.tensor_tensor(
                    out=wq_bp[p].rearrange("p (g k) -> p g k", k=32),
                    in0=wsB[p].rearrange("p (g k) -> p g k", k=32),
                    in1=FA0[:, i * 32:(i + 1) * 32].unsqueeze(-1)
                        .to_broadcast([128, 32, 32]),
                    op=ALU.add), waits=wadd, inc=sV, ser=True)
                v_add[i] = sV.n
                em.emit("vector", lambda e, i=i, p=p: e.tensor_tensor(
                    out=wq_bp[p].rearrange("p (g k) -> p g k", k=32),
                    in0=wq_bp[p].rearrange("p (g k) -> p g k", k=32),
                    in1=FA0[:, i * 32:(i + 1) * 32].unsqueeze(-1)
                        .to_broadcast([128, 32, 32]),
                    op=ALU.subtract), inc=sV, ser=True)
                v_sub[i] = sV.n
                if tap0 is not None and i == 0:
                    em.emit("sync", lambda e: e.dma_start(
                        out=tap0, in_=wq_bp[0]),
                        waits=[(sV, v_sub[0])], inc=sDS, inc_by=16)
                for h in range(2):
                    tr_batch(
                        4,
                        lambda k, p=p, h=h: wq_bp[p][:, (h * 4 + k) * 128:
                                                     (h * 4 + k + 1) * 128],
                        lambda k, p=p, h=h: wq_stp[p][:, h * 4 + k, :],
                        tr_waits=([(sV, v_sub[i])] if h == 0 else []),
                        ev_waits=([(sDWp[p], v_store_last[p])] if h == 0 else []))
                v_tr_last[i] = sT.n
                em.emit("gpsimd", lambda e, i=i, p=p: e.dma_start(
                    out=store_fns[i], in_=wq_stp[p][:]),
                    waits=[(sV, sV.n)], inc=sDWp[p], inc_by=16)
                v_store_last[p] = sDWp[p].n

        fc_srcs = [wfc_d[i * 128:(i + 1) * 128, :] for i in range(NT)]
        fc_stores = [wfct_s[:, i, :, :].rearrange("t p n -> p t n")
                     for i in range(NT)]
        w_round(fc_srcs, fc_stores,
                tap0=(taps["t_wfcq"] if debug_taps else None))
        pr_srcs = []
        pr_stores = []
        for i in range(ND):
            for hf in range(4):
                pr_srcs.append(wpr_d[i * 128:(i + 1) * 128,
                                     hf * 1024:(hf + 1) * 1024])
                pr_stores.append(
                    wprt_s[hf * 8:(hf + 1) * 8, i, :, :]
                    .rearrange("t p n -> p t n"))
        w_round(pr_srcs, pr_stores)
        v_wst = [sDWp[0].n, sDWp[1].n]

        wqkv_list = []
        for W_d, W_sb in [(Wq_d, Wq_sb), (Wk_d, Wk_sb), (Wv_d, Wv_sb)]:
            for t in range(ND):
                wqkv_list.append((W_d[t * 128:(t + 1) * 128, :], W_sb, t))
        wsC = [FA1, FA2]
        v_cp = [0] * len(wqkv_list)
        for j, (src_ap, W_sb, t) in enumerate(wqkv_list):
            p = j % 2
            w = [(sV, v_cp[j - 2])] if j >= 2 else [(sV, sV.n)]
            em.emit("sync", lambda e, src_ap=src_ap, p=p: e.dma_start(
                out=wsC[p], in_=src_ap), waits=w, inc=sDC[p], inc_by=16)
            vd = sDC[p].n
            em.emit("vector", lambda e, W_sb=W_sb, t=t, p=p: e.tensor_copy(
                out=W_sb[:, t, :], in_=wsC[p]),
                waits=[(sDC[p], vd)], inc=sV)
            v_cp[j] = sV.n
        v_wqkv = sV.n
        if debug_taps:
            em.emit("sync", lambda e: e.dma_start(out=taps["t_wv0"], in_=Wv_sb[:, 0, :]),
                    waits=[(sV, v_wqkv)], inc=sDS, inc_by=16)
            em.emit("sync", lambda e: e.dma_start(out=taps["t_wq0"], in_=Wq_sb[:, 0, :]),
                    waits=[(sV, v_wqkv)], inc=sDS, inc_by=16)

        xt_free = (sV, 0)
        v_w_T = sT.n
        v_w_G = sG.n
        for i in range(NT):
            em.emit("sync", lambda e, i=i: e.dma_start(
                out=xtile[:], in_=x_d[i * 128:(i + 1) * 128, :]),
                waits=[xt_free] + ([(sT, v_w_T), (sG, v_w_G), (sV, v_wqkv)]
                                   if i == 0 else []),
                inc=sDX, inc_by=16)
            vdma = sDX.n
            em.emit("vector", lambda e: e.bn_stats(out=bnst[:, 0:6], in_=xtile[:, 0:512]),
                    waits=[(sDX, vdma)] + ([(sT, v_w_T), (sG, v_w_G)] if i == 0 else []))
            em.emit("vector", lambda e: e.bn_stats(out=bnst[:, 6:12], in_=xtile[:, 512:1024]))
            em.emit("vector", lambda e: e.bn_aggr(out=bnag[:], in_=bnst[:]), ser=True)
            em.emit("vector", lambda e: e.tensor_scalar(
                out=stat[:, 5:6], in0=bnag[:, 1:2], scalar1=1e-5, scalar2=None,
                op0=ALU.add), inc=sV, ser=True)
            em.emit("scalar", lambda e: e.activation(
                out=stat[:, 2:3], in_=stat[:, 5:6], func=AF.Ln),
                waits=[(sV, sV.n)], inc=sC)
            em.emit("vector", lambda e: e.tensor_scalar(
                out=stat[:, 3:4], in0=stat[:, 2:3], scalar1=-0.5, scalar2=None,
                op0=ALU.mult), waits=[(sC, sC.n)], inc=sV)
            em.emit("scalar", lambda e: e.activation(
                out=stat[:, 4:5], in_=stat[:, 3:4], func=AF.Exp),
                waits=[(sV, sV.n)], inc=sC)
            em.emit("vector", lambda e: e.tensor_scalar(
                out=xn_f[:], in0=xtile[:], scalar1=bnag[:, 0:1], scalar2=stat[:, 4:5],
                op0=ALU.subtract, op1=ALU.mult), waits=[(sC, sC.n)], inc=sV, ser=True)
            xt_free = (sV, sV.n)
            em.emit("vector", lambda e: e.tensor_copy(out=xnb[:], in_=xn_f[:]),
                    waits=[(sT, sT.n)], inc=sV, ser=True)
            v_xn = sV.n
            em.emit("vector", lambda e: e.reduce_max(
                out=amaxx[:], in_=xn_f[:].rearrange("p (g k) -> p g k", k=32),
                axis=AX.X, apply_absolute_value=True), inc=sV, ser=True)
            v_sc = scale_chain(32, amaxx[:], scx[:], rscx[:], tmpx, rndx[:], sV.n)
            em.emit("vector", lambda e: e.tensor_tensor(
                out=tx_f[:].rearrange("p (g k) -> p g k", k=32),
                in0=xn_f[:].rearrange("p (g k) -> p g k", k=32),
                in1=rscx[:].unsqueeze(-1).to_broadcast([128, 32, 32]),
                op=ALU.mult), waits=[(sV, v_sc)], ser=True)
            em.emit("vector", lambda e: e.tensor_scalar(
                out=rx_b[:], in0=tx_f[:], scalar1=192.0, scalar2=None, op0=ALU.add),
                waits=([(sDWp[0], v_wst[0])] if i == 0 else []), ser=True)
            em.emit("vector", lambda e: e.tensor_scalar(
                out=rx_b[:], in0=rx_b[:], scalar1=192.0, scalar2=None, op0=ALU.subtract), ser=True)
            em.emit("vector", lambda e: e.tensor_tensor(
                out=xqb[:].rearrange("p (g k) -> p g k", k=32),
                in0=rx_b[:].rearrange("p (g k) -> p g k", k=32),
                in1=scx[:].unsqueeze(-1).to_broadcast([128, 32, 32]),
                op=ALU.mult), waits=[(sT, sT.n)], inc=sV, ser=True)
            v_xq = sV.n
            if debug_taps and i == 0:
                em.emit("sync", lambda e: e.dma_start(out=taps["t_xq"], in_=xqb[:]),
                        waits=[(sV, v_xq)], inc=sDS, inc_by=16)
            for t in range(ND):
                pt = pT1 if t % 2 == 0 else pT2
                em.emit("tensor", lambda e, t=t, pt=pt: e.transpose(
                    out=pt[:, 0:128], in_=xnb[:, t * 128:(t + 1) * 128], identity=ident[:]),
                    waits=[(sV, v_xn), (sC, sC.n)], inc=sT)
                em.emit("scalar", lambda e, t=t, i=i, pt=pt: e.copy(
                    out=xnT[:, t, i * 128:(i + 1) * 128], in_=pt[:, 0:128]),
                    waits=[(sT, sT.n)], inc=sC)
            for t in range(ND):
                pt = pT1 if t % 2 == 0 else pT2
                em.emit("tensor", lambda e, t=t, pt=pt: e.transpose(
                    out=pt[:, 0:128], in_=xqb[:, t * 128:(t + 1) * 128], identity=ident[:]),
                    waits=[(sV, v_xq), (sC, sC.n), (sDP, sDP.n)], inc=sT)
                em.emit("scalar", lambda e, t=t, pt=pt: e.copy(
                    out=ttile[:, t, :], in_=pt[:, 0:128]),
                    waits=[(sT, sT.n)] + ([(sDWp[1], v_wst[1])]
                                          if (i == 0 and t == 0) else []), inc=sC)
            em.emit("gpsimd", lambda e, i=i: e.dma_start(
                out=xqt_s[:, i, :, :].rearrange("t p n -> p t n"), in_=ttile[:]),
                waits=[(sC, sC.n)], inc=sDP, inc_by=16)
        if debug_taps:
            em.emit("sync", lambda e: e.dma_start(
                out=taps["t_xnT"], in_=R1[:]), waits=[(sC, sC.n)], inc=sDS, inc_by=16)

        v_b_done = sC.n
        if debug_taps:
            em.emit("sync", lambda e: e.dma_start(out=taps["t_wv1"], in_=Wv_sb[:, 0, :]),
                    waits=[(sC, v_b_done), (sV, sV.n)], inc=sDS, inc_by=16)
        pk_free = {0: (sC, 0), 1: (sC, 0)}
        v_vcred = 0
        v_vstore = 0
        v_vcstore = 0
        vtt = vt_buf[:].rearrange("p (t n) -> p t n", n=128)
        for proj, W_sb in enumerate([Wq_sb, Wk_sb, Wv_sb]):
            if debug_taps and proj == 2:
                em.emit("sync", lambda e: e.dma_start(out=taps["t_wv2"], in_=Wv_sb[:, 0, :]),
                        waits=[(sC, sC.n), (sV, sV.n)], inc=sDS, inc_by=16)
            for c in range(8):
                for m in range(ND):
                    bk = (c * ND + m) % 2
                    pp = pA if bk == 0 else pB
                    for k in range(ND):
                        w = []
                        if proj == 0 and c == 0 and m == 0 and k == 0:
                            w = [(sC, v_b_done), (sV, v_wqkv)]
                        if k == 0:
                            w = w + [pk_free[bk]]
                        em.emit("tensor", lambda e, W_sb=W_sb, m=m, k=k, c=c, pp=pp: e.matmul(
                            out=pp[:], lhsT=W_sb[:, k, m * 128:(m + 1) * 128],
                            rhs=xnT[:, k, c * 512:(c + 1) * 512],
                            start=(k == 0), stop=(k == ND - 1)),
                            waits=w, inc=(sT if k == ND - 1 else None))
                    v_mm = sT.n
                    if proj == 0:
                        em.emit("scalar", lambda e, m=m, pp=pp: e.activation(
                            out=vt_buf[:], in_=pp[:], func=AF.Identity,
                            bias=bqs_sb[:, m:m + 1], scale=SCALE),
                            waits=[(sT, v_mm), (sDP, sDP.n)], inc=sC)
                        pk_free[bk] = (sC, sC.n)
                        em.emit("gpsimd", lambda e, m=m, c=c: e.dma_start(
                            out=qt_s[m, c * 4:(c + 1) * 4, :, :].rearrange("b p n -> p b n"),
                            in_=vt_buf[:].rearrange("p (b n) -> p b n", n=128)),
                            waits=[(sC, sC.n)], inc=sDP, inc_by=16)
                    elif proj == 1:
                        em.emit("scalar", lambda e, m=m, c=c, pp=pp: e.activation(
                            out=KT[:, m, c * 512:(c + 1) * 512], in_=pp[:],
                            func=AF.Identity, bias=bk_sb[:, m:m + 1]),
                            waits=[(sT, v_mm)], inc=sC)
                        pk_free[bk] = (sC, sC.n)
                        em.emit("vector", lambda e, m=m, c=c: e.reduce_sum(
                            out=vc_red[:],
                            in_=KT[:, m, c * 512:(c + 1) * 512].rearrange("p (g k) -> p g k", k=4),
                            axis=AX.X), waits=[(sC, sC.n)], ser=True)
                        em.emit("vector", lambda e, m=m, c=c: e.tensor_scalar(
                            out=KcT[:, m, c * 128:(c + 1) * 128], in0=vc_red[:],
                            scalar1=0.25, scalar2=None, op0=ALU.mult), inc=sV, ser=True)
                    else:
                        em.emit("scalar", lambda e, m=m, pp=pp: e.activation(
                            out=vt_buf[:], in_=pp[:], func=AF.Identity,
                            bias=bv_sb[:, m:m + 1]),
                            waits=[(sT, v_mm), (sVS, v_vstore), (sV, v_vcred),
                                   (sDP, sDP.n)],
                            inc=sC)
                        v_ev = sC.n
                        if debug_taps and c == 0 and m == 0:
                            em.emit("scalar", lambda e, pp=pp: e.copy(
                                out=vpstap[:], in_=pp[:]), waits=[(sC, v_ev)], inc=sC)
                            em.emit("sync", lambda e: e.dma_start(
                                out=taps["t_vps"], in_=vpstap[:]),
                                waits=[(sC, sC.n)], inc=sDS, inc_by=16)
                            pk_free[bk] = (sC, sC.n)
                        else:
                            pk_free[bk] = (sC, v_ev)
                        v_ev = sC.n
                        if debug_taps and c == 0 and m == 0:
                            em.emit("sync", lambda e: e.dma_start(
                                out=taps["t_vtb"], in_=vt_buf[:]),
                                waits=[(sC, v_ev)], inc=sDS, inc_by=16)
                        em.emit("vector", lambda e: e.reduce_sum(
                            out=vc_red[:],
                            in_=vt_buf[:].rearrange("p (g k) -> p g k", k=4),
                            axis=AX.X), waits=[(sC, v_ev)], inc=sV, ser=True)
                        v_vcred = sV.n
                        tr_batch(
                            4,
                            lambda k: vt_buf[:, k * 128:(k + 1) * 128],
                            lambda k: vtt[:, k, :],
                            tr_waits=[(sC, v_ev)],
                            ev_waits=[(sVS, v_vstore)])
                        em.emit("gpsimd", lambda e, m=m, c=c: e.dma_start(
                            out=v_s[c * 4:(c + 1) * 4, m, :, :]
                            .rearrange("t p n -> p t n"), in_=vtt),
                            waits=[(sV, sV.n)], inc=sVS, inc_by=16)
                        v_vstore = sVS.n
                        em.emit("vector", lambda e: e.tensor_scalar(
                            out=vc_bf[:], in0=vc_red[:], scalar1=0.25, scalar2=None,
                            op0=ALU.mult), waits=[(sVC, v_vcstore)], inc=sV, ser=True)
                        tr_batch(
                            1,
                            lambda k: vc_bf[:],
                            lambda k: vc_bf[:],
                            tr_waits=[(sV, sV.n)])
                        em.emit("gpsimd", lambda e, m=m, c=c: e.dma_start(
                            out=vc_s[c, m, :, :], in_=vc_bf[:]),
                            waits=[(sV, sV.n)], inc=sVC, inc_by=16)
                        v_vcstore = sVC.n
        v_cC, v_cV, v_cDP, v_cT = sC.n, sV.n, sDP.n, sT.n
        v_cVS, v_cVC = sVS.n, sVC.n
        if debug_taps:
            em.emit("sync", lambda e: e.dma_start(out=taps["t_KT"], in_=R2[:]),
                    waits=[(sC, v_cC), (sV, v_cV)], inc=sDS, inc_by=16)
            em.emit("sync", lambda e: e.dma_start(out=taps["t_KcT"], in_=R3[:, 0:8192]),
                    waits=[(sC, v_cC), (sV, v_cV)], inc=sDS, inc_by=16)

        em.emit("sync", lambda e: e.dma_start(
            out=cmask[:], in_=cmask_d),
            waits=[(sT, v_cT)], inc=sDS, inc_by=16)
        v_cmask = sDS.n
        for t in range(NT):
            em.emit("sync", lambda e, t=t: e.dma_start(
                out=V_sb[:, t, :].rearrange("p (m n) -> p m n", n=128),
                in_=v_s[t, :, :, :].rearrange("m p n -> p m n")),
                waits=[(sVS, v_cVS), (sT, v_cT)] if t == 0 else [],
                inc=sDV, inc_by=16)

        v_vload = sDV.n
        for dch in range(2):
            pp = pACC[dch]
            for kt in range(NT):
                em.emit("tensor", lambda e, dch=dch, kt=kt, pp=pp: e.matmul(
                    out=pp[0:1, :], lhsT=onc[:], rhs=V_sb[:, kt, dch * 512:(dch + 1) * 512],
                    start=(kt == 0), stop=(kt == NT - 1)),
                    waits=([(sDV, v_vload)] if kt == 0 else []),
                    inc=(sT if kt == NT - 1 else None))
            em.emit("scalar", lambda e, dch=dch, pp=pp: e.copy(
                out=vsum_rowb[:, dch * 512:(dch + 1) * 512], in_=pp[0:1, :]),
                waits=[(sT, sT.n)], inc=sC)
        v_vsum = sC.n

        st_state = {}
        csc_free = {0: (sC, 0), 1: (sC, 0)}
        raw_free = {0: (sC, 0), 1: (sC, 0)}
        st_raw = [0, 0]
        st_q = [0, 0]
        st_qsl_free = [(sT, 0), (sT, 0)]
        v_tail_end = [0, 0]
        v_prev_tail = [0]
        prev_vcmm = [0] * 8
        rawbs = [rawb, r3(20480, 4096)]

        def emit_scores(qb):
            rawb_p = rawbs[qb % 2]
            qsl_p = qsls[qb % 2]
            sq = sDQ if qb % 2 == 0 else sDQ2
            em.emit("sync", lambda e, qb=qb, qsl_p=qsl_p: e.dma_start(
                out=qsl_p[:], in_=qt_s[:, qb, :, :].rearrange("t p n -> p t n")),
                waits=[st_qsl_free[qb % 2]] + ([(sDP, v_cDP)] if qb == 0 else []),
                inc=sq, inc_by=16)
            v_q = sq.n
            st_q[qb % 2] = v_q
            for c in range(8):
                bk = c % 2
                pp = pA if bk == 0 else pB
                for k in range(ND):
                    w = [(sq, v_q)] if (c == 0 and k == 0) else []
                    if qb == 0 and c == 0 and k == 0:
                        w = w + [pk_free[0], pk_free[1]]
                    if k == 0:
                        w = w + [raw_free[bk]]
                    em.emit("tensor", lambda e, k=k, c=c, pp=pp, qsl_p=qsl_p: e.matmul(
                        out=pp[:], lhsT=qsl_p[:, k, :], rhs=KT[:, k, c * 512:(c + 1) * 512],
                        start=(k == 0), stop=(k == ND - 1)),
                        waits=w, inc=(sT if k == ND - 1 else None))
                em.emit("scalar", lambda e, c=c, pp=pp: e.copy(
                    out=rawb_p[:, c * 512:(c + 1) * 512], in_=pp[:]),
                    waits=[(sT, sT.n)] + ([(sV, v_tail_end[qb % 2])] if c == 0 else []), inc=sC)
                raw_free[bk] = (sC, sC.n)
            st_raw[qb % 2] = sC.n
            if debug_taps and qb == 1:
                em.emit("sync", lambda e: e.dma_start(out=taps["t_rawb"], in_=rawb_p[:]),
                        waits=[(sC, st_raw[1])], inc=sDS, inc_by=16)
            st_state["raw_done"] = (sT, sT.n)


        def emit_tail(qb):
            dc = qb // 4
            off = qb % 4
            rawb_p = rawbs[qb % 2]
            qsl_p = qsls[qb % 2]
            for cc_ in range(2):
                pp = pC_ if cc_ == 0 else pD
                for k in range(ND):
                    w = [((sDQ if qb % 2 == 0 else sDQ2), st_q[qb % 2])] \
                        if (cc_ == 0 and k == 0) else []
                    if qb == 0 and cc_ == 0 and k == 0:
                        w = w + [(sV, v_cV)]
                    if k == 0:
                        w = w + [csc_free[cc_]]
                    em.emit("tensor", lambda e, cc_=cc_, k=k, pp=pp, qsl_p=qsl_p: e.matmul(
                        out=pp[:], lhsT=qsl_p[:, k, :], rhs=KcT[:, k, cc_ * 512:(cc_ + 1) * 512],
                        start=(k == 0), stop=(k == ND - 1)),
                        waits=w, inc=(sT if k == ND - 1 else None))
                em.emit("scalar", lambda e, cc_=cc_, pp=pp: e.copy(
                    out=c_sc[:, cc_ * 512:(cc_ + 1) * 512], in_=pp[:]),
                    waits=[(sT, sT.n)] + ([(sV, v_prev_tail[0])] if cc_ == 0 else []), inc=sC)
                csc_free[cc_] = (sC, sC.n)
            v_csc = sC.n
            st_qsl_free[qb % 2] = (sT, sT.n)
            v_vcmm = [0] * 8

            def emit_vc_load(j, w):
                vb = vcs8[j % 4]
                eng, sem = (("sync", sDC[0]) if j % 2 == 0
                            else ("gpsimd", sDC2[0]))
                em.emit(eng, lambda e, j=j, vb=vb: e.dma_start(
                    out=vb[:].rearrange("p (m n) -> p m n", n=128),
                    in_=vc_s[j, :, :, :].rearrange("m p n -> p m n")),
                    waits=w, inc=sem, inc_by=16)

            for j in range(4):
                emit_vc_load(j, ([(sVC, v_cVC), (sT, v_cT)] if qb == 0
                                 else [(sT, prev_vcmm[j + 4])]))
            em.emit("vector", lambda e: e.tensor_max(
                out=t1a[:], in0=rawb_p[:, 0:1024], in1=rawb_p[:, 1024:2048]),
                waits=[(sC, st_raw[qb % 2]), (sDA, sDA.n)])
            em.emit("vector", lambda e: e.tensor_max(
                out=t1b[:], in0=rawb_p[:, 2048:3072], in1=rawb_p[:, 3072:4096]))
            em.emit("vector", lambda e: e.tensor_max(
                out=g8[0][:], in0=t1a[:, 0:512], in1=t1a[:, 512:1024]), ser=True)
            em.emit("vector", lambda e: e.tensor_max(
                out=g8[0][:], in0=g8[0][:], in1=t1b[:, 0:512]), ser=True)
            em.emit("vector", lambda e: e.tensor_max(
                out=g8[0][:], in0=g8[0][:], in1=t1b[:, 512:1024]), inc=sV, ser=True)
            em.emit("vector", lambda e: e.tensor_max(
                out=g8[1][:, 0:256], in0=g8[0][:, 0:256], in1=g8[0][:, 256:512]),
                inc=sV, ser=True)
            for it in range(8):
                cur = g8[(it + 1) % 2][:, 0:256]
                nxt = g8[it % 2][:, 0:256]
                em.emit("vector", lambda e, it=it, cur=cur: e.max(
                    out=vals_b[:, it * 8:(it + 1) * 8], in_=cur),
                    waits=[(sV, sV.n)], inc=sV)
                if it < 7:
                    em.emit("vector", lambda e, it=it, cur=cur, nxt=nxt: e.match_replace(
                        out=nxt, in_to_replace=vals_b[:, it * 8:(it + 1) * 8],
                        in_values=cur, imm_value=NEG),
                        waits=[(sV, sV.n)], inc=sV)
            v_vals = sV.n
            if debug_taps and qb == 1:
                em.emit("vector", lambda e: e.tensor_copy(out=vals_dbg[:], in_=vals_b[:]),
                        waits=[(sV, v_vals)], inc=sV)
                em.emit("sync", lambda e: e.dma_start(out=taps["t_vals"], in_=vals_dbg[:]),
                        waits=[(sV, sV.n)], inc=sDS, inc_by=16)
            em.emit("vector", lambda e: e.tensor_copy(out=tauf[:, 0:1], in_=vals_b[:, 63:64]),
                    waits=[(sV, v_vals)])
            em.emit("vector", lambda e: e.tensor_scalar(
                out=sstat[:, 0:1], in0=vals_b[:, 0:1], scalar1=0.0, scalar2=None,
                op0=ALU.max))
            em.emit("vector", lambda e: e.tensor_scalar(
                out=sstat[:, 1:2], in0=sstat[:, 0:1], scalar1=-1.0, scalar2=None,
                op0=ALU.mult), inc=sV, ser=True)
            v_mu = sV.n
            em.emit("scalar", lambda e: e.activation(
                out=sstat[:, 2:3], in_=sstat[:, 0:1], func=AF.Exp, scale=-1.0),
                waits=[(sV, v_mu)], inc=sC)
            v_emu = sC.n

            v_stt = [0] * 8
            for c in range(8):
                eb = Ebuf[c % 4]
                ms = mselc[c % 2]
                em.emit("scalar", lambda e, c=c, eb=eb: e.activation(
                    out=eb[:], in_=rawb_p[:, c * 512:(c + 1) * 512], func=AF.Exp,
                    bias=sstat[:, 1:2]),
                    waits=([(sV, v_mu)] if c == 0 else [])
                    + ([(sV, v_stt[c - 4])] if c >= 4 else []),
                    inc=sC)
                v_e = sC.n
                em.emit("vector", lambda e, c=c, ms=ms: e.tensor_scalar(
                    out=ms[:], in0=rawb_p[:, c * 512:(c + 1) * 512], scalar1=tauf[:, 0:1],
                    scalar2=None, op0=ALU.is_ge),
                    waits=[(sV, v_mu)] if c == 0 else [])
                em.emit("vector", lambda e, c=c, eb=eb, ms=ms: e.scalar_tensor_tensor(
                    out=Cb[1][:], in0=eb[:], scalar=sstat[:, 2:3],
                    in1=ms[:], op0=ALU.subtract, op1=ALU.mult,
                    accum_out=gsum[:, c:c + 1]),
                    waits=[(sC, v_e), (sC, v_emu)], inc=sV, ser=True)
                v_stt[c] = sV.n
            v_g = sV.n
            em.emit("vector", lambda e: e.reduce_sum(
                out=sstat[:, 14:15], in_=gsum[:], axis=AX.X), waits=[(sV, v_g)])
            em.emit("vector", lambda e: e.scalar_tensor_tensor(
                out=sstat[:, 15:16], in0=sstat[:, 2:3], scalar=4096.0,
                in1=sstat[:, 14:15], op0=ALU.mult, op1=ALU.add), ser=True)
            em.emit("vector", lambda e: e.tensor_scalar(
                out=sstat[:, 15:16], in0=sstat[:, 15:16], scalar1=3.0, scalar2=None,
                op0=ALU.mult), ser=True)
            em.emit("vector", lambda e: e.reciprocal(
                out=sstat[:, 14:15], in_=sstat[:, 15:16]), inc=sV, ser=True)
            v_rz3 = sV.n
            em.emit("vector", lambda e: e.tensor_scalar(
                out=sstat[:, 3:4], in0=sstat[:, 2:3], scalar1=sstat[:, 14:15],
                scalar2=None, op0=ALU.mult), waits=[(sV, v_rz3)], inc=sV, ser=True)
            v_biasg = sV.n
            em.emit("vector", lambda e: e.tensor_copy(
                out=coefb[:], in_=sstat[:, 3:4]), inc=sV, ser=True)
            v_coef = sV.n
            _, ev_coefrow = tr_batch(
                1, lambda k: coefb[:], lambda k: coef_row,
                tr_waits=[(sV, v_coef)], rows=1, ev_eng="scalar")

            v_tr_chunk = {}
            v_mm_chunk = {}
            v_stt2 = [0] * 8
            order = list(range(dc + 1, 8)) + list(range(dc + 1))
            v_biasw = 0
            for idx, c in enumerate(order):
                if c == 0:
                    em.emit("vector", lambda e, off=off, dc=dc: e.tensor_tensor(
                        out=wdiag[:], in0=rawb_p[:, dc * 512:(dc + 1) * 512],
                        in1=cmask[:, off, :], op=ALU.add),
                        waits=([(sDS, v_cmask)] if qb == 0 else []))
                    if dc > 0:
                        em.emit("vector", lambda e, dc=dc: e.reduce_max(
                            out=sstat[:, 4:5], in_=rawb_p[:, 0:dc * 512], axis=AX.X))
                        em.emit("vector", lambda e: e.reduce_max(
                            out=sstat[:, 5:6], in_=wdiag[:], axis=AX.X), ser=True)
                        em.emit("vector", lambda e: e.tensor_tensor(
                            out=sstat[:, 6:7], in0=sstat[:, 4:5], in1=sstat[:, 5:6],
                            op=ALU.max), ser=True)
                    else:
                        em.emit("vector", lambda e: e.reduce_max(
                            out=sstat[:, 6:7], in_=wdiag[:], axis=AX.X), ser=True)
                    em.emit("vector", lambda e: e.tensor_scalar(
                        out=sstat[:, 7:8], in0=sstat[:, 6:7], scalar1=-1.0,
                        scalar2=None, op0=ALU.mult), inc=sV, ser=True)
                    v_nmw = sV.n
                    for cw in range(dc + 1):
                        if cw < dc:
                            fn = (lambda e, cw=cw: e.activation(
                                out=Ewb[0][:],
                                in_=rawb_p[:, cw * 512:(cw + 1) * 512],
                                func=AF.Exp, bias=sstat[:, 7:8],
                                accum_out=swp[:, cw:cw + 1]))
                        else:
                            fn = (lambda e, cw=cw: e.activation(
                                out=Ewb[0][:], in_=wdiag[:],
                                func=AF.Exp, bias=sstat[:, 7:8],
                                accum_out=swp[:, cw:cw + 1]))
                        em.emit("scalar", fn,
                                waits=[(sV, v_nmw)] if cw == 0 else [],
                                inc=sC, ser=True)
                    v_expw = sC.n
                    em.emit("vector", lambda e, dc=dc: e.reduce_sum(
                        out=sstat[:, 8:9], in_=swp[:, 0:dc + 1], axis=AX.X),
                        waits=[(sC, v_expw)])
                    em.emit("vector", lambda e: e.tensor_scalar(
                        out=sstat[:, 9:10], in0=sstat[:, 8:9], scalar1=3.0,
                        scalar2=None, op0=ALU.mult), ser=True)
                    em.emit("vector", lambda e: e.reciprocal(
                        out=sstat[:, 10:11], in_=sstat[:, 9:10]), inc=sV, ser=True)
                    v_biasw = sV.n
                cb = Cb[idx % 2]
                eb = Ebuf[idx % 4]
                ms = mselc[idx % 2]
                em.emit("scalar", lambda e, c=c, eb=eb: e.activation(
                    out=eb[:], in_=rawb_p[:, c * 512:(c + 1) * 512], func=AF.Exp,
                    bias=sstat[:, 1:2]),
                    waits=([(sV, v_biasg), (sV, v_g)] if idx == 0 else [])
                    + ([(sV, v_stt2[idx - 4])] if idx >= 4 else []),
                    inc=sC)
                v_e2 = sC.n
                if c <= dc:
                    em.emit("scalar", lambda e, c=c, dc=dc, idx=idx: e.activation(
                        out=Ewb[idx % 2][:],
                        in_=(rawb_p[:, c * 512:(c + 1) * 512] if c < dc else wdiag[:]),
                        func=AF.Exp, bias=sstat[:, 7:8]),
                        waits=([(sV, v_biasw)] if c == 0 else [])
                        + ([(sV, v_stt2[idx - 1])] if idx >= 1 else []), inc=sC)
                    v_w2 = sC.n
                em.emit("vector", lambda e, c=c, ms=ms: e.tensor_scalar(
                    out=ms[:], in0=rawb_p[:, c * 512:(c + 1) * 512], scalar1=tauf[:, 0:1],
                    scalar2=sstat[:, 14:15], op0=ALU.is_ge, op1=ALU.mult),
                    waits=([(sV, v_biasg)] if idx == 0 else []))
                wst = [(sC, v_e2)]
                if idx >= 2:
                    wst.append((sT, v_tr_chunk[idx - 2]))
                em.emit("vector", lambda e, c=c, cb=cb, eb=eb, ms=ms:
                        e.scalar_tensor_tensor(
                            out=cb[:], in0=eb[:], scalar=sstat[:, 2:3],
                            in1=ms[:], op0=ALU.subtract, op1=ALU.mult),
                        waits=wst, inc=sV, ser=True)
                if c <= dc:
                    em.emit("vector", lambda e, idx=idx: e.tensor_scalar(
                        out=Ewb[idx % 2][:], in0=Ewb[idx % 2][:],
                        scalar1=sstat[:, 10:11], scalar2=None, op0=ALU.mult),
                        waits=[(sC, v_w2)], inc=sV, ser=True)
                    em.emit("vector", lambda e, cb=cb, idx=idx: e.tensor_tensor(
                        out=cb[:], in0=cb[:], in1=Ewb[idx % 2][:], op=ALU.add),
                        inc=sV, ser=True)
                v_cb = sV.n
                v_stt2[idx] = v_cb
                evw = []
                if idx >= 2:
                    evw.append((sT, v_mm_chunk[idx - 2]))
                if qb == 0 and idx == 0:
                    evw.append((sDP, v_cDP))
                v_tr_chunk[idx], ev_c = tr_batch(
                    4, lambda k, cb=cb: cb[:, k * 128:(k + 1) * 128],
                    lambda k, idx=idx: ctp[idx % 2][:, k, :],
                    tr_waits=[(sV, v_cb)], ev_waits=evw, ev_eng="scalar")
                for j in range(4):
                    for dch in range(2):
                        pp = pACC[dch]
                        w = []
                        if j == 0 and dch == 0:
                            w = [ev_c]
                            if idx == 0:
                                w.append((sC, sC.n))
                        em.emit("tensor", lambda e, c=c, j=j, dch=dch, pp=pp, idx=idx:
                                e.matmul(
                                    out=pp[:], lhsT=ctp[idx % 2][:, j, :],
                                    rhs=V_sb[:, c * 4 + j, dch * 512:(dch + 1) * 512],
                                    start=(idx == 0 and j == 0), stop=False),
                                waits=w,
                                inc=(sT if (j == 3 and dch == 1) else None))
                v_mm_chunk[idx] = sT.n

            em.emit("vector", lambda e: e.reduce_max(
                out=scp[:, 0:1], in_=c_sc[:], axis=AX.X), waits=[(sC, v_csc)])
            em.emit("vector", lambda e: e.tensor_scalar(
                out=scp[:, 1:2], in0=scp[:, 0:1], scalar1=-1.0, scalar2=None,
                op0=ALU.mult), inc=sV, ser=True)
            v_nmc = sV.n
            for cc_ in range(2):
                em.emit("scalar", lambda e, cc_=cc_: e.activation(
                    out=Pcb[:, cc_ * 512:(cc_ + 1) * 512],
                    in_=c_sc[:, cc_ * 512:(cc_ + 1) * 512], func=AF.Exp,
                    bias=scp[:, 1:2], accum_out=swp[:, 6 + cc_:7 + cc_]),
                    waits=[(sV, v_nmc)] if cc_ == 0 else [], inc=sC)
            em.emit("vector", lambda e: e.reduce_sum(
                out=sstat[:, 11:12], in_=swp[:, 6:8], axis=AX.X),
                waits=[(sC, sC.n)])
            em.emit("vector", lambda e: e.tensor_scalar(
                out=sstat[:, 12:13], in0=sstat[:, 11:12], scalar1=3.0, scalar2=None,
                op0=ALU.mult), ser=True)
            em.emit("vector", lambda e: e.reciprocal(
                out=sstat[:, 13:14], in_=sstat[:, 12:13]), inc=sV, ser=True)
            em.emit("vector", lambda e: e.tensor_scalar(
                out=Pcb[:], in0=Pcb[:], scalar1=sstat[:, 13:14], scalar2=None,
                op0=ALU.mult), waits=[(sV, sV.n)], inc=sV)
            v_pcb = sV.n
            for h in range(2):
                _, ev_pct = tr_batch(
                    4, lambda k, h=h: Pcb[:, (h * 4 + k) * 128:(h * 4 + k + 1) * 128],
                    lambda k, h=h: PcT[:, h * 4 + k, :],
                    tr_waits=([(sV, v_pcb)] if h == 0 else []), ev_eng="scalar")
            v_pref = [(sDC[0], sDC[0].n), (sDC2[0], sDC2[0].n)]
            for j in range(4):
                for dch in range(2):
                    pp = pACC[dch]
                    w = list(v_pref) if dch == 0 else []
                    if j == 0 and dch == 0:
                        w.append(ev_pct)
                    em.emit("tensor", lambda e, j=j, dch=dch, pp=pp: e.matmul(
                        out=pp[:], lhsT=PcT[:, j, :],
                        rhs=vcs8[j][:, dch * 512:(dch + 1) * 512],
                        start=False, stop=False),
                        waits=w, inc=(sT if dch == 1 else None))
                v_vcmm[j] = sT.n
            for j in range(4, 8):
                emit_vc_load(j, [(sT, v_vcmm[j - 4])])
            v_all = [(sDC[0], sDC[0].n), (sDC2[0], sDC2[0].n)]
            for j in range(4, 8):
                for dch in range(2):
                    pp = pACC[dch]
                    w = list(v_all) if (j == 4 and dch == 0) else []
                    em.emit("tensor", lambda e, j=j, dch=dch, pp=pp: e.matmul(
                        out=pp[:], lhsT=PcT[:, j, :],
                        rhs=vcs8[j % 4][:, dch * 512:(dch + 1) * 512],
                        start=False, stop=False),
                        waits=w, inc=(sT if dch == 1 else None))
                v_vcmm[j] = sT.n
            for idx in range(8):
                prev_vcmm[idx] = v_vcmm[idx]
            for dch in range(2):
                pp = pACC[dch]
                em.emit("tensor", lambda e, dch=dch, pp=pp: e.matmul(
                    out=pp[:], lhsT=coef_row, rhs=vsum_rowb[0:1, dch * 512:(dch + 1) * 512],
                    start=False, stop=True),
                    waits=[ev_coefrow, (sC, v_vsum)], inc=sT)
                em.emit("scalar", lambda e, dch=dch, pp=pp: e.copy(
                    out=abuf[:, dch * 512:(dch + 1) * 512], in_=pp[:]),
                    waits=[(sT, sT.n), (sV, v_g)], inc=sC)
            v_abuf = sC.n
            if debug_taps and qb == 1:
                em.emit("sync", lambda e: e.dma_start(out=taps["t_attn"], in_=abuf[:]),
                        waits=[(sC, v_abuf)], inc=sDS, inc_by=16)
                em.emit("sync", lambda e: e.dma_start(out=taps["t_sstat"], in_=sstat[:]),
                        waits=[(sC, v_abuf)], inc=sDS, inc_by=16)
                em.emit("sync", lambda e: e.dma_start(out=taps["t_pcb"], in_=Pcb[:]),
                        waits=[(sC, v_abuf)], inc=sDS, inc_by=16)
                em.emit("sync", lambda e: e.dma_start(out=taps["t_csc"], in_=c_sc[:]),
                        waits=[(sC, v_abuf)], inc=sDS, inc_by=16)
                em.emit("sync", lambda e: e.dma_start(out=taps["t_vsum"], in_=vsum_rowb),
                        waits=[(sC, v_abuf)], inc=sDS, inc_by=16)
            for h in range(2):
                _, ev_at = tr_batch(
                    4, lambda k, h=h: abuf[:, (h * 4 + k) * 128:(h * 4 + k + 1) * 128],
                    lambda k, h=h: atile[:, h * 4 + k, :],
                    tr_waits=([(sC, v_abuf)] if h == 0 else []),
                    ev_waits=([(sDA, sDA.n)] if h == 0 else []), ev_eng="scalar")
            em.emit("sync", lambda e, qb=qb: e.dma_start(
                out=at_s[:, qb, :, :].rearrange("t p n -> p t n"), in_=atile[:]),
                waits=[ev_at], inc=sDA, inc_by=16)
            v_tail_end[qb % 2] = sV.n
            v_prev_tail[0] = sV.n

        for qb in range(QB):
            emit_scores(qb)
            if qb >= 1:
                emit_tail(qb - 1)
        emit_tail(QB - 1)
        v_s_done_C = sC.n
        v_s_done_T = sT.n
        v_s_atdone = sDA.n
        for t in range(ND):
            em.emit("sync", lambda e, t=t: e.dma_start(
                out=wfcT[:, t, :].rearrange("p (i n) -> p i n", n=128),
                in_=wfct_s[t, :, :, :].rearrange("i p n -> p i n")),
                waits=[(sT, v_s_done_T), (sDWp[0], v_wst[0]),
                       (sDWp[1], v_wst[1])] if t == 0 else [],
                inc=sDW, inc_by=16)
        for hf in range(2):
            em.emit("sync", lambda e, hf=hf: e.dma_start(
                out=brow_st[:, 0:2048], in_=bfc_d[0:1, hf * 2048:(hf + 1) * 2048]),
                waits=[(sV, sV.n)], inc=sDS, inc_by=16)
            em.emit("vector", lambda e, hf=hf: e.tensor_copy(
                out=bfc_rowb[0:1, hf * 2048:(hf + 1) * 2048], in_=brow_st[0:1, 0:2048]),
                waits=[(sDS, sDS.n)] + ([(sT, v_s_done_T)] if hf == 0 else []), inc=sV)
        v_bfcrow = sV.n
        v_wfc = sDW.n
        qsl_free_f = (sT, 0)
        h_free = {0: (sC, v_s_done_C), 1: (sC, v_s_done_C)}
        for i in range(NT):
            em.emit("sync", lambda e, i=i: e.dma_start(
                out=qsl[:], in_=xqt_s[:, i, :, :].rearrange("t p n -> p t n")),
                waits=[(sDW, v_wfc), qsl_free_f]
                + ([(sDP, v_cDP), st_qsl_free[0]] if i == 0 else []),
                inc=sDQ, inc_by=16)
            v_xq = sDQ.n
            for c in range(8):
                bk = c % 2
                pp = pA if bk == 0 else pB
                for k in range(ND):
                    w = [(sDQ, v_xq)] if (c == 0 and k == 0) else []
                    if k == 0:
                        w = w + [h_free[bk]]
                    em.emit("tensor", lambda e, c=c, k=k, pp=pp: e.matmul(
                        out=pp[:], lhsT=qsl[:, k, :], rhs=wfcT[:, k, c * 512:(c + 1) * 512],
                        start=(k == 0), stop=False), waits=w)
                em.emit("tensor", lambda e, c=c, pp=pp: e.matmul(
                    out=pp[:], lhsT=onr[:], rhs=bfc_rowb[0:1, c * 512:(c + 1) * 512],
                    start=False, stop=True),
                    waits=[(sV, v_bfcrow)] if (i == 0 and c == 0) else [], inc=sT)
                em.emit("scalar", lambda e, c=c, pp=pp: e.activation(
                    out=hgb[:, c * 512:(c + 1) * 512], in_=pp[:], func=AF.Gelu),
                    waits=[(sT, sT.n)], inc=sC)
                h_free[bk] = (sC, sC.n)
                em.emit("vector", lambda e, i=i, c=c: e.reduce_max(
                    out=amax_v[:, i, c * 16:(c + 1) * 16],
                    in_=hgb[:, c * 512:(c + 1) * 512].rearrange("p (g k) -> p g k", k=32),
                    axis=AX.X, apply_absolute_value=True),
                    waits=[(sC, sC.n)], inc=sV)
            qsl_free_f = (sT, sT.n)
            em.emit("gpsimd", lambda e, i=i: e.dma_start(
                out=hg_s[i, :, :], in_=hgb[:]), waits=[(sC, sC.n)], inc=sDP, inc_by=16)
            if debug_taps and i == 0:
                em.emit("sync", lambda e: e.dma_start(out=taps["t_hg"], in_=hgb[:]),
                        waits=[(sC, sC.n)], inc=sDS, inc_by=16)
            em.emit("scalar", lambda e: e.nop(), waits=[(sDP, sDP.n), (sV, sV.n)])

        v_f15 = sV.n
        for ch in range(4):
            v_f15 = scale_chain(
                1024, amax_all[:, ch * 1024:(ch + 1) * 1024],
                sc_all[:].rearrange("p t n -> p (t n)")[:, ch * 1024:(ch + 1) * 1024],
                rsc_all[:].rearrange("p t n -> p (t n)")[:, ch * 1024:(ch + 1) * 1024],
                f15, rnd15[:, 0:1024], v_f15)
        v_sch = v_f15
        if debug_taps:
            em.emit("sync", lambda e: e.dma_start(
                out=taps["t_sc"], in_=R3[:, 21504:25600]),
                waits=[(sV, v_sch)], inc=sDS, inc_by=16)

        for t in range(NH):
            em.emit("sync", lambda e, t=t: e.dma_start(
                out=wprT[:, t, :].rearrange("p (i n) -> p i n", n=128),
                in_=wprt_s[t, :, :, :].rearrange("i p n -> p i n")),
                waits=[(sT, sT.n), (sDWp[0], v_wst[0]),
                       (sDWp[1], v_wst[1])] if t == 0 else [],
                inc=sDW, inc_by=16)
        for t in range(ND):
            em.emit("sync", lambda e, t=t: e.dma_start(
                out=wstage[:, 0:1024], in_=Wo_d[t * 128:(t + 1) * 128, :]),
                waits=[(sV, sV.n), (sT, sT.n)], inc=sDS, inc_by=16)
            em.emit("vector", lambda e, t=t: e.tensor_copy(
                out=Wo_sb[:, t, :], in_=wstage[:, 0:1024]),
                waits=[(sDS, sDS.n)], inc=sV)
        em.emit("sync", lambda e: e.dma_start(out=brow_st[:, 0:D], in_=bo_d),
                waits=[(sV, sV.n)], inc=sDS, inc_by=16)
        em.emit("vector", lambda e: e.tensor_copy(out=bo_rowb, in_=brow_st[0:1, 0:D]),
                waits=[(sDS, sDS.n), (sT, sT.n)], inc=sV)
        vv = sV.n
        em.emit("sync", lambda e: e.dma_start(out=brow_st[:, 0:D], in_=bpr_d),
                waits=[(sV, vv)], inc=sDS, inc_by=16)
        em.emit("vector", lambda e: e.tensor_copy(out=bpr_rowb, in_=brow_st[0:1, 0:D]),
                waits=[(sDS, sDS.n)], inc=sV)
        v_f2rows = sV.n
        v_f2w = sDW.n

        hg_free = (sV, 0)
        x_free = (sV, 0)
        y_free = (sDS, 0)
        asl_free = (sT, 0)
        for i in range(NT):
            em.emit("sync", lambda e, i=i: e.dma_start(
                out=hgb[:], in_=hg_s[i, :, :]),
                waits=[(sDW, v_f2w), hg_free, (sDP, sDP.n)], inc=sDH, inc_by=16)
            v_hg = sDH.n
            em.emit("sync", lambda e, i=i: e.dma_start(
                out=asl[:], in_=at_s[:, i, :, :].rearrange("t p n -> p t n")),
                waits=[asl_free, (sDA, v_s_atdone)], inc=sDAL, inc_by=16)
            v_asl = sDAL.n
            em.emit("sync", lambda e, i=i: e.dma_start(
                out=xtile2[:], in_=x_d[i * 128:(i + 1) * 128, :]),
                waits=[x_free], inc=sDX, inc_by=16)
            v_x = sDX.n
            for ch in range(4):
                sl = slice(ch * 1024, (ch + 1) * 1024)
                em.emit("vector", lambda e, i=i, ch=ch, sl=sl: e.tensor_tensor(
                    out=th_c[:].rearrange("p (g k) -> p g k", k=32),
                    in0=hgb[:, sl].rearrange("p (g k) -> p g k", k=32),
                    in1=rsc_all[:, i, ch * 32:(ch + 1) * 32].unsqueeze(-1).to_broadcast([128, 32, 32]),
                    op=ALU.mult),
                    waits=[(sDH, v_hg)] + ([(sV, v_sch), (sV, v_f2rows)] if (i == 0 and ch == 0) else [])
                    + ([(sT, sT.n)] if ch == 0 else []), ser=True)
                em.emit("vector", lambda e, i=i, ch=ch, sl=sl: e.tensor_scalar(
                    out=hq_b[:, sl], in0=th_c[:], scalar1=192.0, scalar2=None,
                    op0=ALU.add),
                    waits=([(sDA, v_s_atdone)] if (i == 0 and ch == 0) else []),
                    ser=True)
                em.emit("vector", lambda e, sl=sl: e.tensor_scalar(
                    out=hq_b[:, sl], in0=hq_b[:, sl], scalar1=192.0, scalar2=None,
                    op0=ALU.subtract), ser=True)
                em.emit("vector", lambda e, i=i, ch=ch, sl=sl: e.tensor_tensor(
                    out=hq_b[:, sl].rearrange("p (g k) -> p g k", k=32),
                    in0=hq_b[:, sl].rearrange("p (g k) -> p g k", k=32),
                    in1=sc_all[:, i, ch * 32:(ch + 1) * 32].unsqueeze(-1).to_broadcast([128, 32, 32]),
                    op=ALU.mult), inc=sV, ser=True)
            v_hq = sV.n
            hg_free = (sV, v_hq)
            for h in range(8):
                _, ev_hqT = tr_batch(
                    4, lambda k, h=h: hq_b[:, (h * 4 + k) * 128:(h * 4 + k + 1) * 128],
                    lambda k, h=h: hqT[:, h * 4 + k, :],
                    tr_waits=([(sV, v_hq)] if h == 0 else []))
            for dch in range(2):
                pp = pACC[dch]
                for kt in range(NH):
                    w = [ev_hqT] if kt == 0 else []
                    em.emit("tensor", lambda e, dch=dch, kt=kt, pp=pp: e.matmul(
                        out=pp[:], lhsT=hqT[:, kt, :], rhs=wprT[:, kt, dch * 512:(dch + 1) * 512],
                        start=(kt == 0), stop=False), waits=w)
                for t in range(ND):
                    w = [(sDAL, v_asl)] if t == 0 else []
                    em.emit("tensor", lambda e, dch=dch, t=t, pp=pp: e.matmul(
                        out=pp[:], lhsT=asl[:, t, :], rhs=Wo_sb[:, t, dch * 512:(dch + 1) * 512],
                        start=False, stop=False), waits=w)
                em.emit("tensor", lambda e, dch=dch, pp=pp: e.matmul(
                    out=pp[:], lhsT=onr[:], rhs=bpr_rowb[0:1, dch * 512:(dch + 1) * 512],
                    start=False, stop=False))
                em.emit("tensor", lambda e, dch=dch, pp=pp: e.matmul(
                    out=pp[:], lhsT=onr[:], rhs=bo_rowb[0:1, dch * 512:(dch + 1) * 512],
                    start=False, stop=True), inc=sT)
                em.emit("vector", lambda e, dch=dch, pp=pp: e.tensor_tensor(
                    out=ybuf[:, 0:512] if dch == 0 else ybuf[:, 512:1024],
                    in0=xtile2[:, dch * 512:(dch + 1) * 512],
                    in1=pp[:], op=ALU.add),
                    waits=[(sT, sT.n), (sDX, v_x), y_free], inc=sV)
            asl_free = (sT, sT.n)
            x_free = (sV, sV.n)
            v_y = sV.n
            em.emit("sync", lambda e, i=i: e.dma_start(
                out=y_d[i * 128:(i + 1) * 128, :], in_=ybuf[:]),
                waits=[(sV, v_y)], inc=sDY, inc_by=16)
            y_free = (sDY, sDY.n)

        em.emit("sync", lambda e: e.nop(),
                waits=[(sDY, sDY.n), (sDS, sDS.n), (sDP, sDP.n), (sDA, sDA.n),
                       (sDX, sDX.n), (sDQ, sDQ.n), (sDV, sDV.n), (sDH, sDH.n),
                       (sDAL, sDAL.n), (sDW, sDW.n),
                       (sDWp[0], sDWp[0].n), (sDWp[1], sDWp[1].n),
                       (sVS, sVS.n), (sVC, sVC.n),
                       (sDC[0], sDC[0].n), (sDC[1], sDC[1].n),
                       (sDC2[0], sDC2[0].n), (sDC2[1], sDC2[1].n),
                       (sDQ2, sDQ2.n)])

        with nc.Block() as block:
            @block.sync
            def _(eng):
                for f in em.ops["sync"]:
                    f(eng)

            @block.vector
            def _(eng):
                for f in em.ops["vector"]:
                    f(eng)

            @block.scalar
            def _(eng):
                for f in em.ops["scalar"]:
                    f(eng)

            @block.tensor
            def _(eng):
                for f in em.ops["tensor"]:
                    f(eng)

            @block.gpsimd
            def _(eng):
                for f in em.ops["gpsimd"]:
                    f(eng)

    return nc


def host_inputs(inp):
    x = np.ascontiguousarray(np.asarray(inp["L0_attn_x"], dtype=np.float32))

    def f32(k):
        return np.ascontiguousarray(np.asarray(inp[k], dtype=np.float32))

    Wq, Wk, Wv, Wo = f32("L0_attn_Wq"), f32("L0_attn_Wk"), f32("L0_attn_Wv"), f32("L0_attn_Wo")
    bq, bk, bv, bo = f32("L0_attn_bq"), f32("L0_attn_bk"), f32("L0_attn_bv"), f32("L0_attn_bo")
    wfc, bfc = f32("L0_ffn_w_fc"), f32("L0_ffn_b_fc")
    wpr, bpr = f32("L0_ffn_w_proj"), f32("L0_ffn_b_proj")

    ident = np.eye(128, dtype=np.float32).astype(BF)
    r = np.arange(128)[:, None]
    cc = np.arange(512)[None, :]
    cmask = np.stack([np.where(cc > (o * 128 + r), np.float32(NEG), np.float32(0.0))
                      for o in range(4)])
    cmask = np.ascontiguousarray(cmask.transpose(1, 0, 2)).astype(BF)
    onc = np.ones((128, 1), dtype=np.float32).astype(BF)
    onr = np.ones((1, 128), dtype=np.float32).astype(BF)

    shared = {
        "Wq": Wq, "Wk": Wk, "Wv": Wv, "Wo": Wo,
        "bqt": np.ascontiguousarray(bq.reshape(ND, 128).T),
        "bkt": np.ascontiguousarray(bk.reshape(ND, 128).T),
        "bvt": np.ascontiguousarray(bv.reshape(ND, 128).T),
        "bo_r": np.ascontiguousarray(bo.reshape(1, D)),
        "w_fc": wfc, "b_fc_r": np.ascontiguousarray(bfc.reshape(1, H)),
        "w_proj": wpr, "b_proj_r": np.ascontiguousarray(bpr.reshape(1, D)),
        "ident": ident, "cmask": cmask, "ones_col": onc, "ones_row": onr,
    }
    return [dict(shared, x=np.ascontiguousarray(x[b])) for b in range(8)]


def kernel(**inputs):
    in_maps = host_inputs(inputs)
    nc = build_nc(debug_taps=False)
    res = run_bass_kernel_spmd(nc, in_maps, core_ids=list(range(8)))
    return np.stack([np.asarray(r["y"], dtype=np.float32) for r in res.results], axis=0)


# version 13 (final)
# speedup vs baseline: 1.0510x; 1.0510x over previous
import numpy as np
import ml_dtypes
import concourse.bass as bass
import concourse.mybir as mybir
from concourse.bass_utils import run_bass_kernel_spmd
from contextlib import ExitStack

F32 = mybir.dt.float32
BF16 = mybir.dt.bfloat16
AF = mybir.ActivationFunctionType
ALU = mybir.AluOpType
AX = mybir.AxisListType
BF = ml_dtypes.bfloat16

S, D, H = 4096, 1024, 4096
NT, ND, NH, QB = 32, 8, 32, 32
SCALE = 1.0 / 32.0
LN2 = float(np.log(2.0))
INVLN2 = float(1.0 / np.log(2.0))
NEG = -1e30


class Sem:
    def __init__(self, h):
        self.h = h
        self.n = 0

    def bump(self, k=1):
        self.n += k
        return self.n


class Em:

    def __init__(self, selfsems=None):
        self.ops = {k: [] for k in ("sync", "vector", "scalar", "tensor", "gpsimd")}
        self.selfsems = selfsems or {}

    SELF_SERIALIZE = False

    def emit(self, eng, fn, waits=(), inc=None, inc_by=1, ser=False):
        selfsem = self.selfsems.get(eng)
        if not (ser or self.SELF_SERIALIZE):
            selfsem = None
        pre = []
        if selfsem is not None:
            is_dma = inc is not None and inc is not selfsem
            pre = [(selfsem.h, selfsem.n)]
            if not is_dma:
                assert inc is None or inc is selfsem
                inc = selfsem
                inc_by = 1
        if inc is None:
            ms = self.selfsems.get(eng)
            if ms is not None:
                inc = ms
                inc_by = 1
        waits = pre + [(s.h, v) for (s, v) in waits if v > 0]
        best = {}
        for h, v in waits:
            if v > best.get(h, 0):
                best[h] = v
        waits = tuple(best.items())
        ih = inc.h if inc is not None else None
        if inc is not None:
            inc.bump(inc_by)

        def run(e, fn=fn, waits=waits, ih=ih, inc_by=inc_by):
            for h, v in waits:
                e.wait_ge(h, v)
            ins = fn(e)
            if ih is not None:
                ins.then_inc(ih, inc_by)
        self.ops[eng].append(run)


def build_nc(debug_taps=False):
    nc = bass.Bass("TRN2", target_bir_lowering=False)

    def din(name, shape, dtype=F32):
        return nc.dram_tensor(name, shape, dtype, kind="ExternalInput").ap()

    x_d = din("x", [S, D])
    Wq_d = din("Wq", [D, D]); Wk_d = din("Wk", [D, D]); Wv_d = din("Wv", [D, D])
    Wo_d = din("Wo", [D, D])
    bqt_d = din("bqt", [128, ND]); bkt_d = din("bkt", [128, ND]); bvt_d = din("bvt", [128, ND])
    bo_d = din("bo_r", [1, D])
    wfc_d = din("w_fc", [H, D]); bfc_d = din("b_fc_r", [1, H])
    wpr_d = din("w_proj", [D, H]); bpr_d = din("b_proj_r", [1, D])
    ident_d = din("ident", [128, 128], BF16)
    cmask_d = din("cmask", [128, 4, 512], BF16)
    onc_d = din("ones_col", [128, 1], BF16)
    onr_d = din("ones_row", [1, 128], BF16)

    y_d = nc.dram_tensor("y", [S, D], F32, kind="ExternalOutput").ap()

    def dsc(name, shape, dtype=BF16):
        return nc.dram_tensor(name, shape, dtype, kind="Internal").ap()

    qt_s = dsc("qt_s", [ND, QB, 128, 128])
    xqt_s = dsc("xqt_s", [ND, NT, 128, 128])
    at_s = dsc("at_s", [ND, QB, 128, 128])
    v_s = dsc("v_s", [NT, ND, 128, 128])
    vc_s = dsc("vc_s", [8, ND, 128, 128])
    wfct_s = dsc("wfct_s", [ND, NT, 128, 128])
    wprt_s = dsc("wprt_s", [NH, ND, 128, 128])
    hg_s = dsc("hg_s", [NT, 128, H])

    taps = {}
    if debug_taps:
        def mtap(name, shape, dtype=BF16):
            taps[name] = nc.dram_tensor(name, shape, dtype, kind="ExternalOutput").ap()
        mtap("t_xnT", [128, ND * S]); mtap("t_KT", [128, ND * S])
        mtap("t_KcT", [128, ND * 1024]); mtap("t_rawb", [128, S])
        mtap("t_vals", [128, 64]); mtap("t_attn", [128, D])
        mtap("t_wfcq", [128, D]); mtap("t_hg", [128, H])
        mtap("t_sc", [128, NT * 128]); mtap("t_xq", [128, D])
        mtap("t_vtb", [128, 512])
        mtap("t_vps", [128, 512], F32)
        mtap("t_tr0", [128, 128])
        mtap("t_wv0", [128, 1024]); mtap("t_wq0", [128, 1024])
        mtap("t_wv1", [128, 1024]); mtap("t_wv2", [128, 1024])
        mtap("t_sstat", [128, 16], F32); mtap("t_pcb", [128, 1024])
        mtap("t_csc", [128, 1024]); mtap("t_vsum", [1, D])


    em = Em()
    with ExitStack() as ctx:
        def sbt(name, shape, dtype=F32):
            return ctx.enter_context(nc.sbuf_tensor(name, shape, dtype))

        def pst(name, shape, dtype=F32):
            return ctx.enter_context(nc.psum_tensor(name, shape, dtype))

        def mksem(name):
            return Sem(ctx.enter_context(nc.semaphore(name)))

        sDS = mksem("sDS"); sDP = mksem("sDP")
        sT = mksem("sT"); sV = mksem("sV"); sC = mksem("sC"); sG = mksem("sG")
        sDX = mksem("sDX"); sDQ = mksem("sDQ"); sDV = mksem("sDV")
        sDC = [mksem("sDC0"), mksem("sDC1")]
        sDH = mksem("sDH"); sDAL = mksem("sDAL"); sDA = mksem("sDA")
        sDW = mksem("sDW"); sDY = mksem("sDY")
        sDWp = [mksem("sDWa"), mksem("sDWb")]
        sVS = mksem("sVS"); sVC = mksem("sVC")
        sDC2 = [mksem("sDC2a"), mksem("sDC2b")]
        sDQ2 = mksem("sDQ2"); sDH2 = mksem("sDH2")
        em.__init__(selfsems={"vector": sV, "scalar": sC, "gpsimd": sG})

        R1 = sbt("R1", [128, 32768], BF16)
        R2 = sbt("R2", [128, 32768], BF16)
        R3 = sbt("R3", [128, 32768], BF16)
        FA = sbt("FA", [128, 3072], F32)

        def r3(lo, n):
            return R3[:, lo:lo + n]

        def r3v(lo, nt, n):
            return R3[:, lo:lo + nt * n].rearrange("p (t n) -> p t n", n=n)

        xnT = R1[:].rearrange("p (t n) -> p t n", n=S)
        V_sb = R1[:].rearrange("p (t n) -> p t n", n=D)
        wfcT = xnT
        bo_rowb = R1[0:1, 0:D]
        bpr_rowb = R1[0:1, D:2 * D]
        KT = R2[:].rearrange("p (t n) -> p t n", n=S)
        wprT = R2[:].rearrange("p (t n) -> p t n", n=D)
        bfc_rowb = R2[0:1, 0:H]

        wq_bp = [r3(0, 1024), r3(1024, 1024)]
        wq_stp = [r3v(2048, 8, 128), r3v(3072, 8, 128)]
        rndW = r3(4096, 1024)
        xnb = r3(0, 1024); xqb = r3(1024, 1024); rx_b = r3(2048, 1024)
        ttile = r3v(3072, ND, 128)
        KcT = r3v(0, ND, 1024)
        rawb = r3(8192, 4096)
        t1a = r3(12288, 1024); t1b = r3(13312, 1024)
        g8 = [r3(14336, 512), r3(14848, 512)]
        wdiag = r3(12288, 512)
        Cb = [r3(12800, 512), r3(13312, 512)]
        vcs = [r3(13824, 512), r3(14336, 512)]
        coef_row = R3[0:1, 14848:14976]
        vals_b = r3(12288, 64)
        vals_dbg = r3(12352, 64)
        qsl = r3v(15360, ND, 128)
        vcs8 = [r3(16384 + i * 1024, 1024) for i in range(4)]
        ctp = [r3v(13824, 4, 128), r3v(14336, 4, 128)]
        qsls = [r3v(15360, 8, 128), r3v(24576, 8, 128)]
        vsum_rowb = R3[0:1, 25600:26624]
        cmask = R3[:, 26624:28672].rearrange("p (t n) -> p t n", n=512)
        c_sc = r3(28672, 1024)
        Pcb = r3(29696, 1024)
        PcT = r3v(30720, 8, 128)
        mselc = [r3(31744, 512), r3(32256, 512)]
        abuf = r3(31744, 1024)
        atile = r3v(12288, ND, 128)
        hgb = r3(8192, 4096)
        amax_all = r3(16384, 4096)
        amax_v = r3v(16384, NT, 128)
        rnd15 = r3(20480, 1024)
        T_all = r3v(21504, NT, 128)
        Wo_sb = r3v(0, ND, 1024)
        hq_b = r3(12288, 4096)
        hqT = r3v(16384, NH, 128)
        asl = r3v(20480, ND, 128)
        Wq_sb = r3v(8192, ND, 1024)
        Wk_sb = r3v(16384, ND, 1024)
        Wv_sb = r3v(24576, ND, 1024)

        FA0 = FA[:, 0:1024]; FA1 = FA[:, 1024:2048]; FA2 = FA[:, 2048:3072]
        wstage = FA[:, 0:1536]
        xtile = FA[:, 0:1024]
        xn_f = FA[:, 1024:2048]
        tx_f = FA[:, 2048:3072]
        Ebuf = [FA[:, 0:512], FA[:, 512:1024],
                FA[:, 1024:1536], FA[:, 1536:2048]]
        Ewb = [FA[:, 2048:2560], FA[:, 2560:3072]]
        th_c = FA[:, 0:1024]
        xtile2 = FA[:, 1024:2048]
        ybuf = FA[:, 2048:3072]
        brow_st = FA[0:1, 0:2048]
        f15 = (FA[:, 0:1024], FA[:, 1024:2048], FA[:, 2048:3072])

        ident = sbt("identsb", [128, 128], BF16)
        onc = sbt("onc", [128, 1], BF16)
        onr = sbt("onr", [1, 128], BF16)
        bq_sb = sbt("bq_sb", [128, ND]); bk_sb = sbt("bk_sb", [128, ND])
        bv_sb = sbt("bv_sb", [128, ND]); bqs_sb = sbt("bqs_sb", [128, ND])
        amaxw = sbt("amaxw", [128, 32]); scw = sbt("scw", [128, 32], BF16)
        rscw = sbt("rscw", [128, 32])
        tmpw = [sbt(f"tmpw{i}", [128, 32]) for i in range(3)]
        rndw = sbt("rndw", [128, 32], BF16)
        trev = [sbt(f"trev{i}", [128, 128], BF16) for i in range(2)]
        bnst = sbt("bnst", [128, 12]); bnag = sbt("bnag", [128, 2])
        vc_bf = sbt("vc_bf", [128, 128], BF16)
        vt_buf = sbt("vt_buf", [128, 512], BF16)
        sstat = sbt("sstat", [128, 16])
        tauf = sbt("tauf", [128, 2])
        swp = sbt("swp", [128, 8]); gsum = sbt("gsum", [128, 8])
        scp = sbt("scp", [128, 2])
        coefb = sbt("coefb", [128, 1], BF16)
        negc = sbt("negc", [128, 1])
        stat = sstat
        amaxx, scx, rscx, tmpx, rndx = amaxw, scw, rscw, tmpw, rndw
        vc_red = FA[:, 0:128]
        vpstap = FA[:, 1024:1536]

        pA = pst("pA", [128, 512]); pB = pst("pB", [128, 512])
        pC_ = pst("pC_", [128, 512]); pD = pst("pD", [128, 512])
        pACC = [pst("pACC0", [128, 512]), pst("pACC1", [128, 512])]
        pT1 = pst("pT1", [128, 512], BF16); pT2 = pst("pT2", [128, 512], BF16)

        def pts(s):
            bank = pT1 if (s % 8) < 4 else pT2
            q = s % 4
            return bank[:, q * 128:(q + 1) * 128]

        for dst, src in [(ident, ident_d), (onc, onc_d), (onr, onr_d),
                         (bq_sb, bqt_d), (bk_sb, bkt_d), (bv_sb, bvt_d)]:
            em.emit("sync", lambda e, d=dst, s=src: e.dma_start(out=d[:], in_=s),
                    inc=sDS, inc_by=16)
        v_consts = sDS.n
        em.emit("vector", lambda e: e.tensor_scalar(
            out=bqs_sb[:], in0=bq_sb[:], scalar1=SCALE, scalar2=None, op0=ALU.mult),
            waits=[(sDS, v_consts)], inc=sV)

        def scale_chain(G, amax_ap, sc_ap, rsc_ap, tmp, rnd_ap, wait_v, waits0=()):
            t0, t1, t2 = tmp
            em.emit("vector", lambda e: e.tensor_scalar(
                out=t0[:, :G], in0=amax_ap, scalar1=2.0 ** -40, scalar2=None,
                op0=ALU.max), waits=list(waits0) + [(sV, wait_v)], inc=sV, ser=True)
            em.emit("scalar", lambda e: e.activation(
                out=t1[:, :G], in_=t0[:, :G], func=AF.Ln), waits=[(sV, sV.n)], inc=sC)
            cc = sC.n
            em.emit("vector", lambda e: e.tensor_scalar(
                out=rnd_ap, in0=t1[:, :G], scalar1=INVLN2, scalar2=191.5,
                op0=ALU.mult, op1=ALU.add), waits=[(sC, cc)], ser=True)
            em.emit("vector", lambda e: e.tensor_scalar(
                out=t1[:, :G], in0=rnd_ap, scalar1=192.0, scalar2=None,
                op0=ALU.subtract), inc=sV, ser=True)
            em.emit("scalar", lambda e: e.activation(
                out=t2[:, :G], in_=t1[:, :G], func=AF.Exp, scale=LN2),
                waits=[(sV, sV.n)], inc=sC)
            cc = sC.n
            em.emit("vector", lambda e: e.tensor_scalar(
                out=t1[:, :G], in0=t2[:, :G], scalar1=2.0, scalar2=None, op0=ALU.mult),
                waits=[(sC, cc)], ser=True)
            em.emit("vector", lambda e: e.tensor_tensor(
                out=t1[:, :G], in0=t0[:, :G], in1=t1[:, :G], op=ALU.is_ge), ser=True)
            em.emit("vector", lambda e: e.tensor_scalar(
                out=t1[:, :G], in0=t1[:, :G], scalar1=1.0, scalar2=None, op0=ALU.add), ser=True)
            em.emit("vector", lambda e: e.tensor_tensor(
                out=t2[:, :G], in0=t2[:, :G], in1=t1[:, :G], op=ALU.mult), ser=True)
            em.emit("vector", lambda e: e.tensor_tensor(
                out=t1[:, :G], in0=t0[:, :G], in1=t2[:, :G], op=ALU.is_lt), ser=True)
            em.emit("vector", lambda e: e.tensor_scalar(
                out=t1[:, :G], in0=t1[:, :G], scalar1=-0.5, scalar2=1.0,
                op0=ALU.mult, op1=ALU.add), ser=True)
            em.emit("vector", lambda e: e.tensor_tensor(
                out=t2[:, :G], in0=t2[:, :G], in1=t1[:, :G], op=ALU.mult), ser=True)
            em.emit("vector", lambda e: e.tensor_scalar(
                out=sc_ap, in0=t2[:, :G], scalar1=0.125, scalar2=None,
                op0=ALU.mult), inc=sV, ser=True)
            def _recip(e):
                with nc.allow_low_precision(reason="reciprocal of exact powers of two is exact in bf16"):
                    return e.reciprocal(out=rsc_ap, in_=sc_ap)
            em.emit("vector", _recip, waits=[(sV, sV.n)], inc=sV, ser=True)
            return sV.n

        em.emit("vector", lambda e: e.memset(negc[:], -192.0 * LN2), inc=sV)

        def chainT(G, amax_ap, T_ap, t0, t2, t3, rnd_ap, wait_v):
            em.emit("vector", lambda e: e.tensor_scalar(
                out=t0, in0=amax_ap, scalar1=2.0 ** -40, scalar2=None,
                op0=ALU.max), waits=[(sV, wait_v)], inc=sV, ser=True)
            em.emit("scalar", lambda e: e.activation(
                out=t3, in_=t0, func=AF.Ln), waits=[(sV, sV.n)], inc=sC)
            em.emit("vector", lambda e: e.tensor_scalar(
                out=rnd_ap, in0=t3, scalar1=INVLN2, scalar2=191.5,
                op0=ALU.mult, op1=ALU.add), waits=[(sC, sC.n)], ser=True)
            em.emit("scalar", lambda e: e.activation(
                out=t2, in_=rnd_ap, func=AF.Exp, scale=LN2, bias=negc[:, 0:1]),
                waits=[(sV, sV.n)], inc=sC)
            em.emit("vector", lambda e: e.tensor_scalar(
                out=t3, in0=t2, scalar1=2.0, scalar2=None, op0=ALU.mult),
                waits=[(sC, sC.n)], ser=True)
            em.emit("vector", lambda e: e.tensor_tensor(
                out=t3, in0=t0, in1=t3, op=ALU.is_ge), ser=True)
            em.emit("vector", lambda e: e.tensor_scalar(
                out=t3, in0=t3, scalar1=1.0, scalar2=None, op0=ALU.add), ser=True)
            em.emit("vector", lambda e: e.tensor_tensor(
                out=t2, in0=t2, in1=t3, op=ALU.mult), ser=True)
            em.emit("vector", lambda e: e.tensor_tensor(
                out=t3, in0=t0, in1=t2, op=ALU.is_lt), ser=True)
            em.emit("vector", lambda e: e.tensor_scalar(
                out=t3, in0=t3, scalar1=-0.5, scalar2=1.0,
                op0=ALU.mult, op1=ALU.add), ser=True)
            em.emit("vector", lambda e: e.tensor_tensor(
                out=t2, in0=t2, in1=t3, op=ALU.mult), ser=True)
            em.emit("vector", lambda e: e.tensor_copy(out=rnd_ap, in_=t2), ser=True)
            em.emit("vector", lambda e: e.tensor_scalar(
                out=T_ap, in0=rnd_ap, scalar1=24.0, scalar2=None,
                op0=ALU.mult), inc=sV, ser=True)
            return sV.n

        tr_bank = [0]
        v_evac_bank = [None, None]
        v_store_last = [0, 0]

        def tr_batch(n, in_fn, out_fn, tr_waits=(), ev_waits=(), rows=128,
                     ev_eng="vector"):
            b = tr_bank[0]
            tr_bank[0] ^= 1
            bank = pT1 if b == 0 else pT2
            for k in range(n):
                w = list(tr_waits) if k == 0 else []
                if k == 0 and v_evac_bank[b] is not None:
                    w.append(v_evac_bank[b])
                em.emit("tensor", lambda e, k=k, bank=bank: e.transpose(
                    out=bank[0:rows, k * 128:(k + 1) * 128], in_=in_fn(k),
                    identity=ident[:]), waits=w, inc=(sT if k == n - 1 else None))
            v_end = sT.n
            esem = sV if ev_eng == "vector" else sC
            for k in range(n):
                w = [(sT, v_end)] + (list(ev_waits) if k == 0 else [])
                if ev_eng == "vector":
                    em.emit("vector", lambda e, k=k, bank=bank: e.tensor_copy(
                        out=out_fn(k), in_=bank[0:rows, k * 128:(k + 1) * 128]),
                        waits=w, inc=sV, ser=True)
                else:
                    em.emit("scalar", lambda e, k=k, bank=bank: e.copy(
                        out=out_fn(k), in_=bank[0:rows, k * 128:(k + 1) * 128]),
                        waits=w, inc=sC)
            v_evac_bank[b] = (esem, esem.n)
            return v_end, (esem, esem.n)

        def w_round(srcs, store_fns, tap0=None):
            n = len(srcs)
            wsA = [FA0, FA1]
            v_amax = [0] * n
            v_dma = [0] * n
            for i in range(n):
                p = i % 2
                w = [(sV, v_amax[i - 2])] if i >= 2 else [(sV, sV.n)]
                em.emit("sync", lambda e, i=i, p=p: e.dma_start(
                    out=wsA[p], in_=srcs[i]), waits=w, inc=sDC[p], inc_by=16)
                v_dma[i] = sDC[p].n
                em.emit("vector", lambda e, i=i, p=p: e.reduce_max(
                    out=FA2[:, i * 32:(i + 1) * 32],
                    in_=wsA[p].rearrange("p (g k) -> p g k", k=32),
                    axis=AX.X, apply_absolute_value=True),
                    waits=[(sDC[p], v_dma[i])], inc=sV)
                v_amax[i] = sV.n
            v_chain = chainT(n * 32, FA2[:, 0:n * 32], FA0[:, 0:n * 32],
                             FA0[:, 0:n * 32] if False else FA0[:, 0:n * 32],
                             FA1[:, 0:n * 32], FA2[:, 0:n * 32],
                             rndW[:, 0:n * 32], sV.n)
            wsB = [FA1, FA2]
            v_add = [0] * n
            v_sub = [0] * n
            v_tr_last = [0] * n
            for i in range(n):
                p = i % 2
                w = [(sV, v_add[i - 2])] if i >= 2 else [(sV, v_chain)]
                em.emit("sync", lambda e, i=i, p=p: e.dma_start(
                    out=wsB[p], in_=srcs[i]), waits=w, inc=sDC[p], inc_by=16)
                v_dmab = sDC[p].n
                wadd = [(sDC[p], v_dmab)]
                if i >= 2:
                    wadd.append((sT, v_tr_last[i - 2]))
                em.emit("vector", lambda e, i=i, p=p: e.tensor_tensor(
                    out=wq_bp[p].rearrange("p (g k) -> p g k", k=32),
                    in0=wsB[p].rearrange("p (g k) -> p g k", k=32),
                    in1=FA0[:, i * 32:(i + 1) * 32].unsqueeze(-1)
                        .to_broadcast([128, 32, 32]),
                    op=ALU.add), waits=wadd, inc=sV, ser=True)
                v_add[i] = sV.n
                em.emit("vector", lambda e, i=i, p=p: e.tensor_tensor(
                    out=wq_bp[p].rearrange("p (g k) -> p g k", k=32),
                    in0=wq_bp[p].rearrange("p (g k) -> p g k", k=32),
                    in1=FA0[:, i * 32:(i + 1) * 32].unsqueeze(-1)
                        .to_broadcast([128, 32, 32]),
                    op=ALU.subtract), inc=sV, ser=True)
                v_sub[i] = sV.n
                if tap0 is not None and i == 0:
                    em.emit("sync", lambda e: e.dma_start(
                        out=tap0, in_=wq_bp[0]),
                        waits=[(sV, v_sub[0])], inc=sDS, inc_by=16)
                for h in range(2):
                    tr_batch(
                        4,
                        lambda k, p=p, h=h: wq_bp[p][:, (h * 4 + k) * 128:
                                                     (h * 4 + k + 1) * 128],
                        lambda k, p=p, h=h: wq_stp[p][:, h * 4 + k, :],
                        tr_waits=([(sV, v_sub[i])] if h == 0 else []),
                        ev_waits=([(sDWp[p], v_store_last[p])] if h == 0 else []))
                v_tr_last[i] = sT.n
                em.emit("gpsimd", lambda e, i=i, p=p: e.dma_start(
                    out=store_fns[i], in_=wq_stp[p][:]),
                    waits=[(sV, sV.n)], inc=sDWp[p], inc_by=16)
                v_store_last[p] = sDWp[p].n

        fc_srcs = [wfc_d[i * 128:(i + 1) * 128, :] for i in range(NT)]
        fc_stores = [wfct_s[:, i, :, :].rearrange("t p n -> p t n")
                     for i in range(NT)]
        w_round(fc_srcs, fc_stores,
                tap0=(taps["t_wfcq"] if debug_taps else None))
        pr_srcs = []
        pr_stores = []
        for i in range(ND):
            for hf in range(4):
                pr_srcs.append(wpr_d[i * 128:(i + 1) * 128,
                                     hf * 1024:(hf + 1) * 1024])
                pr_stores.append(
                    wprt_s[hf * 8:(hf + 1) * 8, i, :, :]
                    .rearrange("t p n -> p t n"))
        w_round(pr_srcs, pr_stores)
        v_wst = [sDWp[0].n, sDWp[1].n]

        wqkv_list = []
        for W_d, W_sb in [(Wq_d, Wq_sb), (Wk_d, Wk_sb), (Wv_d, Wv_sb)]:
            for t in range(ND):
                wqkv_list.append((W_d[t * 128:(t + 1) * 128, :], W_sb, t))
        wsC = [FA1, FA2]
        v_cp = [0] * len(wqkv_list)
        for j, (src_ap, W_sb, t) in enumerate(wqkv_list):
            p = j % 2
            w = [(sV, v_cp[j - 2])] if j >= 2 else [(sV, sV.n)]
            em.emit("sync", lambda e, src_ap=src_ap, p=p: e.dma_start(
                out=wsC[p], in_=src_ap), waits=w, inc=sDC[p], inc_by=16)
            vd = sDC[p].n
            em.emit("vector", lambda e, W_sb=W_sb, t=t, p=p: e.tensor_copy(
                out=W_sb[:, t, :], in_=wsC[p]),
                waits=[(sDC[p], vd)], inc=sV)
            v_cp[j] = sV.n
        v_wqkv = sV.n
        if debug_taps:
            em.emit("sync", lambda e: e.dma_start(out=taps["t_wv0"], in_=Wv_sb[:, 0, :]),
                    waits=[(sV, v_wqkv)], inc=sDS, inc_by=16)
            em.emit("sync", lambda e: e.dma_start(out=taps["t_wq0"], in_=Wq_sb[:, 0, :]),
                    waits=[(sV, v_wqkv)], inc=sDS, inc_by=16)

        xt_free = (sV, 0)
        v_w_T = sT.n
        v_w_G = sG.n
        for i in range(NT):
            em.emit("sync", lambda e, i=i: e.dma_start(
                out=xtile[:], in_=x_d[i * 128:(i + 1) * 128, :]),
                waits=[xt_free] + ([(sT, v_w_T), (sG, v_w_G), (sV, v_wqkv)]
                                   if i == 0 else []),
                inc=sDX, inc_by=16)
            vdma = sDX.n
            em.emit("vector", lambda e: e.bn_stats(out=bnst[:, 0:6], in_=xtile[:, 0:512]),
                    waits=[(sDX, vdma)] + ([(sT, v_w_T), (sG, v_w_G)] if i == 0 else []))
            em.emit("vector", lambda e: e.bn_stats(out=bnst[:, 6:12], in_=xtile[:, 512:1024]))
            em.emit("vector", lambda e: e.bn_aggr(out=bnag[:], in_=bnst[:]), ser=True)
            em.emit("vector", lambda e: e.tensor_scalar(
                out=stat[:, 5:6], in0=bnag[:, 1:2], scalar1=1e-5, scalar2=None,
                op0=ALU.add), inc=sV, ser=True)
            em.emit("scalar", lambda e: e.activation(
                out=stat[:, 2:3], in_=stat[:, 5:6], func=AF.Ln),
                waits=[(sV, sV.n)], inc=sC)
            em.emit("vector", lambda e: e.tensor_scalar(
                out=stat[:, 3:4], in0=stat[:, 2:3], scalar1=-0.5, scalar2=None,
                op0=ALU.mult), waits=[(sC, sC.n)], inc=sV)
            em.emit("scalar", lambda e: e.activation(
                out=stat[:, 4:5], in_=stat[:, 3:4], func=AF.Exp),
                waits=[(sV, sV.n)], inc=sC)
            em.emit("vector", lambda e: e.tensor_scalar(
                out=xn_f[:], in0=xtile[:], scalar1=bnag[:, 0:1], scalar2=stat[:, 4:5],
                op0=ALU.subtract, op1=ALU.mult), waits=[(sC, sC.n)], inc=sV, ser=True)
            xt_free = (sV, sV.n)
            em.emit("vector", lambda e: e.tensor_copy(out=xnb[:], in_=xn_f[:]),
                    waits=[(sT, sT.n)], inc=sV, ser=True)
            v_xn = sV.n
            em.emit("vector", lambda e: e.reduce_max(
                out=amaxx[:], in_=xn_f[:].rearrange("p (g k) -> p g k", k=32),
                axis=AX.X, apply_absolute_value=True), inc=sV, ser=True)
            v_sc = scale_chain(32, amaxx[:], scx[:], rscx[:], tmpx, rndx[:], sV.n)
            em.emit("vector", lambda e: e.tensor_tensor(
                out=tx_f[:].rearrange("p (g k) -> p g k", k=32),
                in0=xn_f[:].rearrange("p (g k) -> p g k", k=32),
                in1=rscx[:].unsqueeze(-1).to_broadcast([128, 32, 32]),
                op=ALU.mult), waits=[(sV, v_sc)], ser=True)
            em.emit("vector", lambda e: e.tensor_scalar(
                out=rx_b[:], in0=tx_f[:], scalar1=192.0, scalar2=None, op0=ALU.add),
                waits=([(sDWp[0], v_wst[0])] if i == 0 else []), ser=True)
            em.emit("vector", lambda e: e.tensor_scalar(
                out=rx_b[:], in0=rx_b[:], scalar1=192.0, scalar2=None, op0=ALU.subtract), ser=True)
            em.emit("vector", lambda e: e.tensor_tensor(
                out=xqb[:].rearrange("p (g k) -> p g k", k=32),
                in0=rx_b[:].rearrange("p (g k) -> p g k", k=32),
                in1=scx[:].unsqueeze(-1).to_broadcast([128, 32, 32]),
                op=ALU.mult), waits=[(sT, sT.n)], inc=sV, ser=True)
            v_xq = sV.n
            if debug_taps and i == 0:
                em.emit("sync", lambda e: e.dma_start(out=taps["t_xq"], in_=xqb[:]),
                        waits=[(sV, v_xq)], inc=sDS, inc_by=16)
            for t in range(ND):
                pt = pT1 if t % 2 == 0 else pT2
                em.emit("tensor", lambda e, t=t, pt=pt: e.transpose(
                    out=pt[:, 0:128], in_=xnb[:, t * 128:(t + 1) * 128], identity=ident[:]),
                    waits=[(sV, v_xn), (sC, sC.n)], inc=sT)
                em.emit("scalar", lambda e, t=t, i=i, pt=pt: e.copy(
                    out=xnT[:, t, i * 128:(i + 1) * 128], in_=pt[:, 0:128]),
                    waits=[(sT, sT.n)], inc=sC)
            for t in range(ND):
                pt = pT1 if t % 2 == 0 else pT2
                em.emit("tensor", lambda e, t=t, pt=pt: e.transpose(
                    out=pt[:, 0:128], in_=xqb[:, t * 128:(t + 1) * 128], identity=ident[:]),
                    waits=[(sV, v_xq), (sC, sC.n), (sDP, sDP.n)], inc=sT)
                em.emit("scalar", lambda e, t=t, pt=pt: e.copy(
                    out=ttile[:, t, :], in_=pt[:, 0:128]),
                    waits=[(sT, sT.n)] + ([(sDWp[1], v_wst[1])]
                                          if (i == 0 and t == 0) else []), inc=sC)
            em.emit("gpsimd", lambda e, i=i: e.dma_start(
                out=xqt_s[:, i, :, :].rearrange("t p n -> p t n"), in_=ttile[:]),
                waits=[(sC, sC.n)], inc=sDP, inc_by=16)
        if debug_taps:
            em.emit("sync", lambda e: e.dma_start(
                out=taps["t_xnT"], in_=R1[:]), waits=[(sC, sC.n)], inc=sDS, inc_by=16)

        v_b_done = sC.n
        if debug_taps:
            em.emit("sync", lambda e: e.dma_start(out=taps["t_wv1"], in_=Wv_sb[:, 0, :]),
                    waits=[(sC, v_b_done), (sV, sV.n)], inc=sDS, inc_by=16)
        pk_free = {0: (sC, 0), 1: (sC, 0)}
        v_vcred = 0
        v_vstore = 0
        v_vcstore = 0
        vtt = vt_buf[:].rearrange("p (t n) -> p t n", n=128)
        for proj, W_sb in enumerate([Wq_sb, Wk_sb, Wv_sb]):
            if debug_taps and proj == 2:
                em.emit("sync", lambda e: e.dma_start(out=taps["t_wv2"], in_=Wv_sb[:, 0, :]),
                        waits=[(sC, sC.n), (sV, sV.n)], inc=sDS, inc_by=16)
            for c in range(8):
                for m in range(ND):
                    bk = (c * ND + m) % 2
                    pp = pA if bk == 0 else pB
                    for k in range(ND):
                        w = []
                        if proj == 0 and c == 0 and m == 0 and k == 0:
                            w = [(sC, v_b_done), (sV, v_wqkv)]
                        if k == 0:
                            w = w + [pk_free[bk]]
                        em.emit("tensor", lambda e, W_sb=W_sb, m=m, k=k, c=c, pp=pp: e.matmul(
                            out=pp[:], lhsT=W_sb[:, k, m * 128:(m + 1) * 128],
                            rhs=xnT[:, k, c * 512:(c + 1) * 512],
                            start=(k == 0), stop=(k == ND - 1)),
                            waits=w, inc=(sT if k == ND - 1 else None))
                    v_mm = sT.n
                    if proj == 0:
                        em.emit("scalar", lambda e, m=m, pp=pp: e.activation(
                            out=vt_buf[:], in_=pp[:], func=AF.Identity,
                            bias=bqs_sb[:, m:m + 1], scale=SCALE),
                            waits=[(sT, v_mm), (sDP, sDP.n)], inc=sC)
                        pk_free[bk] = (sC, sC.n)
                        em.emit("gpsimd", lambda e, m=m, c=c: e.dma_start(
                            out=qt_s[m, c * 4:(c + 1) * 4, :, :].rearrange("b p n -> p b n"),
                            in_=vt_buf[:].rearrange("p (b n) -> p b n", n=128)),
                            waits=[(sC, sC.n)], inc=sDP, inc_by=16)
                    elif proj == 1:
                        em.emit("scalar", lambda e, m=m, c=c, pp=pp: e.activation(
                            out=KT[:, m, c * 512:(c + 1) * 512], in_=pp[:],
                            func=AF.Identity, bias=bk_sb[:, m:m + 1]),
                            waits=[(sT, v_mm)], inc=sC)
                        pk_free[bk] = (sC, sC.n)
                        em.emit("vector", lambda e, m=m, c=c: e.reduce_sum(
                            out=vc_red[:],
                            in_=KT[:, m, c * 512:(c + 1) * 512].rearrange("p (g k) -> p g k", k=4),
                            axis=AX.X), waits=[(sC, sC.n)], ser=True)
                        em.emit("vector", lambda e, m=m, c=c: e.tensor_scalar(
                            out=KcT[:, m, c * 128:(c + 1) * 128], in0=vc_red[:],
                            scalar1=0.25, scalar2=None, op0=ALU.mult), inc=sV, ser=True)
                    else:
                        em.emit("scalar", lambda e, m=m, pp=pp: e.activation(
                            out=vt_buf[:], in_=pp[:], func=AF.Identity,
                            bias=bv_sb[:, m:m + 1]),
                            waits=[(sT, v_mm), (sVS, v_vstore), (sV, v_vcred),
                                   (sDP, sDP.n)],
                            inc=sC)
                        v_ev = sC.n
                        if debug_taps and c == 0 and m == 0:
                            em.emit("scalar", lambda e, pp=pp: e.copy(
                                out=vpstap[:], in_=pp[:]), waits=[(sC, v_ev)], inc=sC)
                            em.emit("sync", lambda e: e.dma_start(
                                out=taps["t_vps"], in_=vpstap[:]),
                                waits=[(sC, sC.n)], inc=sDS, inc_by=16)
                            pk_free[bk] = (sC, sC.n)
                        else:
                            pk_free[bk] = (sC, v_ev)
                        v_ev = sC.n
                        if debug_taps and c == 0 and m == 0:
                            em.emit("sync", lambda e: e.dma_start(
                                out=taps["t_vtb"], in_=vt_buf[:]),
                                waits=[(sC, v_ev)], inc=sDS, inc_by=16)
                        em.emit("vector", lambda e: e.reduce_sum(
                            out=vc_red[:],
                            in_=vt_buf[:].rearrange("p (g k) -> p g k", k=4),
                            axis=AX.X), waits=[(sC, v_ev)], inc=sV, ser=True)
                        v_vcred = sV.n
                        tr_batch(
                            4,
                            lambda k: vt_buf[:, k * 128:(k + 1) * 128],
                            lambda k: vtt[:, k, :],
                            tr_waits=[(sC, v_ev)],
                            ev_waits=[(sVS, v_vstore)])
                        em.emit("gpsimd", lambda e, m=m, c=c: e.dma_start(
                            out=v_s[c * 4:(c + 1) * 4, m, :, :]
                            .rearrange("t p n -> p t n"), in_=vtt),
                            waits=[(sV, sV.n)], inc=sVS, inc_by=16)
                        v_vstore = sVS.n
                        em.emit("vector", lambda e: e.tensor_scalar(
                            out=vc_bf[:], in0=vc_red[:], scalar1=0.25, scalar2=None,
                            op0=ALU.mult), waits=[(sVC, v_vcstore)], inc=sV, ser=True)
                        tr_batch(
                            1,
                            lambda k: vc_bf[:],
                            lambda k: vc_bf[:],
                            tr_waits=[(sV, sV.n)])
                        em.emit("gpsimd", lambda e, m=m, c=c: e.dma_start(
                            out=vc_s[c, m, :, :], in_=vc_bf[:]),
                            waits=[(sV, sV.n)], inc=sVC, inc_by=16)
                        v_vcstore = sVC.n
        v_cC, v_cV, v_cDP, v_cT = sC.n, sV.n, sDP.n, sT.n
        v_cVS, v_cVC = sVS.n, sVC.n
        if debug_taps:
            em.emit("sync", lambda e: e.dma_start(out=taps["t_KT"], in_=R2[:]),
                    waits=[(sC, v_cC), (sV, v_cV)], inc=sDS, inc_by=16)
            em.emit("sync", lambda e: e.dma_start(out=taps["t_KcT"], in_=R3[:, 0:8192]),
                    waits=[(sC, v_cC), (sV, v_cV)], inc=sDS, inc_by=16)

        em.emit("sync", lambda e: e.dma_start(
            out=cmask[:], in_=cmask_d),
            waits=[(sT, v_cT)], inc=sDS, inc_by=16)
        v_cmask = sDS.n
        for t in range(NT):
            em.emit("sync", lambda e, t=t: e.dma_start(
                out=V_sb[:, t, :].rearrange("p (m n) -> p m n", n=128),
                in_=v_s[t, :, :, :].rearrange("m p n -> p m n")),
                waits=[(sVS, v_cVS), (sT, v_cT)] if t == 0 else [],
                inc=sDV, inc_by=16)

        v_vload = sDV.n
        for dch in range(2):
            pp = pACC[dch]
            for kt in range(NT):
                em.emit("tensor", lambda e, dch=dch, kt=kt, pp=pp: e.matmul(
                    out=pp[0:1, :], lhsT=onc[:], rhs=V_sb[:, kt, dch * 512:(dch + 1) * 512],
                    start=(kt == 0), stop=(kt == NT - 1)),
                    waits=([(sDV, v_vload)] if kt == 0 else []),
                    inc=(sT if kt == NT - 1 else None))
            em.emit("scalar", lambda e, dch=dch, pp=pp: e.copy(
                out=vsum_rowb[:, dch * 512:(dch + 1) * 512], in_=pp[0:1, :]),
                waits=[(sT, sT.n)], inc=sC)
        v_vsum = sC.n

        st_state = {}
        csc_free = {0: (sC, 0), 1: (sC, 0)}
        raw_free = {0: (sC, 0), 1: (sC, 0)}
        st_raw = [0, 0]
        st_q = [0, 0]
        st_qsl_free = [(sT, 0), (sT, 0)]
        v_tail_end = [0, 0]
        v_prev_tail = [0]
        prev_vcmm = [0] * 8
        rawbs = [rawb, r3(20480, 4096)]

        def emit_scores(qb):
            rawb_p = rawbs[qb % 2]
            qsl_p = qsls[qb % 2]
            sq = sDQ if qb % 2 == 0 else sDQ2
            em.emit("sync", lambda e, qb=qb, qsl_p=qsl_p: e.dma_start(
                out=qsl_p[:], in_=qt_s[:, qb, :, :].rearrange("t p n -> p t n")),
                waits=[st_qsl_free[qb % 2]] + ([(sDP, v_cDP)] if qb == 0 else []),
                inc=sq, inc_by=16)
            v_q = sq.n
            st_q[qb % 2] = v_q
            for c in range(8):
                bk = c % 2
                pp = pA if bk == 0 else pB
                for k in range(ND):
                    w = [(sq, v_q)] if (c == 0 and k == 0) else []
                    if qb == 0 and c == 0 and k == 0:
                        w = w + [pk_free[0], pk_free[1]]
                    if k == 0:
                        w = w + [raw_free[bk]]
                    em.emit("tensor", lambda e, k=k, c=c, pp=pp, qsl_p=qsl_p: e.matmul(
                        out=pp[:], lhsT=qsl_p[:, k, :], rhs=KT[:, k, c * 512:(c + 1) * 512],
                        start=(k == 0), stop=(k == ND - 1)),
                        waits=w, inc=(sT if k == ND - 1 else None))
                em.emit("scalar", lambda e, c=c, pp=pp: e.copy(
                    out=rawb_p[:, c * 512:(c + 1) * 512], in_=pp[:]),
                    waits=[(sT, sT.n)] + ([(sV, v_tail_end[qb % 2])] if c == 0 else []), inc=sC)
                raw_free[bk] = (sC, sC.n)
            st_raw[qb % 2] = sC.n
            if debug_taps and qb == 1:
                em.emit("sync", lambda e: e.dma_start(out=taps["t_rawb"], in_=rawb_p[:]),
                        waits=[(sC, st_raw[1])], inc=sDS, inc_by=16)
            st_state["raw_done"] = (sT, sT.n)


        def emit_tail(qb):
            dc = qb // 4
            off = qb % 4
            rawb_p = rawbs[qb % 2]
            qsl_p = qsls[qb % 2]
            for cc_ in range(2):
                pp = pC_ if cc_ == 0 else pD
                for k in range(ND):
                    w = [((sDQ if qb % 2 == 0 else sDQ2), st_q[qb % 2])] \
                        if (cc_ == 0 and k == 0) else []
                    if qb == 0 and cc_ == 0 and k == 0:
                        w = w + [(sV, v_cV)]
                    if k == 0:
                        w = w + [csc_free[cc_]]
                    em.emit("tensor", lambda e, cc_=cc_, k=k, pp=pp, qsl_p=qsl_p: e.matmul(
                        out=pp[:], lhsT=qsl_p[:, k, :], rhs=KcT[:, k, cc_ * 512:(cc_ + 1) * 512],
                        start=(k == 0), stop=(k == ND - 1)),
                        waits=w, inc=(sT if k == ND - 1 else None))
                em.emit("scalar", lambda e, cc_=cc_, pp=pp: e.copy(
                    out=c_sc[:, cc_ * 512:(cc_ + 1) * 512], in_=pp[:]),
                    waits=[(sT, sT.n)] + ([(sV, v_prev_tail[0])] if cc_ == 0 else []), inc=sC)
                csc_free[cc_] = (sC, sC.n)
            v_csc = sC.n
            st_qsl_free[qb % 2] = (sT, sT.n)
            v_vcmm = [0] * 8

            def emit_vc_load(j, w):
                vb = vcs8[j % 4]
                eng, sem = (("sync", sDC[0]) if j % 2 == 0
                            else ("gpsimd", sDC2[0]))
                em.emit(eng, lambda e, j=j, vb=vb: e.dma_start(
                    out=vb[:].rearrange("p (m n) -> p m n", n=128),
                    in_=vc_s[j, :, :, :].rearrange("m p n -> p m n")),
                    waits=w, inc=sem, inc_by=16)

            for j in range(4):
                emit_vc_load(j, ([(sVC, v_cVC), (sT, v_cT)] if qb == 0
                                 else [(sT, prev_vcmm[j + 4])]))
            em.emit("vector", lambda e: e.tensor_max(
                out=t1a[:], in0=rawb_p[:, 0:1024], in1=rawb_p[:, 1024:2048]),
                waits=[(sC, st_raw[qb % 2]), (sDA, sDA.n)])
            em.emit("vector", lambda e: e.tensor_max(
                out=t1b[:], in0=rawb_p[:, 2048:3072], in1=rawb_p[:, 3072:4096]))
            em.emit("vector", lambda e: e.tensor_max(
                out=g8[0][:], in0=t1a[:, 0:512], in1=t1a[:, 512:1024]), ser=True)
            em.emit("vector", lambda e: e.tensor_max(
                out=g8[0][:], in0=g8[0][:], in1=t1b[:, 0:512]), ser=True)
            em.emit("vector", lambda e: e.tensor_max(
                out=g8[0][:], in0=g8[0][:], in1=t1b[:, 512:1024]), inc=sV, ser=True)
            em.emit("vector", lambda e: e.tensor_max(
                out=g8[1][:, 0:256], in0=g8[0][:, 0:256], in1=g8[0][:, 256:512]),
                inc=sV, ser=True)
            for it in range(8):
                cur = g8[(it + 1) % 2][:, 0:256]
                nxt = g8[it % 2][:, 0:256]
                em.emit("vector", lambda e, it=it, cur=cur: e.max(
                    out=vals_b[:, it * 8:(it + 1) * 8], in_=cur),
                    waits=[(sV, sV.n)], inc=sV)
                if it < 7:
                    em.emit("vector", lambda e, it=it, cur=cur, nxt=nxt: e.match_replace(
                        out=nxt, in_to_replace=vals_b[:, it * 8:(it + 1) * 8],
                        in_values=cur, imm_value=NEG),
                        waits=[(sV, sV.n)], inc=sV)
            v_vals = sV.n
            if debug_taps and qb == 1:
                em.emit("vector", lambda e: e.tensor_copy(out=vals_dbg[:], in_=vals_b[:]),
                        waits=[(sV, v_vals)], inc=sV)
                em.emit("sync", lambda e: e.dma_start(out=taps["t_vals"], in_=vals_dbg[:]),
                        waits=[(sV, sV.n)], inc=sDS, inc_by=16)
            em.emit("vector", lambda e: e.tensor_copy(out=tauf[:, 0:1], in_=vals_b[:, 63:64]),
                    waits=[(sV, v_vals)])
            em.emit("vector", lambda e: e.tensor_scalar(
                out=sstat[:, 0:1], in0=vals_b[:, 0:1], scalar1=0.0, scalar2=None,
                op0=ALU.max))
            em.emit("vector", lambda e: e.tensor_scalar(
                out=sstat[:, 1:2], in0=sstat[:, 0:1], scalar1=-1.0, scalar2=None,
                op0=ALU.mult), inc=sV, ser=True)
            v_mu = sV.n
            em.emit("scalar", lambda e: e.activation(
                out=sstat[:, 2:3], in_=sstat[:, 0:1], func=AF.Exp, scale=-1.0),
                waits=[(sV, v_mu)], inc=sC)
            v_emu = sC.n

            v_stt = [0] * 8
            for c in range(8):
                eb = Ebuf[c % 4]
                ms = mselc[c % 2]
                em.emit("scalar", lambda e, c=c, eb=eb: e.activation(
                    out=eb[:], in_=rawb_p[:, c * 512:(c + 1) * 512], func=AF.Exp,
                    bias=sstat[:, 1:2]),
                    waits=([(sV, v_mu)] if c == 0 else [])
                    + ([(sV, v_stt[c - 4])] if c >= 4 else []),
                    inc=sC)
                v_e = sC.n
                em.emit("vector", lambda e, c=c, ms=ms: e.tensor_scalar(
                    out=ms[:], in0=rawb_p[:, c * 512:(c + 1) * 512], scalar1=tauf[:, 0:1],
                    scalar2=None, op0=ALU.is_ge),
                    waits=[(sV, v_mu)] if c == 0 else [])
                em.emit("vector", lambda e, c=c, eb=eb, ms=ms: e.scalar_tensor_tensor(
                    out=Cb[1][:], in0=eb[:], scalar=sstat[:, 2:3],
                    in1=ms[:], op0=ALU.subtract, op1=ALU.mult,
                    accum_out=gsum[:, c:c + 1]),
                    waits=[(sC, v_e), (sC, v_emu)], inc=sV, ser=True)
                v_stt[c] = sV.n
            v_g = sV.n
            em.emit("vector", lambda e: e.reduce_sum(
                out=sstat[:, 14:15], in_=gsum[:], axis=AX.X), waits=[(sV, v_g)])
            em.emit("vector", lambda e: e.scalar_tensor_tensor(
                out=sstat[:, 15:16], in0=sstat[:, 2:3], scalar=4096.0,
                in1=sstat[:, 14:15], op0=ALU.mult, op1=ALU.add), ser=True)
            em.emit("vector", lambda e: e.tensor_scalar(
                out=sstat[:, 15:16], in0=sstat[:, 15:16], scalar1=3.0, scalar2=None,
                op0=ALU.mult), ser=True)
            em.emit("vector", lambda e: e.reciprocal(
                out=sstat[:, 14:15], in_=sstat[:, 15:16]), inc=sV, ser=True)
            v_rz3 = sV.n
            em.emit("vector", lambda e: e.tensor_scalar(
                out=sstat[:, 3:4], in0=sstat[:, 2:3], scalar1=sstat[:, 14:15],
                scalar2=None, op0=ALU.mult), waits=[(sV, v_rz3)], inc=sV, ser=True)
            v_biasg = sV.n
            em.emit("vector", lambda e: e.tensor_copy(
                out=coefb[:], in_=sstat[:, 3:4]), inc=sV, ser=True)
            v_coef = sV.n
            _, ev_coefrow = tr_batch(
                1, lambda k: coefb[:], lambda k: coef_row,
                tr_waits=[(sV, v_coef)], rows=1, ev_eng="scalar")

            v_tr_chunk = {}
            v_mm_chunk = {}
            v_stt2 = [0] * 8
            order = list(range(dc + 1, 8)) + list(range(dc + 1))
            v_biasw = 0
            for idx, c in enumerate(order):
                if c == 0:
                    em.emit("vector", lambda e, off=off, dc=dc: e.tensor_tensor(
                        out=wdiag[:], in0=rawb_p[:, dc * 512:(dc + 1) * 512],
                        in1=cmask[:, off, :], op=ALU.add),
                        waits=([(sDS, v_cmask)] if qb == 0 else []))
                    if dc > 0:
                        em.emit("vector", lambda e, dc=dc: e.reduce_max(
                            out=sstat[:, 4:5], in_=rawb_p[:, 0:dc * 512], axis=AX.X))
                        em.emit("vector", lambda e: e.reduce_max(
                            out=sstat[:, 5:6], in_=wdiag[:], axis=AX.X), ser=True)
                        em.emit("vector", lambda e: e.tensor_tensor(
                            out=sstat[:, 6:7], in0=sstat[:, 4:5], in1=sstat[:, 5:6],
                            op=ALU.max), ser=True)
                    else:
                        em.emit("vector", lambda e: e.reduce_max(
                            out=sstat[:, 6:7], in_=wdiag[:], axis=AX.X), ser=True)
                    em.emit("vector", lambda e: e.tensor_scalar(
                        out=sstat[:, 7:8], in0=sstat[:, 6:7], scalar1=-1.0,
                        scalar2=None, op0=ALU.mult), inc=sV, ser=True)
                    v_nmw = sV.n
                    for cw in range(dc + 1):
                        if cw < dc:
                            fn = (lambda e, cw=cw: e.activation(
                                out=Ewb[0][:],
                                in_=rawb_p[:, cw * 512:(cw + 1) * 512],
                                func=AF.Exp, bias=sstat[:, 7:8],
                                accum_out=swp[:, cw:cw + 1]))
                        else:
                            fn = (lambda e, cw=cw: e.activation(
                                out=Ewb[0][:], in_=wdiag[:],
                                func=AF.Exp, bias=sstat[:, 7:8],
                                accum_out=swp[:, cw:cw + 1]))
                        em.emit("scalar", fn,
                                waits=[(sV, v_nmw)] if cw == 0 else [],
                                inc=sC, ser=True)
                    v_expw = sC.n
                    em.emit("vector", lambda e, dc=dc: e.reduce_sum(
                        out=sstat[:, 8:9], in_=swp[:, 0:dc + 1], axis=AX.X),
                        waits=[(sC, v_expw)])
                    em.emit("vector", lambda e: e.tensor_scalar(
                        out=sstat[:, 9:10], in0=sstat[:, 8:9], scalar1=3.0,
                        scalar2=None, op0=ALU.mult), ser=True)
                    em.emit("vector", lambda e: e.reciprocal(
                        out=sstat[:, 10:11], in_=sstat[:, 9:10]), inc=sV, ser=True)
                    v_biasw = sV.n
                cb = Cb[idx % 2]
                eb = Ebuf[idx % 4]
                ms = mselc[idx % 2]
                em.emit("scalar", lambda e, c=c, eb=eb: e.activation(
                    out=eb[:], in_=rawb_p[:, c * 512:(c + 1) * 512], func=AF.Exp,
                    bias=sstat[:, 1:2]),
                    waits=([(sV, v_biasg), (sV, v_g)] if idx == 0 else [])
                    + ([(sV, v_stt2[idx - 4])] if idx >= 4 else []),
                    inc=sC)
                v_e2 = sC.n
                if c <= dc:
                    em.emit("scalar", lambda e, c=c, dc=dc, idx=idx: e.activation(
                        out=Ewb[idx % 2][:],
                        in_=(rawb_p[:, c * 512:(c + 1) * 512] if c < dc else wdiag[:]),
                        func=AF.Exp, bias=sstat[:, 7:8]),
                        waits=([(sV, v_biasw)] if c == 0 else [])
                        + ([(sV, v_stt2[idx - 1])] if idx >= 1 else []), inc=sC)
                    v_w2 = sC.n
                em.emit("vector", lambda e, c=c, ms=ms: e.tensor_scalar(
                    out=ms[:], in0=rawb_p[:, c * 512:(c + 1) * 512], scalar1=tauf[:, 0:1],
                    scalar2=sstat[:, 14:15], op0=ALU.is_ge, op1=ALU.mult),
                    waits=([(sV, v_biasg)] if idx == 0 else []))
                wst = [(sC, v_e2)]
                if idx >= 2:
                    wst.append((sT, v_tr_chunk[idx - 2]))
                em.emit("vector", lambda e, c=c, cb=cb, eb=eb, ms=ms:
                        e.scalar_tensor_tensor(
                            out=cb[:], in0=eb[:], scalar=sstat[:, 2:3],
                            in1=ms[:], op0=ALU.subtract, op1=ALU.mult),
                        waits=wst, inc=sV, ser=True)
                if c <= dc:
                    em.emit("vector", lambda e, idx=idx: e.tensor_scalar(
                        out=Ewb[idx % 2][:], in0=Ewb[idx % 2][:],
                        scalar1=sstat[:, 10:11], scalar2=None, op0=ALU.mult),
                        waits=[(sC, v_w2)], inc=sV, ser=True)
                    em.emit("vector", lambda e, cb=cb, idx=idx: e.tensor_tensor(
                        out=cb[:], in0=cb[:], in1=Ewb[idx % 2][:], op=ALU.add),
                        inc=sV, ser=True)
                v_cb = sV.n
                v_stt2[idx] = v_cb
                evw = []
                if idx >= 2:
                    evw.append((sT, v_mm_chunk[idx - 2]))
                if qb == 0 and idx == 0:
                    evw.append((sDP, v_cDP))
                v_tr_chunk[idx], ev_c = tr_batch(
                    4, lambda k, cb=cb: cb[:, k * 128:(k + 1) * 128],
                    lambda k, idx=idx: ctp[idx % 2][:, k, :],
                    tr_waits=[(sV, v_cb)], ev_waits=evw, ev_eng="scalar")
                for j in range(4):
                    for dch in range(2):
                        pp = pACC[dch]
                        w = []
                        if j == 0 and dch == 0:
                            w = [ev_c]
                            if idx == 0:
                                w.append((sC, sC.n))
                        em.emit("tensor", lambda e, c=c, j=j, dch=dch, pp=pp, idx=idx:
                                e.matmul(
                                    out=pp[:], lhsT=ctp[idx % 2][:, j, :],
                                    rhs=V_sb[:, c * 4 + j, dch * 512:(dch + 1) * 512],
                                    start=(idx == 0 and j == 0), stop=False),
                                waits=w,
                                inc=(sT if (j == 3 and dch == 1) else None))
                v_mm_chunk[idx] = sT.n

            em.emit("vector", lambda e: e.reduce_max(
                out=scp[:, 0:1], in_=c_sc[:], axis=AX.X), waits=[(sC, v_csc)])
            em.emit("vector", lambda e: e.tensor_scalar(
                out=scp[:, 1:2], in0=scp[:, 0:1], scalar1=-1.0, scalar2=None,
                op0=ALU.mult), inc=sV, ser=True)
            v_nmc = sV.n
            for cc_ in range(2):
                em.emit("scalar", lambda e, cc_=cc_: e.activation(
                    out=Pcb[:, cc_ * 512:(cc_ + 1) * 512],
                    in_=c_sc[:, cc_ * 512:(cc_ + 1) * 512], func=AF.Exp,
                    bias=scp[:, 1:2], accum_out=swp[:, 6 + cc_:7 + cc_]),
                    waits=[(sV, v_nmc)] if cc_ == 0 else [], inc=sC)
            em.emit("vector", lambda e: e.reduce_sum(
                out=sstat[:, 11:12], in_=swp[:, 6:8], axis=AX.X),
                waits=[(sC, sC.n)])
            em.emit("vector", lambda e: e.tensor_scalar(
                out=sstat[:, 12:13], in0=sstat[:, 11:12], scalar1=3.0, scalar2=None,
                op0=ALU.mult), ser=True)
            em.emit("vector", lambda e: e.reciprocal(
                out=sstat[:, 13:14], in_=sstat[:, 12:13]), inc=sV, ser=True)
            em.emit("vector", lambda e: e.tensor_scalar(
                out=Pcb[:], in0=Pcb[:], scalar1=sstat[:, 13:14], scalar2=None,
                op0=ALU.mult), waits=[(sV, sV.n)], inc=sV)
            v_pcb = sV.n
            for h in range(2):
                _, ev_pct = tr_batch(
                    4, lambda k, h=h: Pcb[:, (h * 4 + k) * 128:(h * 4 + k + 1) * 128],
                    lambda k, h=h: PcT[:, h * 4 + k, :],
                    tr_waits=([(sV, v_pcb)] if h == 0 else []), ev_eng="scalar")
            v_pref = [(sDC[0], sDC[0].n), (sDC2[0], sDC2[0].n)]
            for j in range(4):
                for dch in range(2):
                    pp = pACC[dch]
                    w = list(v_pref) if dch == 0 else []
                    if j == 0 and dch == 0:
                        w.append(ev_pct)
                    em.emit("tensor", lambda e, j=j, dch=dch, pp=pp: e.matmul(
                        out=pp[:], lhsT=PcT[:, j, :],
                        rhs=vcs8[j][:, dch * 512:(dch + 1) * 512],
                        start=False, stop=False),
                        waits=w, inc=(sT if dch == 1 else None))
                v_vcmm[j] = sT.n
            for j in range(4, 8):
                emit_vc_load(j, [(sT, v_vcmm[j - 4])])
            v_all = [(sDC[0], sDC[0].n), (sDC2[0], sDC2[0].n)]
            for j in range(4, 8):
                for dch in range(2):
                    pp = pACC[dch]
                    w = list(v_all) if (j == 4 and dch == 0) else []
                    em.emit("tensor", lambda e, j=j, dch=dch, pp=pp: e.matmul(
                        out=pp[:], lhsT=PcT[:, j, :],
                        rhs=vcs8[j % 4][:, dch * 512:(dch + 1) * 512],
                        start=False, stop=False),
                        waits=w, inc=(sT if dch == 1 else None))
                v_vcmm[j] = sT.n
            for idx in range(8):
                prev_vcmm[idx] = v_vcmm[idx]
            for dch in range(2):
                pp = pACC[dch]
                em.emit("tensor", lambda e, dch=dch, pp=pp: e.matmul(
                    out=pp[:], lhsT=coef_row, rhs=vsum_rowb[0:1, dch * 512:(dch + 1) * 512],
                    start=False, stop=True),
                    waits=[ev_coefrow, (sC, v_vsum)], inc=sT)
                em.emit("scalar", lambda e, dch=dch, pp=pp: e.copy(
                    out=abuf[:, dch * 512:(dch + 1) * 512], in_=pp[:]),
                    waits=[(sT, sT.n), (sV, v_g)], inc=sC)
            v_abuf = sC.n
            if debug_taps and qb == 1:
                em.emit("sync", lambda e: e.dma_start(out=taps["t_attn"], in_=abuf[:]),
                        waits=[(sC, v_abuf)], inc=sDS, inc_by=16)
                em.emit("sync", lambda e: e.dma_start(out=taps["t_sstat"], in_=sstat[:]),
                        waits=[(sC, v_abuf)], inc=sDS, inc_by=16)
                em.emit("sync", lambda e: e.dma_start(out=taps["t_pcb"], in_=Pcb[:]),
                        waits=[(sC, v_abuf)], inc=sDS, inc_by=16)
                em.emit("sync", lambda e: e.dma_start(out=taps["t_csc"], in_=c_sc[:]),
                        waits=[(sC, v_abuf)], inc=sDS, inc_by=16)
                em.emit("sync", lambda e: e.dma_start(out=taps["t_vsum"], in_=vsum_rowb),
                        waits=[(sC, v_abuf)], inc=sDS, inc_by=16)
            for h in range(2):
                _, ev_at = tr_batch(
                    4, lambda k, h=h: abuf[:, (h * 4 + k) * 128:(h * 4 + k + 1) * 128],
                    lambda k, h=h: atile[:, h * 4 + k, :],
                    tr_waits=([(sC, v_abuf)] if h == 0 else []),
                    ev_waits=([(sDA, sDA.n)] if h == 0 else []), ev_eng="scalar")
            em.emit("sync", lambda e, qb=qb: e.dma_start(
                out=at_s[:, qb, :, :].rearrange("t p n -> p t n"), in_=atile[:]),
                waits=[ev_at], inc=sDA, inc_by=16)
            v_tail_end[qb % 2] = sV.n
            v_prev_tail[0] = sV.n

        for qb in range(QB):
            emit_scores(qb)
            if qb >= 1:
                emit_tail(qb - 1)
        emit_tail(QB - 1)
        v_s_done_C = sC.n
        v_s_done_T = sT.n
        v_s_atdone = sDA.n
        for t in range(ND):
            em.emit("sync", lambda e, t=t: e.dma_start(
                out=wfcT[:, t, :].rearrange("p (i n) -> p i n", n=128),
                in_=wfct_s[t, :, :, :].rearrange("i p n -> p i n")),
                waits=[(sT, v_s_done_T), (sDWp[0], v_wst[0]),
                       (sDWp[1], v_wst[1])] if t == 0 else [],
                inc=sDW, inc_by=16)
        for hf in range(2):
            em.emit("sync", lambda e, hf=hf: e.dma_start(
                out=brow_st[:, 0:2048], in_=bfc_d[0:1, hf * 2048:(hf + 1) * 2048]),
                waits=[(sV, sV.n)], inc=sDS, inc_by=16)
            em.emit("vector", lambda e, hf=hf: e.tensor_copy(
                out=bfc_rowb[0:1, hf * 2048:(hf + 1) * 2048], in_=brow_st[0:1, 0:2048]),
                waits=[(sDS, sDS.n)] + ([(sT, v_s_done_T)] if hf == 0 else []), inc=sV)
        v_bfcrow = sV.n
        v_wfc = sDW.n
        qsl_free_f = (sT, 0)
        h_free = {0: (sC, v_s_done_C), 1: (sC, v_s_done_C)}
        for i in range(NT):
            em.emit("sync", lambda e, i=i: e.dma_start(
                out=qsl[:], in_=xqt_s[:, i, :, :].rearrange("t p n -> p t n")),
                waits=[(sDW, v_wfc), qsl_free_f]
                + ([(sDP, v_cDP), st_qsl_free[0]] if i == 0 else []),
                inc=sDQ, inc_by=16)
            v_xq = sDQ.n
            for c in range(8):
                bk = c % 2
                pp = pA if bk == 0 else pB
                for k in range(ND):
                    w = [(sDQ, v_xq)] if (c == 0 and k == 0) else []
                    if k == 0:
                        w = w + [h_free[bk]]
                    em.emit("tensor", lambda e, c=c, k=k, pp=pp: e.matmul(
                        out=pp[:], lhsT=qsl[:, k, :], rhs=wfcT[:, k, c * 512:(c + 1) * 512],
                        start=(k == 0), stop=False), waits=w)
                em.emit("tensor", lambda e, c=c, pp=pp: e.matmul(
                    out=pp[:], lhsT=onr[:], rhs=bfc_rowb[0:1, c * 512:(c + 1) * 512],
                    start=False, stop=True),
                    waits=[(sV, v_bfcrow)] if (i == 0 and c == 0) else [], inc=sT)
                em.emit("scalar", lambda e, c=c, pp=pp: e.activation(
                    out=hgb[:, c * 512:(c + 1) * 512], in_=pp[:], func=AF.Gelu),
                    waits=[(sT, sT.n)], inc=sC)
                h_free[bk] = (sC, sC.n)
                em.emit("vector", lambda e, i=i, c=c: e.reduce_max(
                    out=amax_v[:, i, c * 16:(c + 1) * 16],
                    in_=hgb[:, c * 512:(c + 1) * 512].rearrange("p (g k) -> p g k", k=32),
                    axis=AX.X, apply_absolute_value=True),
                    waits=[(sC, sC.n)], inc=sV)
            qsl_free_f = (sT, sT.n)
            em.emit("gpsimd", lambda e, i=i: e.dma_start(
                out=hg_s[i, :, :], in_=hgb[:]), waits=[(sC, sC.n)], inc=sDP, inc_by=16)
            if debug_taps and i == 0:
                em.emit("sync", lambda e: e.dma_start(out=taps["t_hg"], in_=hgb[:]),
                        waits=[(sC, sC.n)], inc=sDS, inc_by=16)
            em.emit("scalar", lambda e: e.nop(), waits=[(sDP, sDP.n), (sV, sV.n)])

        v_f15 = sV.n
        for ch in range(4):
            v_f15 = chainT(
                1024, amax_all[:, ch * 1024:(ch + 1) * 1024],
                T_all[:].rearrange("p t n -> p (t n)")[:, ch * 1024:(ch + 1) * 1024],
                f15[0], f15[1], f15[2], rnd15[:, 0:1024], v_f15)
        v_sch = v_f15
        if debug_taps:
            em.emit("sync", lambda e: e.dma_start(
                out=taps["t_sc"], in_=R3[:, 21504:25600]),
                waits=[(sV, v_sch)], inc=sDS, inc_by=16)

        for t in range(NH):
            em.emit("sync", lambda e, t=t: e.dma_start(
                out=wprT[:, t, :].rearrange("p (i n) -> p i n", n=128),
                in_=wprt_s[t, :, :, :].rearrange("i p n -> p i n")),
                waits=[(sT, sT.n), (sDWp[0], v_wst[0]),
                       (sDWp[1], v_wst[1])] if t == 0 else [],
                inc=sDW, inc_by=16)
        for t in range(ND):
            em.emit("sync", lambda e, t=t: e.dma_start(
                out=wstage[:, 0:1024], in_=Wo_d[t * 128:(t + 1) * 128, :]),
                waits=[(sV, sV.n), (sT, sT.n)], inc=sDS, inc_by=16)
            em.emit("vector", lambda e, t=t: e.tensor_copy(
                out=Wo_sb[:, t, :], in_=wstage[:, 0:1024]),
                waits=[(sDS, sDS.n)], inc=sV)
        em.emit("sync", lambda e: e.dma_start(out=brow_st[:, 0:D], in_=bo_d),
                waits=[(sV, sV.n)], inc=sDS, inc_by=16)
        em.emit("vector", lambda e: e.tensor_copy(out=bo_rowb, in_=brow_st[0:1, 0:D]),
                waits=[(sDS, sDS.n), (sT, sT.n)], inc=sV)
        vv = sV.n
        em.emit("sync", lambda e: e.dma_start(out=brow_st[:, 0:D], in_=bpr_d),
                waits=[(sV, vv)], inc=sDS, inc_by=16)
        em.emit("vector", lambda e: e.tensor_copy(out=bpr_rowb, in_=brow_st[0:1, 0:D]),
                waits=[(sDS, sDS.n)], inc=sV)
        v_f2rows = sV.n
        v_f2w = sDW.n

        x_free = (sV, 0)
        y_free = (sDS, 0)
        asl_free = (sT, 0)
        hgbs = [hgb, r3(25600, 4096)]
        sDHp = [sDH, sDH2]
        v_hq_p = [0, 0]
        v_trend_p = [0, 0]
        st_hg = [0, 0]

        def emit_f2_load(i):
            hgb_p = hgbs[i % 2]
            em.emit("sync", lambda e, i=i, hgb_p=hgb_p: e.dma_start(
                out=hgb_p[:], in_=hg_s[i, :, :]),
                waits=[(sV, v_hq_p[i % 2])]
                + ([(sDW, v_f2w), (sDP, sDP.n)] if i == 0 else []),
                inc=sDHp[i % 2], inc_by=16)
            st_hg[i % 2] = sDHp[i % 2].n

        emit_f2_load(0)
        for i in range(NT):
            if i + 1 < NT:
                emit_f2_load(i + 1)
            hgb_p = hgbs[i % 2]
            hq_p = hq_b
            v_hg = st_hg[i % 2]
            em.emit("sync", lambda e, i=i: e.dma_start(
                out=asl[:], in_=at_s[:, i, :, :].rearrange("t p n -> p t n")),
                waits=[asl_free, (sDA, v_s_atdone)], inc=sDAL, inc_by=16)
            v_asl = sDAL.n
            em.emit("sync", lambda e, i=i: e.dma_start(
                out=xtile2[:], in_=x_d[i * 128:(i + 1) * 128, :]),
                waits=[x_free], inc=sDX, inc_by=16)
            v_x = sDX.n
            for ch in range(4):
                sl = slice(ch * 1024, (ch + 1) * 1024)
                em.emit("vector", lambda e, i=i, ch=ch, sl=sl, hgb_p=hgb_p,
                        hq_p=hq_p: e.tensor_tensor(
                    out=hq_p[:, sl].rearrange("p (g k) -> p g k", k=32),
                    in0=hgb_p[:, sl].rearrange("p (g k) -> p g k", k=32),
                    in1=T_all[:, i, ch * 32:(ch + 1) * 32].unsqueeze(-1)
                        .to_broadcast([128, 32, 32]),
                    op=ALU.add),
                    waits=[(sDHp[i % 2], v_hg)]
                    + ([(sV, v_sch), (sV, v_f2rows), (sDA, v_s_atdone)]
                       if (i == 0 and ch == 0) else [])
                    + ([(sT, v_trend_p[0])] if ch == 0 else []), ser=True)
                em.emit("vector", lambda e, i=i, ch=ch, sl=sl, hq_p=hq_p:
                        e.tensor_tensor(
                    out=hq_p[:, sl].rearrange("p (g k) -> p g k", k=32),
                    in0=hq_p[:, sl].rearrange("p (g k) -> p g k", k=32),
                    in1=T_all[:, i, ch * 32:(ch + 1) * 32].unsqueeze(-1)
                        .to_broadcast([128, 32, 32]),
                    op=ALU.subtract), inc=sV, ser=True)
            v_hq = sV.n
            v_hq_p[i % 2] = v_hq
            for h in range(8):
                _, ev_hqT = tr_batch(
                    4, lambda k, h=h, hq_p=hq_p:
                    hq_p[:, (h * 4 + k) * 128:(h * 4 + k + 1) * 128],
                    lambda k, h=h: hqT[:, h * 4 + k, :],
                    tr_waits=([(sV, v_hq)] if h == 0 else []))
            v_trend_p[0] = sT.n
            for dch in range(2):
                pp = pACC[dch]
                for kt in range(NH):
                    w = [ev_hqT] if kt == 0 else []
                    em.emit("tensor", lambda e, dch=dch, kt=kt, pp=pp: e.matmul(
                        out=pp[:], lhsT=hqT[:, kt, :], rhs=wprT[:, kt, dch * 512:(dch + 1) * 512],
                        start=(kt == 0), stop=False), waits=w)
                for t in range(ND):
                    w = [(sDAL, v_asl)] if t == 0 else []
                    em.emit("tensor", lambda e, dch=dch, t=t, pp=pp: e.matmul(
                        out=pp[:], lhsT=asl[:, t, :], rhs=Wo_sb[:, t, dch * 512:(dch + 1) * 512],
                        start=False, stop=False), waits=w)
                em.emit("tensor", lambda e, dch=dch, pp=pp: e.matmul(
                    out=pp[:], lhsT=onr[:], rhs=bpr_rowb[0:1, dch * 512:(dch + 1) * 512],
                    start=False, stop=False))
                em.emit("tensor", lambda e, dch=dch, pp=pp: e.matmul(
                    out=pp[:], lhsT=onr[:], rhs=bo_rowb[0:1, dch * 512:(dch + 1) * 512],
                    start=False, stop=True), inc=sT)
                em.emit("vector", lambda e, dch=dch, pp=pp: e.tensor_tensor(
                    out=ybuf[:, 0:512] if dch == 0 else ybuf[:, 512:1024],
                    in0=xtile2[:, dch * 512:(dch + 1) * 512],
                    in1=pp[:], op=ALU.add),
                    waits=[(sT, sT.n), (sDX, v_x), y_free], inc=sV)
            asl_free = (sT, sT.n)
            x_free = (sV, sV.n)
            v_y = sV.n
            em.emit("sync", lambda e, i=i: e.dma_start(
                out=y_d[i * 128:(i + 1) * 128, :], in_=ybuf[:]),
                waits=[(sV, v_y)], inc=sDY, inc_by=16)
            y_free = (sDY, sDY.n)

        em.emit("sync", lambda e: e.nop(),
                waits=[(sDY, sDY.n), (sDS, sDS.n), (sDP, sDP.n), (sDA, sDA.n),
                       (sDX, sDX.n), (sDQ, sDQ.n), (sDV, sDV.n), (sDH, sDH.n),
                       (sDAL, sDAL.n), (sDW, sDW.n),
                       (sDWp[0], sDWp[0].n), (sDWp[1], sDWp[1].n),
                       (sVS, sVS.n), (sVC, sVC.n),
                       (sDC[0], sDC[0].n), (sDC[1], sDC[1].n),
                       (sDC2[0], sDC2[0].n), (sDC2[1], sDC2[1].n),
                       (sDQ2, sDQ2.n), (sDH2, sDH2.n)])

        with nc.Block() as block:
            @block.sync
            def _(eng):
                for f in em.ops["sync"]:
                    f(eng)

            @block.vector
            def _(eng):
                for f in em.ops["vector"]:
                    f(eng)

            @block.scalar
            def _(eng):
                for f in em.ops["scalar"]:
                    f(eng)

            @block.tensor
            def _(eng):
                for f in em.ops["tensor"]:
                    f(eng)

            @block.gpsimd
            def _(eng):
                for f in em.ops["gpsimd"]:
                    f(eng)

    return nc


def host_inputs(inp):
    x = np.ascontiguousarray(np.asarray(inp["L0_attn_x"], dtype=np.float32))

    def f32(k):
        return np.ascontiguousarray(np.asarray(inp[k], dtype=np.float32))

    Wq, Wk, Wv, Wo = f32("L0_attn_Wq"), f32("L0_attn_Wk"), f32("L0_attn_Wv"), f32("L0_attn_Wo")
    bq, bk, bv, bo = f32("L0_attn_bq"), f32("L0_attn_bk"), f32("L0_attn_bv"), f32("L0_attn_bo")
    wfc, bfc = f32("L0_ffn_w_fc"), f32("L0_ffn_b_fc")
    wpr, bpr = f32("L0_ffn_w_proj"), f32("L0_ffn_b_proj")

    ident = np.eye(128, dtype=np.float32).astype(BF)
    r = np.arange(128)[:, None]
    cc = np.arange(512)[None, :]
    cmask = np.stack([np.where(cc > (o * 128 + r), np.float32(NEG), np.float32(0.0))
                      for o in range(4)])
    cmask = np.ascontiguousarray(cmask.transpose(1, 0, 2)).astype(BF)
    onc = np.ones((128, 1), dtype=np.float32).astype(BF)
    onr = np.ones((1, 128), dtype=np.float32).astype(BF)

    shared = {
        "Wq": Wq, "Wk": Wk, "Wv": Wv, "Wo": Wo,
        "bqt": np.ascontiguousarray(bq.reshape(ND, 128).T),
        "bkt": np.ascontiguousarray(bk.reshape(ND, 128).T),
        "bvt": np.ascontiguousarray(bv.reshape(ND, 128).T),
        "bo_r": np.ascontiguousarray(bo.reshape(1, D)),
        "w_fc": wfc, "b_fc_r": np.ascontiguousarray(bfc.reshape(1, H)),
        "w_proj": wpr, "b_proj_r": np.ascontiguousarray(bpr.reshape(1, D)),
        "ident": ident, "cmask": cmask, "ones_col": onc, "ones_row": onr,
    }
    return [dict(shared, x=np.ascontiguousarray(x[b])) for b in range(8)]


def kernel(**inputs):
    in_maps = host_inputs(inputs)
    nc = build_nc(debug_taps=False)
    res = run_bass_kernel_spmd(nc, in_maps, core_ids=list(range(8)))
    return np.stack([np.asarray(r["y"], dtype=np.float32) for r in res.results], axis=0)


# version 14
# speedup vs baseline: 1.0543x; 1.0032x over previous
import numpy as np
import ml_dtypes
import concourse.bass as bass
import concourse.mybir as mybir
from concourse.bass_utils import run_bass_kernel_spmd
from contextlib import ExitStack

F32 = mybir.dt.float32
BF16 = mybir.dt.bfloat16
AF = mybir.ActivationFunctionType
ALU = mybir.AluOpType
AX = mybir.AxisListType
BF = ml_dtypes.bfloat16

S, D, H = 4096, 1024, 4096
NT, ND, NH, QB = 32, 8, 32, 32
SCALE = 1.0 / 32.0
LN2 = float(np.log(2.0))
INVLN2 = float(1.0 / np.log(2.0))
NEG = -1e30


class Sem:
    def __init__(self, h):
        self.h = h
        self.n = 0

    def bump(self, k=1):
        self.n += k
        return self.n


class Em:

    def __init__(self, selfsems=None):
        self.ops = {k: [] for k in ("sync", "vector", "scalar", "tensor", "gpsimd")}
        self.selfsems = selfsems or {}

    SELF_SERIALIZE = False

    def emit(self, eng, fn, waits=(), inc=None, inc_by=1, ser=False):
        selfsem = self.selfsems.get(eng)
        if not (ser or self.SELF_SERIALIZE):
            selfsem = None
        pre = []
        if selfsem is not None:
            is_dma = inc is not None and inc is not selfsem
            pre = [(selfsem.h, selfsem.n)]
            if not is_dma:
                assert inc is None or inc is selfsem
                inc = selfsem
                inc_by = 1
        if inc is None:
            ms = self.selfsems.get(eng)
            if ms is not None:
                inc = ms
                inc_by = 1
        waits = pre + [(s.h, v) for (s, v) in waits if v > 0]
        best = {}
        for h, v in waits:
            if v > best.get(h, 0):
                best[h] = v
        waits = tuple(best.items())
        ih = inc.h if inc is not None else None
        if inc is not None:
            inc.bump(inc_by)

        def run(e, fn=fn, waits=waits, ih=ih, inc_by=inc_by):
            for h, v in waits:
                e.wait_ge(h, v)
            ins = fn(e)
            if ih is not None:
                ins.then_inc(ih, inc_by)
        self.ops[eng].append(run)


def build_nc(debug_taps=False):
    nc = bass.Bass("TRN2", target_bir_lowering=False)

    def din(name, shape, dtype=F32):
        return nc.dram_tensor(name, shape, dtype, kind="ExternalInput").ap()

    x_d = din("x", [S, D])
    Wq_d = din("Wq", [D, D]); Wk_d = din("Wk", [D, D]); Wv_d = din("Wv", [D, D])
    Wo_d = din("Wo", [D, D])
    bqt_d = din("bqt", [128, ND]); bkt_d = din("bkt", [128, ND]); bvt_d = din("bvt", [128, ND])
    bo_d = din("bo_r", [1, D])
    wfc_d = din("w_fc", [H, D]); bfc_d = din("b_fc_r", [1, H])
    wpr_d = din("w_proj", [D, H]); bpr_d = din("b_proj_r", [1, D])
    ident_d = din("ident", [128, 128], BF16)
    cmask_d = din("cmask", [128, 4, 512], BF16)
    onc_d = din("ones_col", [128, 1], BF16)
    onr_d = din("ones_row", [1, 128], BF16)

    y_d = nc.dram_tensor("y", [S, D], F32, kind="ExternalOutput").ap()

    def dsc(name, shape, dtype=BF16):
        return nc.dram_tensor(name, shape, dtype, kind="Internal").ap()

    qt_s = dsc("qt_s", [ND, QB, 128, 128])
    xqt_s = dsc("xqt_s", [ND, NT, 128, 128])
    at_s = dsc("at_s", [ND, QB, 128, 128])
    v_s = dsc("v_s", [NT, ND, 128, 128])
    vc_s = dsc("vc_s", [8, ND, 128, 128])
    wfct_s = dsc("wfct_s", [ND, NT, 128, 128])
    wprt_s = dsc("wprt_s", [NH, ND, 128, 128])
    hg_s = dsc("hg_s", [NT, 128, H])

    taps = {}
    if debug_taps:
        def mtap(name, shape, dtype=BF16):
            taps[name] = nc.dram_tensor(name, shape, dtype, kind="ExternalOutput").ap()
        mtap("t_xnT", [128, ND * S]); mtap("t_KT", [128, ND * S])
        mtap("t_KcT", [128, ND * 1024]); mtap("t_rawb", [128, S])
        mtap("t_vals", [128, 64]); mtap("t_attn", [128, D])
        mtap("t_wfcq", [128, D]); mtap("t_hg", [128, H])
        mtap("t_sc", [128, NT * 128]); mtap("t_xq", [128, D])
        mtap("t_vtb", [128, 512])
        mtap("t_vps", [128, 512], F32)
        mtap("t_tr0", [128, 128])
        mtap("t_wv0", [128, 1024]); mtap("t_wq0", [128, 1024])
        mtap("t_wv1", [128, 1024]); mtap("t_wv2", [128, 1024])
        mtap("t_sstat", [128, 16], F32); mtap("t_pcb", [128, 1024])
        mtap("t_csc", [128, 1024]); mtap("t_vsum", [1, D])


    em = Em()
    with ExitStack() as ctx:
        def sbt(name, shape, dtype=F32):
            return ctx.enter_context(nc.sbuf_tensor(name, shape, dtype))

        def pst(name, shape, dtype=F32):
            return ctx.enter_context(nc.psum_tensor(name, shape, dtype))

        def mksem(name):
            return Sem(ctx.enter_context(nc.semaphore(name)))

        sDS = mksem("sDS"); sDP = mksem("sDP")
        sT = mksem("sT"); sV = mksem("sV"); sC = mksem("sC"); sG = mksem("sG")
        sDX = mksem("sDX"); sDQ = mksem("sDQ"); sDV = mksem("sDV")
        sDC = [mksem("sDC0"), mksem("sDC1")]
        sDH = mksem("sDH"); sDAL = mksem("sDAL"); sDA = mksem("sDA")
        sDW = mksem("sDW"); sDY = mksem("sDY")
        sDWp = [mksem("sDWa"), mksem("sDWb")]
        sVS = mksem("sVS"); sVC = mksem("sVC")
        sDC2 = [mksem("sDC2a"), mksem("sDC2b")]
        sDQ2 = mksem("sDQ2"); sDH2 = mksem("sDH2")
        em.__init__(selfsems={"vector": sV, "scalar": sC, "gpsimd": sG})

        R1 = sbt("R1", [128, 32768], BF16)
        R2 = sbt("R2", [128, 32768], BF16)
        R3 = sbt("R3", [128, 32768], BF16)
        FA = sbt("FA", [128, 3072], F32)

        def r3(lo, n):
            return R3[:, lo:lo + n]

        def r3v(lo, nt, n):
            return R3[:, lo:lo + nt * n].rearrange("p (t n) -> p t n", n=n)

        xnT = R1[:].rearrange("p (t n) -> p t n", n=S)
        V_sb = R1[:].rearrange("p (t n) -> p t n", n=D)
        wfcT = xnT
        bo_rowb = R1[0:1, 0:D]
        bpr_rowb = R1[0:1, D:2 * D]
        KT = R2[:].rearrange("p (t n) -> p t n", n=S)
        wprT = R2[:].rearrange("p (t n) -> p t n", n=D)
        bfc_rowb = R2[0:1, 0:H]

        wq_bp = [r3(0, 1024), r3(1024, 1024)]
        wq_stp = [r3v(2048, 8, 128), r3v(3072, 8, 128)]
        rndW = r3(4096, 1024)
        xnb = r3(0, 1024); xqb = r3(1024, 1024); rx_b = r3(2048, 1024)
        ttile = r3v(3072, ND, 128)
        KcT = r3v(0, ND, 1024)
        rawb = r3(8192, 4096)
        t1a = r3(12288, 1024); t1b = r3(13312, 1024)
        g8 = [r3(14336, 512), r3(14848, 512)]
        wdiag = r3(12288, 512)
        Cb = [r3(12800, 512), r3(13312, 512)]
        vcs = [r3(13824, 512), r3(14336, 512)]
        coef_row = R3[0:1, 14848:14976]
        vals_b = r3(12288, 64)
        vals_dbg = r3(12352, 64)
        qsl = r3v(15360, ND, 128)
        vcs8 = [r3(16384 + i * 1024, 1024) for i in range(4)]
        ctp = [r3v(13824, 4, 128), r3v(14336, 4, 128)]
        qsls = [r3v(15360, 8, 128), r3v(24576, 8, 128)]
        vsum_rowb = R3[0:1, 25600:26624]
        cmask = R3[:, 26624:28672].rearrange("p (t n) -> p t n", n=512)
        c_sc = r3(28672, 1024)
        Pcb = r3(29696, 1024)
        PcT = r3v(30720, 8, 128)
        mselc = [r3(31744, 512), r3(32256, 512)]
        abuf = r3(31744, 1024)
        atile = r3v(12288, ND, 128)
        hgb = r3(8192, 4096)
        amax_all = r3(16384, 4096)
        amax_v = r3v(16384, NT, 128)
        rnd15 = r3(20480, 1024)
        T_all = r3v(21504, NT, 128)
        Wo_sb = r3v(0, ND, 1024)
        hq_b = r3(12288, 4096)
        hqT = r3v(16384, NH, 128)
        asl = r3v(20480, ND, 128)
        Wq_sb = r3v(8192, ND, 1024)
        Wk_sb = r3v(16384, ND, 1024)
        Wv_sb = r3v(24576, ND, 1024)

        FA0 = FA[:, 0:1024]; FA1 = FA[:, 1024:2048]; FA2 = FA[:, 2048:3072]
        wstage = FA[:, 0:1536]
        xtile = FA[:, 0:1024]
        xn_f = FA[:, 1024:2048]
        tx_f = FA[:, 2048:3072]
        Ebuf = [FA[:, 0:512], FA[:, 512:1024],
                FA[:, 1024:1536], FA[:, 1536:2048]]
        Ewb = [FA[:, 2048:2560], FA[:, 2560:3072]]
        th_c = FA[:, 0:1024]
        xtile2 = FA[:, 1024:2048]
        ybuf = FA[:, 2048:3072]
        brow_st = FA[0:1, 0:2048]
        f15 = (FA[:, 0:1024], FA[:, 1024:2048], FA[:, 2048:3072])

        ident = sbt("identsb", [128, 128], BF16)
        onc = sbt("onc", [128, 1], BF16)
        onr = sbt("onr", [1, 128], BF16)
        bq_sb = sbt("bq_sb", [128, ND]); bk_sb = sbt("bk_sb", [128, ND])
        bv_sb = sbt("bv_sb", [128, ND]); bqs_sb = sbt("bqs_sb", [128, ND])
        amaxw = sbt("amaxw", [128, 32]); scw = sbt("scw", [128, 32], BF16)
        rscw = sbt("rscw", [128, 32])
        tmpw = [sbt(f"tmpw{i}", [128, 32]) for i in range(3)]
        rndw = sbt("rndw", [128, 32], BF16)
        trev = [sbt(f"trev{i}", [128, 128], BF16) for i in range(2)]
        bnst = sbt("bnst", [128, 12]); bnag = sbt("bnag", [128, 2])
        vc_bf = sbt("vc_bf", [128, 128], BF16)
        vt_buf = sbt("vt_buf", [128, 512], BF16)
        sstat = sbt("sstat", [128, 16])
        tauf = sbt("tauf", [128, 2])
        swp = sbt("swp", [128, 8]); gsum = sbt("gsum", [128, 8])
        scp = sbt("scp", [128, 2])
        coefb = sbt("coefb", [128, 1], BF16)
        negc = sbt("negc", [128, 1])
        stat = sstat
        amaxx, scx, rscx, tmpx, rndx = amaxw, scw, rscw, tmpw, rndw
        vc_red = FA[:, 0:128]
        vpstap = FA[:, 1024:1536]

        pA = pst("pA", [128, 512]); pB = pst("pB", [128, 512])
        pC_ = pst("pC_", [128, 512]); pD = pst("pD", [128, 512])
        pACC = [pst("pACC0", [128, 512]), pst("pACC1", [128, 512])]
        pT1 = pst("pT1", [128, 512], BF16); pT2 = pst("pT2", [128, 512], BF16)

        def pts(s):
            bank = pT1 if (s % 8) < 4 else pT2
            q = s % 4
            return bank[:, q * 128:(q + 1) * 128]

        for dst, src in [(ident, ident_d), (onc, onc_d), (onr, onr_d),
                         (bq_sb, bqt_d), (bk_sb, bkt_d), (bv_sb, bvt_d)]:
            em.emit("sync", lambda e, d=dst, s=src: e.dma_start(out=d[:], in_=s),
                    inc=sDS, inc_by=16)
        v_consts = sDS.n
        em.emit("vector", lambda e: e.tensor_scalar(
            out=bqs_sb[:], in0=bq_sb[:], scalar1=SCALE, scalar2=None, op0=ALU.mult),
            waits=[(sDS, v_consts)], inc=sV)

        def scale_chain(G, amax_ap, sc_ap, rsc_ap, tmp, rnd_ap, wait_v, waits0=()):
            t0, t1, t2 = tmp
            em.emit("vector", lambda e: e.tensor_scalar(
                out=t0[:, :G], in0=amax_ap, scalar1=2.0 ** -40, scalar2=None,
                op0=ALU.max), waits=list(waits0) + [(sV, wait_v)], inc=sV, ser=True)
            em.emit("scalar", lambda e: e.activation(
                out=t1[:, :G], in_=t0[:, :G], func=AF.Ln), waits=[(sV, sV.n)], inc=sC)
            cc = sC.n
            em.emit("vector", lambda e: e.tensor_scalar(
                out=rnd_ap, in0=t1[:, :G], scalar1=INVLN2, scalar2=191.5,
                op0=ALU.mult, op1=ALU.add), waits=[(sC, cc)], ser=True)
            em.emit("vector", lambda e: e.tensor_scalar(
                out=t1[:, :G], in0=rnd_ap, scalar1=192.0, scalar2=None,
                op0=ALU.subtract), inc=sV, ser=True)
            em.emit("scalar", lambda e: e.activation(
                out=t2[:, :G], in_=t1[:, :G], func=AF.Exp, scale=LN2),
                waits=[(sV, sV.n)], inc=sC)
            cc = sC.n
            em.emit("vector", lambda e: e.tensor_scalar(
                out=t1[:, :G], in0=t2[:, :G], scalar1=2.0, scalar2=None, op0=ALU.mult),
                waits=[(sC, cc)], ser=True)
            em.emit("vector", lambda e: e.tensor_tensor(
                out=t1[:, :G], in0=t0[:, :G], in1=t1[:, :G], op=ALU.is_ge), ser=True)
            em.emit("vector", lambda e: e.tensor_scalar(
                out=t1[:, :G], in0=t1[:, :G], scalar1=1.0, scalar2=None, op0=ALU.add), ser=True)
            em.emit("vector", lambda e: e.tensor_tensor(
                out=t2[:, :G], in0=t2[:, :G], in1=t1[:, :G], op=ALU.mult), ser=True)
            em.emit("vector", lambda e: e.tensor_tensor(
                out=t1[:, :G], in0=t0[:, :G], in1=t2[:, :G], op=ALU.is_lt), ser=True)
            em.emit("vector", lambda e: e.tensor_scalar(
                out=t1[:, :G], in0=t1[:, :G], scalar1=-0.5, scalar2=1.0,
                op0=ALU.mult, op1=ALU.add), ser=True)
            em.emit("vector", lambda e: e.tensor_tensor(
                out=t2[:, :G], in0=t2[:, :G], in1=t1[:, :G], op=ALU.mult), ser=True)
            em.emit("vector", lambda e: e.tensor_scalar(
                out=sc_ap, in0=t2[:, :G], scalar1=0.125, scalar2=None,
                op0=ALU.mult), inc=sV, ser=True)
            def _recip(e):
                with nc.allow_low_precision(reason="reciprocal of exact powers of two is exact in bf16"):
                    return e.reciprocal(out=rsc_ap, in_=sc_ap)
            em.emit("vector", _recip, waits=[(sV, sV.n)], inc=sV, ser=True)
            return sV.n

        em.emit("vector", lambda e: e.memset(negc[:], -192.0 * LN2), inc=sV)

        def chainT(G, amax_ap, T_ap, t0, t2, t3, rnd_ap, wait_v):
            em.emit("vector", lambda e: e.tensor_scalar(
                out=t0, in0=amax_ap, scalar1=2.0 ** -40, scalar2=None,
                op0=ALU.max), waits=[(sV, wait_v)], inc=sV, ser=True)
            em.emit("scalar", lambda e: e.activation(
                out=t3, in_=t0, func=AF.Ln), waits=[(sV, sV.n)], inc=sC)
            em.emit("vector", lambda e: e.tensor_scalar(
                out=rnd_ap, in0=t3, scalar1=INVLN2, scalar2=191.5,
                op0=ALU.mult, op1=ALU.add), waits=[(sC, sC.n)], ser=True)
            em.emit("scalar", lambda e: e.activation(
                out=t2, in_=rnd_ap, func=AF.Exp, scale=LN2, bias=negc[:, 0:1]),
                waits=[(sV, sV.n)], inc=sC)
            em.emit("vector", lambda e: e.tensor_scalar(
                out=t3, in0=t2, scalar1=2.0, scalar2=None, op0=ALU.mult),
                waits=[(sC, sC.n)], ser=True)
            em.emit("vector", lambda e: e.tensor_tensor(
                out=t3, in0=t0, in1=t3, op=ALU.is_ge), ser=True)
            em.emit("vector", lambda e: e.tensor_scalar(
                out=t3, in0=t3, scalar1=1.0, scalar2=None, op0=ALU.add), ser=True)
            em.emit("vector", lambda e: e.tensor_tensor(
                out=t2, in0=t2, in1=t3, op=ALU.mult), ser=True)
            em.emit("vector", lambda e: e.tensor_tensor(
                out=t3, in0=t0, in1=t2, op=ALU.is_lt), ser=True)
            em.emit("vector", lambda e: e.tensor_scalar(
                out=t3, in0=t3, scalar1=-0.5, scalar2=1.0,
                op0=ALU.mult, op1=ALU.add), ser=True)
            em.emit("vector", lambda e: e.tensor_tensor(
                out=t2, in0=t2, in1=t3, op=ALU.mult), ser=True)
            em.emit("vector", lambda e: e.tensor_copy(out=rnd_ap, in_=t2), ser=True)
            em.emit("vector", lambda e: e.tensor_scalar(
                out=T_ap, in0=rnd_ap, scalar1=24.0, scalar2=None,
                op0=ALU.mult), inc=sV, ser=True)
            return sV.n

        tr_bank = [0]
        v_evac_bank = [None, None]
        v_store_last = [0, 0]

        def tr_batch(n, in_fn, out_fn, tr_waits=(), ev_waits=(), rows=128,
                     ev_eng="vector"):
            b = tr_bank[0]
            tr_bank[0] ^= 1
            bank = pT1 if b == 0 else pT2
            for k in range(n):
                w = list(tr_waits) if k == 0 else []
                if k == 0 and v_evac_bank[b] is not None:
                    w.append(v_evac_bank[b])
                em.emit("tensor", lambda e, k=k, bank=bank: e.transpose(
                    out=bank[0:rows, k * 128:(k + 1) * 128], in_=in_fn(k),
                    identity=ident[:]), waits=w, inc=(sT if k == n - 1 else None))
            v_end = sT.n
            esem = sV if ev_eng == "vector" else sC
            for k in range(n):
                w = [(sT, v_end)] + (list(ev_waits) if k == 0 else [])
                if ev_eng == "vector":
                    em.emit("vector", lambda e, k=k, bank=bank: e.tensor_copy(
                        out=out_fn(k), in_=bank[0:rows, k * 128:(k + 1) * 128]),
                        waits=w, inc=sV, ser=True)
                else:
                    em.emit("scalar", lambda e, k=k, bank=bank: e.copy(
                        out=out_fn(k), in_=bank[0:rows, k * 128:(k + 1) * 128]),
                        waits=w, inc=sC)
            v_evac_bank[b] = (esem, esem.n)
            return v_end, (esem, esem.n)

        def w_round(srcs, store_fns, tap0=None):
            n = len(srcs)
            wsA = [FA0, FA1]
            v_amax = [0] * n
            v_dma = [0] * n
            for i in range(n):
                p = i % 2
                w = [(sV, v_amax[i - 2])] if i >= 2 else [(sV, sV.n)]
                em.emit("sync", lambda e, i=i, p=p: e.dma_start(
                    out=wsA[p], in_=srcs[i]), waits=w, inc=sDC[p], inc_by=16)
                v_dma[i] = sDC[p].n
                em.emit("vector", lambda e, i=i, p=p: e.reduce_max(
                    out=FA2[:, i * 32:(i + 1) * 32],
                    in_=wsA[p].rearrange("p (g k) -> p g k", k=32),
                    axis=AX.X, apply_absolute_value=True),
                    waits=[(sDC[p], v_dma[i])], inc=sV)
                v_amax[i] = sV.n
            v_chain = chainT(n * 32, FA2[:, 0:n * 32], FA0[:, 0:n * 32],
                             FA0[:, 0:n * 32] if False else FA0[:, 0:n * 32],
                             FA1[:, 0:n * 32], FA2[:, 0:n * 32],
                             rndW[:, 0:n * 32], sV.n)
            wsB = [FA1, FA2]
            v_add = [0] * n
            v_sub = [0] * n
            v_tr_last = [0] * n
            for i in range(n):
                p = i % 2
                w = [(sV, v_add[i - 2])] if i >= 2 else [(sV, v_chain)]
                em.emit("sync", lambda e, i=i, p=p: e.dma_start(
                    out=wsB[p], in_=srcs[i]), waits=w, inc=sDC[p], inc_by=16)
                v_dmab = sDC[p].n
                wadd = [(sDC[p], v_dmab)]
                if i >= 2:
                    wadd.append((sT, v_tr_last[i - 2]))
                em.emit("vector", lambda e, i=i, p=p: e.tensor_tensor(
                    out=wq_bp[p].rearrange("p (g k) -> p g k", k=32),
                    in0=wsB[p].rearrange("p (g k) -> p g k", k=32),
                    in1=FA0[:, i * 32:(i + 1) * 32].unsqueeze(-1)
                        .to_broadcast([128, 32, 32]),
                    op=ALU.add), waits=wadd, inc=sV, ser=True)
                v_add[i] = sV.n
                em.emit("vector", lambda e, i=i, p=p: e.tensor_tensor(
                    out=wq_bp[p].rearrange("p (g k) -> p g k", k=32),
                    in0=wq_bp[p].rearrange("p (g k) -> p g k", k=32),
                    in1=FA0[:, i * 32:(i + 1) * 32].unsqueeze(-1)
                        .to_broadcast([128, 32, 32]),
                    op=ALU.subtract), inc=sV, ser=True)
                v_sub[i] = sV.n
                if tap0 is not None and i == 0:
                    em.emit("sync", lambda e: e.dma_start(
                        out=tap0, in_=wq_bp[0]),
                        waits=[(sV, v_sub[0])], inc=sDS, inc_by=16)
                for h in range(2):
                    tr_batch(
                        4,
                        lambda k, p=p, h=h: wq_bp[p][:, (h * 4 + k) * 128:
                                                     (h * 4 + k + 1) * 128],
                        lambda k, p=p, h=h: wq_stp[p][:, h * 4 + k, :],
                        tr_waits=([(sV, v_sub[i])] if h == 0 else []),
                        ev_waits=([(sDWp[p], v_store_last[p])] if h == 0 else []))
                v_tr_last[i] = sT.n
                em.emit("gpsimd", lambda e, i=i, p=p: e.dma_start(
                    out=store_fns[i], in_=wq_stp[p][:]),
                    waits=[(sV, sV.n)], inc=sDWp[p], inc_by=16)
                v_store_last[p] = sDWp[p].n

        fc_srcs = [wfc_d[i * 128:(i + 1) * 128, :] for i in range(NT)]
        fc_stores = [wfct_s[:, i, :, :].rearrange("t p n -> p t n")
                     for i in range(NT)]
        w_round(fc_srcs, fc_stores,
                tap0=(taps["t_wfcq"] if debug_taps else None))
        pr_srcs = []
        pr_stores = []
        for i in range(ND):
            for hf in range(4):
                pr_srcs.append(wpr_d[i * 128:(i + 1) * 128,
                                     hf * 1024:(hf + 1) * 1024])
                pr_stores.append(
                    wprt_s[hf * 8:(hf + 1) * 8, i, :, :]
                    .rearrange("t p n -> p t n"))
        w_round(pr_srcs, pr_stores)
        v_wst = [sDWp[0].n, sDWp[1].n]

        wqkv_list = []
        for W_d, W_sb in [(Wq_d, Wq_sb), (Wk_d, Wk_sb), (Wv_d, Wv_sb)]:
            for t in range(ND):
                wqkv_list.append((W_d[t * 128:(t + 1) * 128, :], W_sb, t))
        wsC = [FA1, FA2]
        v_cp = [0] * len(wqkv_list)
        for j, (src_ap, W_sb, t) in enumerate(wqkv_list):
            p = j % 2
            w = [(sV, v_cp[j - 2])] if j >= 2 else [(sV, sV.n)]
            em.emit("sync", lambda e, src_ap=src_ap, p=p: e.dma_start(
                out=wsC[p], in_=src_ap), waits=w, inc=sDC[p], inc_by=16)
            vd = sDC[p].n
            em.emit("vector", lambda e, W_sb=W_sb, t=t, p=p: e.tensor_copy(
                out=W_sb[:, t, :], in_=wsC[p]),
                waits=[(sDC[p], vd)], inc=sV)
            v_cp[j] = sV.n
        v_wqkv = sV.n
        if debug_taps:
            em.emit("sync", lambda e: e.dma_start(out=taps["t_wv0"], in_=Wv_sb[:, 0, :]),
                    waits=[(sV, v_wqkv)], inc=sDS, inc_by=16)
            em.emit("sync", lambda e: e.dma_start(out=taps["t_wq0"], in_=Wq_sb[:, 0, :]),
                    waits=[(sV, v_wqkv)], inc=sDS, inc_by=16)

        xt_free = (sV, 0)
        v_w_T = sT.n
        v_w_G = sG.n
        for i in range(NT):
            em.emit("sync", lambda e, i=i: e.dma_start(
                out=xtile[:], in_=x_d[i * 128:(i + 1) * 128, :]),
                waits=[xt_free] + ([(sT, v_w_T), (sG, v_w_G), (sV, v_wqkv)]
                                   if i == 0 else []),
                inc=sDX, inc_by=16)
            vdma = sDX.n
            em.emit("vector", lambda e: e.bn_stats(out=bnst[:, 0:6], in_=xtile[:, 0:512]),
                    waits=[(sDX, vdma)] + ([(sT, v_w_T), (sG, v_w_G)] if i == 0 else []))
            em.emit("vector", lambda e: e.bn_stats(out=bnst[:, 6:12], in_=xtile[:, 512:1024]))
            em.emit("vector", lambda e: e.bn_aggr(out=bnag[:], in_=bnst[:]), ser=True)
            em.emit("vector", lambda e: e.tensor_scalar(
                out=stat[:, 5:6], in0=bnag[:, 1:2], scalar1=1e-5, scalar2=None,
                op0=ALU.add), inc=sV, ser=True)
            em.emit("scalar", lambda e: e.activation(
                out=stat[:, 2:3], in_=stat[:, 5:6], func=AF.Ln),
                waits=[(sV, sV.n)], inc=sC)
            em.emit("vector", lambda e: e.tensor_scalar(
                out=stat[:, 3:4], in0=stat[:, 2:3], scalar1=-0.5, scalar2=None,
                op0=ALU.mult), waits=[(sC, sC.n)], inc=sV)
            em.emit("scalar", lambda e: e.activation(
                out=stat[:, 4:5], in_=stat[:, 3:4], func=AF.Exp),
                waits=[(sV, sV.n)], inc=sC)
            em.emit("vector", lambda e: e.tensor_scalar(
                out=xn_f[:], in0=xtile[:], scalar1=bnag[:, 0:1], scalar2=stat[:, 4:5],
                op0=ALU.subtract, op1=ALU.mult), waits=[(sC, sC.n)], inc=sV, ser=True)
            xt_free = (sV, sV.n)
            em.emit("vector", lambda e: e.tensor_copy(out=xnb[:], in_=xn_f[:]),
                    waits=[(sT, sT.n)], inc=sV, ser=True)
            v_xn = sV.n
            em.emit("vector", lambda e: e.reduce_max(
                out=amaxx[:], in_=xn_f[:].rearrange("p (g k) -> p g k", k=32),
                axis=AX.X, apply_absolute_value=True), inc=sV, ser=True)
            v_sc = scale_chain(32, amaxx[:], scx[:], rscx[:], tmpx, rndx[:], sV.n)
            em.emit("vector", lambda e: e.tensor_tensor(
                out=tx_f[:].rearrange("p (g k) -> p g k", k=32),
                in0=xn_f[:].rearrange("p (g k) -> p g k", k=32),
                in1=rscx[:].unsqueeze(-1).to_broadcast([128, 32, 32]),
                op=ALU.mult), waits=[(sV, v_sc)], ser=True)
            em.emit("vector", lambda e: e.tensor_scalar(
                out=rx_b[:], in0=tx_f[:], scalar1=192.0, scalar2=None, op0=ALU.add),
                waits=([(sDWp[0], v_wst[0])] if i == 0 else []), ser=True)
            em.emit("vector", lambda e: e.tensor_scalar(
                out=rx_b[:], in0=rx_b[:], scalar1=192.0, scalar2=None, op0=ALU.subtract), ser=True)
            em.emit("vector", lambda e: e.tensor_tensor(
                out=xqb[:].rearrange("p (g k) -> p g k", k=32),
                in0=rx_b[:].rearrange("p (g k) -> p g k", k=32),
                in1=scx[:].unsqueeze(-1).to_broadcast([128, 32, 32]),
                op=ALU.mult), waits=[(sT, sT.n)], inc=sV, ser=True)
            v_xq = sV.n
            if debug_taps and i == 0:
                em.emit("sync", lambda e: e.dma_start(out=taps["t_xq"], in_=xqb[:]),
                        waits=[(sV, v_xq)], inc=sDS, inc_by=16)
            for t in range(ND):
                pt = pT1 if t % 2 == 0 else pT2
                em.emit("tensor", lambda e, t=t, pt=pt: e.transpose(
                    out=pt[:, 0:128], in_=xnb[:, t * 128:(t + 1) * 128], identity=ident[:]),
                    waits=[(sV, v_xn), (sC, sC.n)], inc=sT)
                em.emit("scalar", lambda e, t=t, i=i, pt=pt: e.copy(
                    out=xnT[:, t, i * 128:(i + 1) * 128], in_=pt[:, 0:128]),
                    waits=[(sT, sT.n)], inc=sC)
            for t in range(ND):
                pt = pT1 if t % 2 == 0 else pT2
                em.emit("tensor", lambda e, t=t, pt=pt: e.transpose(
                    out=pt[:, 0:128], in_=xqb[:, t * 128:(t + 1) * 128], identity=ident[:]),
                    waits=[(sV, v_xq), (sC, sC.n), (sDP, sDP.n)], inc=sT)
                em.emit("scalar", lambda e, t=t, pt=pt: e.copy(
                    out=ttile[:, t, :], in_=pt[:, 0:128]),
                    waits=[(sT, sT.n)] + ([(sDWp[1], v_wst[1])]
                                          if (i == 0 and t == 0) else []), inc=sC)
            em.emit("gpsimd", lambda e, i=i: e.dma_start(
                out=xqt_s[:, i, :, :].rearrange("t p n -> p t n"), in_=ttile[:]),
                waits=[(sC, sC.n)], inc=sDP, inc_by=16)
        if debug_taps:
            em.emit("sync", lambda e: e.dma_start(
                out=taps["t_xnT"], in_=R1[:]), waits=[(sC, sC.n)], inc=sDS, inc_by=16)

        v_b_done = sC.n
        if debug_taps:
            em.emit("sync", lambda e: e.dma_start(out=taps["t_wv1"], in_=Wv_sb[:, 0, :]),
                    waits=[(sC, v_b_done), (sV, sV.n)], inc=sDS, inc_by=16)
        pk_free = {0: (sC, 0), 1: (sC, 0)}
        v_vcred = 0
        v_vstore = 0
        v_vcstore = 0
        vtt = vt_buf[:].rearrange("p (t n) -> p t n", n=128)
        for proj, W_sb in enumerate([Wq_sb, Wk_sb, Wv_sb]):
            if debug_taps and proj == 2:
                em.emit("sync", lambda e: e.dma_start(out=taps["t_wv2"], in_=Wv_sb[:, 0, :]),
                        waits=[(sC, sC.n), (sV, sV.n)], inc=sDS, inc_by=16)
            for c in range(8):
                for m in range(ND):
                    bk = (c * ND + m) % 2
                    pp = pA if bk == 0 else pB
                    for k in range(ND):
                        w = []
                        if proj == 0 and c == 0 and m == 0 and k == 0:
                            w = [(sC, v_b_done), (sV, v_wqkv)]
                        if k == 0:
                            w = w + [pk_free[bk]]
                        em.emit("tensor", lambda e, W_sb=W_sb, m=m, k=k, c=c, pp=pp: e.matmul(
                            out=pp[:], lhsT=W_sb[:, k, m * 128:(m + 1) * 128],
                            rhs=xnT[:, k, c * 512:(c + 1) * 512],
                            start=(k == 0), stop=(k == ND - 1)),
                            waits=w, inc=(sT if k == ND - 1 else None))
                    v_mm = sT.n
                    if proj == 0:
                        em.emit("scalar", lambda e, m=m, pp=pp: e.activation(
                            out=vt_buf[:], in_=pp[:], func=AF.Identity,
                            bias=bqs_sb[:, m:m + 1], scale=SCALE),
                            waits=[(sT, v_mm), (sDP, sDP.n)], inc=sC)
                        pk_free[bk] = (sC, sC.n)
                        em.emit("gpsimd", lambda e, m=m, c=c: e.dma_start(
                            out=qt_s[m, c * 4:(c + 1) * 4, :, :].rearrange("b p n -> p b n"),
                            in_=vt_buf[:].rearrange("p (b n) -> p b n", n=128)),
                            waits=[(sC, sC.n)], inc=sDP, inc_by=16)
                    elif proj == 1:
                        em.emit("scalar", lambda e, m=m, c=c, pp=pp: e.activation(
                            out=KT[:, m, c * 512:(c + 1) * 512], in_=pp[:],
                            func=AF.Identity, bias=bk_sb[:, m:m + 1]),
                            waits=[(sT, v_mm)], inc=sC)
                        pk_free[bk] = (sC, sC.n)
                        em.emit("vector", lambda e, m=m, c=c: e.reduce_sum(
                            out=vc_red[:],
                            in_=KT[:, m, c * 512:(c + 1) * 512].rearrange("p (g k) -> p g k", k=4),
                            axis=AX.X), waits=[(sC, sC.n)], ser=True)
                        em.emit("vector", lambda e, m=m, c=c: e.tensor_scalar(
                            out=KcT[:, m, c * 128:(c + 1) * 128], in0=vc_red[:],
                            scalar1=0.25, scalar2=None, op0=ALU.mult), inc=sV, ser=True)
                    else:
                        em.emit("scalar", lambda e, m=m, pp=pp: e.activation(
                            out=vt_buf[:], in_=pp[:], func=AF.Identity,
                            bias=bv_sb[:, m:m + 1]),
                            waits=[(sT, v_mm), (sVS, v_vstore), (sV, v_vcred),
                                   (sDP, sDP.n)],
                            inc=sC)
                        v_ev = sC.n
                        if debug_taps and c == 0 and m == 0:
                            em.emit("scalar", lambda e, pp=pp: e.copy(
                                out=vpstap[:], in_=pp[:]), waits=[(sC, v_ev)], inc=sC)
                            em.emit("sync", lambda e: e.dma_start(
                                out=taps["t_vps"], in_=vpstap[:]),
                                waits=[(sC, sC.n)], inc=sDS, inc_by=16)
                            pk_free[bk] = (sC, sC.n)
                        else:
                            pk_free[bk] = (sC, v_ev)
                        v_ev = sC.n
                        if debug_taps and c == 0 and m == 0:
                            em.emit("sync", lambda e: e.dma_start(
                                out=taps["t_vtb"], in_=vt_buf[:]),
                                waits=[(sC, v_ev)], inc=sDS, inc_by=16)
                        em.emit("vector", lambda e: e.reduce_sum(
                            out=vc_red[:],
                            in_=vt_buf[:].rearrange("p (g k) -> p g k", k=4),
                            axis=AX.X), waits=[(sC, v_ev)], inc=sV, ser=True)
                        v_vcred = sV.n
                        tr_batch(
                            4,
                            lambda k: vt_buf[:, k * 128:(k + 1) * 128],
                            lambda k: vtt[:, k, :],
                            tr_waits=[(sC, v_ev)],
                            ev_waits=[(sVS, v_vstore)])
                        em.emit("gpsimd", lambda e, m=m, c=c: e.dma_start(
                            out=v_s[c * 4:(c + 1) * 4, m, :, :]
                            .rearrange("t p n -> p t n"), in_=vtt),
                            waits=[(sV, sV.n)], inc=sVS, inc_by=16)
                        v_vstore = sVS.n
                        em.emit("vector", lambda e: e.tensor_scalar(
                            out=vc_bf[:], in0=vc_red[:], scalar1=0.25, scalar2=None,
                            op0=ALU.mult), waits=[(sVC, v_vcstore)], inc=sV, ser=True)
                        tr_batch(
                            1,
                            lambda k: vc_bf[:],
                            lambda k: vc_bf[:],
                            tr_waits=[(sV, sV.n)])
                        em.emit("gpsimd", lambda e, m=m, c=c: e.dma_start(
                            out=vc_s[c, m, :, :], in_=vc_bf[:]),
                            waits=[(sV, sV.n)], inc=sVC, inc_by=16)
                        v_vcstore = sVC.n
        v_cC, v_cV, v_cDP, v_cT = sC.n, sV.n, sDP.n, sT.n
        v_cVS, v_cVC = sVS.n, sVC.n
        if debug_taps:
            em.emit("sync", lambda e: e.dma_start(out=taps["t_KT"], in_=R2[:]),
                    waits=[(sC, v_cC), (sV, v_cV)], inc=sDS, inc_by=16)
            em.emit("sync", lambda e: e.dma_start(out=taps["t_KcT"], in_=R3[:, 0:8192]),
                    waits=[(sC, v_cC), (sV, v_cV)], inc=sDS, inc_by=16)

        em.emit("sync", lambda e: e.dma_start(
            out=cmask[:], in_=cmask_d),
            waits=[(sT, v_cT)], inc=sDS, inc_by=16)
        v_cmask = sDS.n
        for t in range(NT):
            em.emit("sync", lambda e, t=t: e.dma_start(
                out=V_sb[:, t, :].rearrange("p (m n) -> p m n", n=128),
                in_=v_s[t, :, :, :].rearrange("m p n -> p m n")),
                waits=[(sVS, v_cVS), (sT, v_cT)] if t == 0 else [],
                inc=sDV, inc_by=16)

        v_vload = sDV.n
        for dch in range(2):
            pp = pACC[dch]
            for kt in range(NT):
                em.emit("tensor", lambda e, dch=dch, kt=kt, pp=pp: e.matmul(
                    out=pp[0:1, :], lhsT=onc[:], rhs=V_sb[:, kt, dch * 512:(dch + 1) * 512],
                    start=(kt == 0), stop=(kt == NT - 1)),
                    waits=([(sDV, v_vload)] if kt == 0 else []),
                    inc=(sT if kt == NT - 1 else None))
            em.emit("scalar", lambda e, dch=dch, pp=pp: e.copy(
                out=vsum_rowb[:, dch * 512:(dch + 1) * 512], in_=pp[0:1, :]),
                waits=[(sT, sT.n)], inc=sC)
        v_vsum = sC.n

        st_state = {}
        csc_free = {0: (sC, 0), 1: (sC, 0)}
        raw_free = {0: (sC, 0), 1: (sC, 0)}
        st_raw = [0, 0]
        st_q = [0, 0]
        st_qsl_free = [(sT, 0), (sT, 0)]
        v_tail_end = [0, 0]
        v_prev_tail = [0]
        prev_vcmm = [0] * 8
        rawbs = [rawb, r3(20480, 4096)]

        def emit_scores(qb):
            rawb_p = rawbs[qb % 2]
            qsl_p = qsls[qb % 2]
            sq = sDQ if qb % 2 == 0 else sDQ2
            em.emit("sync", lambda e, qb=qb, qsl_p=qsl_p: e.dma_start(
                out=qsl_p[:], in_=qt_s[:, qb, :, :].rearrange("t p n -> p t n")),
                waits=[st_qsl_free[qb % 2]] + ([(sDP, v_cDP)] if qb == 0 else []),
                inc=sq, inc_by=16)
            v_q = sq.n
            st_q[qb % 2] = v_q
            for c in range(8):
                bk = c % 2
                pp = pA if bk == 0 else pB
                for k in range(ND):
                    w = [(sq, v_q)] if (c == 0 and k == 0) else []
                    if qb == 0 and c == 0 and k == 0:
                        w = w + [pk_free[0], pk_free[1]]
                    if k == 0:
                        w = w + [raw_free[bk]]
                    em.emit("tensor", lambda e, k=k, c=c, pp=pp, qsl_p=qsl_p: e.matmul(
                        out=pp[:], lhsT=qsl_p[:, k, :], rhs=KT[:, k, c * 512:(c + 1) * 512],
                        start=(k == 0), stop=(k == ND - 1)),
                        waits=w, inc=(sT if k == ND - 1 else None))
                em.emit("scalar", lambda e, c=c, pp=pp: e.copy(
                    out=rawb_p[:, c * 512:(c + 1) * 512], in_=pp[:]),
                    waits=[(sT, sT.n)] + ([(sV, v_tail_end[qb % 2])] if c == 0 else []), inc=sC)
                raw_free[bk] = (sC, sC.n)
            st_raw[qb % 2] = sC.n
            if debug_taps and qb == 1:
                em.emit("sync", lambda e: e.dma_start(out=taps["t_rawb"], in_=rawb_p[:]),
                        waits=[(sC, st_raw[1])], inc=sDS, inc_by=16)
            st_state["raw_done"] = (sT, sT.n)


        def emit_tail(qb):
            dc = qb // 4
            off = qb % 4
            rawb_p = rawbs[qb % 2]
            qsl_p = qsls[qb % 2]
            for cc_ in range(2):
                pp = pC_ if cc_ == 0 else pD
                for k in range(ND):
                    w = [((sDQ if qb % 2 == 0 else sDQ2), st_q[qb % 2])] \
                        if (cc_ == 0 and k == 0) else []
                    if qb == 0 and cc_ == 0 and k == 0:
                        w = w + [(sV, v_cV)]
                    if k == 0:
                        w = w + [csc_free[cc_]]
                    em.emit("tensor", lambda e, cc_=cc_, k=k, pp=pp, qsl_p=qsl_p: e.matmul(
                        out=pp[:], lhsT=qsl_p[:, k, :], rhs=KcT[:, k, cc_ * 512:(cc_ + 1) * 512],
                        start=(k == 0), stop=(k == ND - 1)),
                        waits=w, inc=(sT if k == ND - 1 else None))
                em.emit("scalar", lambda e, cc_=cc_, pp=pp: e.copy(
                    out=c_sc[:, cc_ * 512:(cc_ + 1) * 512], in_=pp[:]),
                    waits=[(sT, sT.n)] + ([(sV, v_prev_tail[0])] if cc_ == 0 else []), inc=sC)
                csc_free[cc_] = (sC, sC.n)
            v_csc = sC.n
            st_qsl_free[qb % 2] = (sT, sT.n)
            v_vcmm = [0] * 8

            def emit_vc_load(j, w):
                vb = vcs8[j % 4]
                eng, sem = (("sync", sDC[0]) if j % 2 == 0
                            else ("gpsimd", sDC2[0]))
                em.emit(eng, lambda e, j=j, vb=vb: e.dma_start(
                    out=vb[:].rearrange("p (m n) -> p m n", n=128),
                    in_=vc_s[j, :, :, :].rearrange("m p n -> p m n")),
                    waits=w, inc=sem, inc_by=16)

            for j in range(4):
                emit_vc_load(j, ([(sVC, v_cVC), (sT, v_cT)] if qb == 0
                                 else [(sT, prev_vcmm[j + 4])]))
            em.emit("vector", lambda e: e.tensor_max(
                out=t1a[:], in0=rawb_p[:, 0:1024], in1=rawb_p[:, 1024:2048]),
                waits=[(sC, st_raw[qb % 2]), (sDA, sDA.n)])
            em.emit("vector", lambda e: e.tensor_max(
                out=t1b[:], in0=rawb_p[:, 2048:3072], in1=rawb_p[:, 3072:4096]))
            em.emit("vector", lambda e: e.tensor_max(
                out=g8[0][:], in0=t1a[:, 0:512], in1=t1a[:, 512:1024]), ser=True)
            em.emit("vector", lambda e: e.tensor_max(
                out=g8[0][:], in0=g8[0][:], in1=t1b[:, 0:512]), ser=True)
            em.emit("vector", lambda e: e.tensor_max(
                out=g8[0][:], in0=g8[0][:], in1=t1b[:, 512:1024]), inc=sV, ser=True)
            em.emit("vector", lambda e: e.tensor_max(
                out=g8[1][:, 0:256], in0=g8[0][:, 0:256], in1=g8[0][:, 256:512]),
                inc=sV, ser=True)
            for it in range(8):
                cur = g8[(it + 1) % 2][:, 0:256]
                nxt = g8[it % 2][:, 0:256]
                em.emit("vector", lambda e, it=it, cur=cur: e.max(
                    out=vals_b[:, it * 8:(it + 1) * 8], in_=cur),
                    waits=[(sV, sV.n)], inc=sV)
                if it < 7:
                    em.emit("vector", lambda e, it=it, cur=cur, nxt=nxt: e.match_replace(
                        out=nxt, in_to_replace=vals_b[:, it * 8:(it + 1) * 8],
                        in_values=cur, imm_value=NEG),
                        waits=[(sV, sV.n)], inc=sV)
            v_vals = sV.n
            if debug_taps and qb == 1:
                em.emit("vector", lambda e: e.tensor_copy(out=vals_dbg[:], in_=vals_b[:]),
                        waits=[(sV, v_vals)], inc=sV)
                em.emit("sync", lambda e: e.dma_start(out=taps["t_vals"], in_=vals_dbg[:]),
                        waits=[(sV, sV.n)], inc=sDS, inc_by=16)
            em.emit("vector", lambda e: e.tensor_copy(out=tauf[:, 0:1], in_=vals_b[:, 63:64]),
                    waits=[(sV, v_vals)])
            em.emit("vector", lambda e: e.tensor_scalar(
                out=sstat[:, 0:1], in0=vals_b[:, 0:1], scalar1=0.0, scalar2=None,
                op0=ALU.max))
            em.emit("vector", lambda e: e.tensor_scalar(
                out=sstat[:, 1:2], in0=sstat[:, 0:1], scalar1=-1.0, scalar2=None,
                op0=ALU.mult), inc=sV, ser=True)
            v_mu = sV.n
            em.emit("scalar", lambda e: e.activation(
                out=sstat[:, 2:3], in_=sstat[:, 0:1], func=AF.Exp, scale=-1.0),
                waits=[(sV, v_mu)], inc=sC)
            v_emu = sC.n

            v_stt = [0] * 8
            for c in range(8):
                eb = Ebuf[c % 4]
                ms = mselc[c % 2]
                em.emit("scalar", lambda e, c=c, eb=eb: e.activation(
                    out=eb[:], in_=rawb_p[:, c * 512:(c + 1) * 512], func=AF.Exp,
                    bias=sstat[:, 1:2]),
                    waits=([(sV, v_mu)] if c == 0 else [])
                    + ([(sV, v_stt[c - 4])] if c >= 4 else []),
                    inc=sC)
                v_e = sC.n
                em.emit("vector", lambda e, c=c, ms=ms: e.tensor_scalar(
                    out=ms[:], in0=rawb_p[:, c * 512:(c + 1) * 512], scalar1=tauf[:, 0:1],
                    scalar2=None, op0=ALU.is_ge),
                    waits=[(sV, v_mu)] if c == 0 else [])
                em.emit("vector", lambda e, c=c, eb=eb, ms=ms: e.scalar_tensor_tensor(
                    out=Cb[1][:], in0=eb[:], scalar=sstat[:, 2:3],
                    in1=ms[:], op0=ALU.subtract, op1=ALU.mult,
                    accum_out=gsum[:, c:c + 1]),
                    waits=[(sC, v_e), (sC, v_emu)], inc=sV, ser=True)
                v_stt[c] = sV.n
            v_g = sV.n
            em.emit("vector", lambda e: e.reduce_sum(
                out=sstat[:, 14:15], in_=gsum[:], axis=AX.X), waits=[(sV, v_g)])
            em.emit("vector", lambda e: e.scalar_tensor_tensor(
                out=sstat[:, 15:16], in0=sstat[:, 2:3], scalar=4096.0,
                in1=sstat[:, 14:15], op0=ALU.mult, op1=ALU.add), ser=True)
            em.emit("vector", lambda e: e.tensor_scalar(
                out=sstat[:, 15:16], in0=sstat[:, 15:16], scalar1=3.0, scalar2=None,
                op0=ALU.mult), ser=True)
            em.emit("vector", lambda e: e.reciprocal(
                out=sstat[:, 14:15], in_=sstat[:, 15:16]), inc=sV, ser=True)
            v_rz3 = sV.n
            em.emit("vector", lambda e: e.tensor_scalar(
                out=sstat[:, 3:4], in0=sstat[:, 2:3], scalar1=sstat[:, 14:15],
                scalar2=None, op0=ALU.mult), waits=[(sV, v_rz3)], inc=sV, ser=True)
            v_biasg = sV.n
            em.emit("vector", lambda e: e.tensor_copy(
                out=coefb[:], in_=sstat[:, 3:4]), inc=sV, ser=True)
            v_coef = sV.n
            _, ev_coefrow = tr_batch(
                1, lambda k: coefb[:], lambda k: coef_row,
                tr_waits=[(sV, v_coef)], rows=1, ev_eng="scalar")

            v_tr_chunk = {}
            v_mm_chunk = {}
            v_stt2 = [0] * 8
            order = list(range(dc + 1, 8)) + list(range(dc + 1))
            v_biasw = 0
            for idx, c in enumerate(order):
                if c == 0:
                    em.emit("vector", lambda e, off=off, dc=dc: e.tensor_tensor(
                        out=wdiag[:], in0=rawb_p[:, dc * 512:(dc + 1) * 512],
                        in1=cmask[:, off, :], op=ALU.add),
                        waits=([(sDS, v_cmask)] if qb == 0 else []))
                    if dc > 0:
                        em.emit("vector", lambda e, dc=dc: e.reduce_max(
                            out=sstat[:, 4:5], in_=rawb_p[:, 0:dc * 512], axis=AX.X))
                        em.emit("vector", lambda e: e.reduce_max(
                            out=sstat[:, 5:6], in_=wdiag[:], axis=AX.X), ser=True)
                        em.emit("vector", lambda e: e.tensor_tensor(
                            out=sstat[:, 6:7], in0=sstat[:, 4:5], in1=sstat[:, 5:6],
                            op=ALU.max), ser=True)
                    else:
                        em.emit("vector", lambda e: e.reduce_max(
                            out=sstat[:, 6:7], in_=wdiag[:], axis=AX.X), ser=True)
                    em.emit("vector", lambda e: e.tensor_scalar(
                        out=sstat[:, 7:8], in0=sstat[:, 6:7], scalar1=-1.0,
                        scalar2=None, op0=ALU.mult), inc=sV, ser=True)
                    v_nmw = sV.n
                    for cw in range(dc + 1):
                        if cw < dc:
                            fn = (lambda e, cw=cw: e.activation(
                                out=Ewb[0][:],
                                in_=rawb_p[:, cw * 512:(cw + 1) * 512],
                                func=AF.Exp, bias=sstat[:, 7:8],
                                accum_out=swp[:, cw:cw + 1]))
                        else:
                            fn = (lambda e, cw=cw: e.activation(
                                out=Ewb[0][:], in_=wdiag[:],
                                func=AF.Exp, bias=sstat[:, 7:8],
                                accum_out=swp[:, cw:cw + 1]))
                        em.emit("scalar", fn,
                                waits=[(sV, v_nmw)] if cw == 0 else [],
                                inc=sC, ser=True)
                    v_expw = sC.n
                    em.emit("vector", lambda e, dc=dc: e.reduce_sum(
                        out=sstat[:, 8:9], in_=swp[:, 0:dc + 1], axis=AX.X),
                        waits=[(sC, v_expw)])
                    em.emit("vector", lambda e: e.tensor_scalar(
                        out=sstat[:, 9:10], in0=sstat[:, 8:9], scalar1=3.0,
                        scalar2=None, op0=ALU.mult), ser=True)
                    em.emit("vector", lambda e: e.reciprocal(
                        out=sstat[:, 10:11], in_=sstat[:, 9:10]), inc=sV, ser=True)
                    v_biasw = sV.n
                cb = Cb[idx % 2]
                eb = Ebuf[idx % 4]
                ms = mselc[idx % 2]
                em.emit("scalar", lambda e, c=c, eb=eb: e.activation(
                    out=eb[:], in_=rawb_p[:, c * 512:(c + 1) * 512], func=AF.Exp,
                    bias=sstat[:, 1:2]),
                    waits=([(sV, v_biasg), (sV, v_g)] if idx == 0 else [])
                    + ([(sV, v_stt2[idx - 4])] if idx >= 4 else []),
                    inc=sC)
                v_e2 = sC.n
                if c <= dc:
                    em.emit("scalar", lambda e, c=c, dc=dc, idx=idx: e.activation(
                        out=Ewb[idx % 2][:],
                        in_=(rawb_p[:, c * 512:(c + 1) * 512] if c < dc else wdiag[:]),
                        func=AF.Exp, bias=sstat[:, 7:8]),
                        waits=([(sV, v_biasw)] if c == 0 else [])
                        + ([(sV, v_stt2[idx - 1])] if idx >= 1 else []), inc=sC)
                    v_w2 = sC.n
                em.emit("vector", lambda e, c=c, ms=ms: e.tensor_scalar(
                    out=ms[:], in0=rawb_p[:, c * 512:(c + 1) * 512], scalar1=tauf[:, 0:1],
                    scalar2=sstat[:, 14:15], op0=ALU.is_ge, op1=ALU.mult),
                    waits=([(sV, v_biasg)] if idx == 0 else []))
                wst = [(sC, v_e2)]
                if idx >= 2:
                    wst.append((sT, v_tr_chunk[idx - 2]))
                em.emit("vector", lambda e, c=c, cb=cb, eb=eb, ms=ms:
                        e.scalar_tensor_tensor(
                            out=cb[:], in0=eb[:], scalar=sstat[:, 2:3],
                            in1=ms[:], op0=ALU.subtract, op1=ALU.mult),
                        waits=wst, inc=sV, ser=True)
                if c <= dc:
                    em.emit("vector", lambda e, idx=idx: e.tensor_scalar(
                        out=Ewb[idx % 2][:], in0=Ewb[idx % 2][:],
                        scalar1=sstat[:, 10:11], scalar2=None, op0=ALU.mult),
                        waits=[(sC, v_w2)], inc=sV, ser=True)
                    em.emit("vector", lambda e, cb=cb, idx=idx: e.tensor_tensor(
                        out=cb[:], in0=cb[:], in1=Ewb[idx % 2][:], op=ALU.add),
                        inc=sV, ser=True)
                v_cb = sV.n
                v_stt2[idx] = v_cb
                evw = []
                if idx >= 2:
                    evw.append((sT, v_mm_chunk[idx - 2]))
                if qb == 0 and idx == 0:
                    evw.append((sDP, v_cDP))
                v_tr_chunk[idx], ev_c = tr_batch(
                    4, lambda k, cb=cb: cb[:, k * 128:(k + 1) * 128],
                    lambda k, idx=idx: ctp[idx % 2][:, k, :],
                    tr_waits=[(sV, v_cb)], ev_waits=evw, ev_eng="scalar")
                for j in range(4):
                    for dch in range(2):
                        pp = pACC[dch]
                        w = []
                        if j == 0 and dch == 0:
                            w = [ev_c]
                            if idx == 0:
                                w.append((sC, sC.n))
                        em.emit("tensor", lambda e, c=c, j=j, dch=dch, pp=pp, idx=idx:
                                e.matmul(
                                    out=pp[:], lhsT=ctp[idx % 2][:, j, :],
                                    rhs=V_sb[:, c * 4 + j, dch * 512:(dch + 1) * 512],
                                    start=(idx == 0 and j == 0), stop=False),
                                waits=w,
                                inc=(sT if (j == 3 and dch == 1) else None))
                v_mm_chunk[idx] = sT.n

            em.emit("vector", lambda e: e.reduce_max(
                out=scp[:, 0:1], in_=c_sc[:], axis=AX.X), waits=[(sC, v_csc)],
                inc=sV, ser=True)
            v_tail_end[qb % 2] = sV.n
            em.emit("vector", lambda e: e.tensor_scalar(
                out=scp[:, 1:2], in0=scp[:, 0:1], scalar1=-1.0, scalar2=None,
                op0=ALU.mult), inc=sV, ser=True)
            v_nmc = sV.n
            for cc_ in range(2):
                em.emit("scalar", lambda e, cc_=cc_: e.activation(
                    out=Pcb[:, cc_ * 512:(cc_ + 1) * 512],
                    in_=c_sc[:, cc_ * 512:(cc_ + 1) * 512], func=AF.Exp,
                    bias=scp[:, 1:2], accum_out=swp[:, 6 + cc_:7 + cc_]),
                    waits=[(sV, v_nmc)] if cc_ == 0 else [], inc=sC)
            em.emit("vector", lambda e: e.reduce_sum(
                out=sstat[:, 11:12], in_=swp[:, 6:8], axis=AX.X),
                waits=[(sC, sC.n)])
            em.emit("vector", lambda e: e.tensor_scalar(
                out=sstat[:, 12:13], in0=sstat[:, 11:12], scalar1=3.0, scalar2=None,
                op0=ALU.mult), ser=True)
            em.emit("vector", lambda e: e.reciprocal(
                out=sstat[:, 13:14], in_=sstat[:, 12:13]), inc=sV, ser=True)
            em.emit("vector", lambda e: e.tensor_scalar(
                out=Pcb[:], in0=Pcb[:], scalar1=sstat[:, 13:14], scalar2=None,
                op0=ALU.mult), waits=[(sV, sV.n)], inc=sV)
            v_pcb = sV.n
            for h in range(2):
                _, ev_pct = tr_batch(
                    4, lambda k, h=h: Pcb[:, (h * 4 + k) * 128:(h * 4 + k + 1) * 128],
                    lambda k, h=h: PcT[:, h * 4 + k, :],
                    tr_waits=([(sV, v_pcb)] if h == 0 else []), ev_eng="scalar")
            v_pref = [(sDC[0], sDC[0].n), (sDC2[0], sDC2[0].n)]
            for j in range(4):
                for dch in range(2):
                    pp = pACC[dch]
                    w = list(v_pref) if dch == 0 else []
                    if j == 0 and dch == 0:
                        w.append(ev_pct)
                    em.emit("tensor", lambda e, j=j, dch=dch, pp=pp: e.matmul(
                        out=pp[:], lhsT=PcT[:, j, :],
                        rhs=vcs8[j][:, dch * 512:(dch + 1) * 512],
                        start=False, stop=False),
                        waits=w, inc=(sT if dch == 1 else None))
                v_vcmm[j] = sT.n
            for j in range(4, 8):
                emit_vc_load(j, [(sT, v_vcmm[j - 4])])
            v_all = [(sDC[0], sDC[0].n), (sDC2[0], sDC2[0].n)]
            for j in range(4, 8):
                for dch in range(2):
                    pp = pACC[dch]
                    w = list(v_all) if (j == 4 and dch == 0) else []
                    em.emit("tensor", lambda e, j=j, dch=dch, pp=pp: e.matmul(
                        out=pp[:], lhsT=PcT[:, j, :],
                        rhs=vcs8[j % 4][:, dch * 512:(dch + 1) * 512],
                        start=False, stop=False),
                        waits=w, inc=(sT if dch == 1 else None))
                v_vcmm[j] = sT.n
            for idx in range(8):
                prev_vcmm[idx] = v_vcmm[idx]
            for dch in range(2):
                pp = pACC[dch]
                em.emit("tensor", lambda e, dch=dch, pp=pp: e.matmul(
                    out=pp[:], lhsT=coef_row, rhs=vsum_rowb[0:1, dch * 512:(dch + 1) * 512],
                    start=False, stop=True),
                    waits=[ev_coefrow, (sC, v_vsum)], inc=sT)
                em.emit("scalar", lambda e, dch=dch, pp=pp: e.copy(
                    out=abuf[:, dch * 512:(dch + 1) * 512], in_=pp[:]),
                    waits=[(sT, sT.n), (sV, v_g)], inc=sC)
            v_abuf = sC.n
            if debug_taps and qb == 1:
                em.emit("sync", lambda e: e.dma_start(out=taps["t_attn"], in_=abuf[:]),
                        waits=[(sC, v_abuf)], inc=sDS, inc_by=16)
                em.emit("sync", lambda e: e.dma_start(out=taps["t_sstat"], in_=sstat[:]),
                        waits=[(sC, v_abuf)], inc=sDS, inc_by=16)
                em.emit("sync", lambda e: e.dma_start(out=taps["t_pcb"], in_=Pcb[:]),
                        waits=[(sC, v_abuf)], inc=sDS, inc_by=16)
                em.emit("sync", lambda e: e.dma_start(out=taps["t_csc"], in_=c_sc[:]),
                        waits=[(sC, v_abuf)], inc=sDS, inc_by=16)
                em.emit("sync", lambda e: e.dma_start(out=taps["t_vsum"], in_=vsum_rowb),
                        waits=[(sC, v_abuf)], inc=sDS, inc_by=16)
            for h in range(2):
                _, ev_at = tr_batch(
                    4, lambda k, h=h: abuf[:, (h * 4 + k) * 128:(h * 4 + k + 1) * 128],
                    lambda k, h=h: atile[:, h * 4 + k, :],
                    tr_waits=([(sC, v_abuf)] if h == 0 else []),
                    ev_waits=([(sDA, sDA.n)] if h == 0 else []), ev_eng="scalar")
            em.emit("sync", lambda e, qb=qb: e.dma_start(
                out=at_s[:, qb, :, :].rearrange("t p n -> p t n"), in_=atile[:]),
                waits=[ev_at], inc=sDA, inc_by=16)
            v_prev_tail[0] = sV.n

        for qb in range(QB):
            emit_scores(qb)
            if qb >= 1:
                emit_tail(qb - 1)
        emit_tail(QB - 1)
        v_s_done_C = sC.n
        v_s_done_T = sT.n
        v_s_atdone = sDA.n
        for t in range(ND):
            em.emit("sync", lambda e, t=t: e.dma_start(
                out=wfcT[:, t, :].rearrange("p (i n) -> p i n", n=128),
                in_=wfct_s[t, :, :, :].rearrange("i p n -> p i n")),
                waits=[(sT, v_s_done_T), (sDWp[0], v_wst[0]),
                       (sDWp[1], v_wst[1])] if t == 0 else [],
                inc=sDW, inc_by=16)
        for hf in range(2):
            em.emit("sync", lambda e, hf=hf: e.dma_start(
                out=brow_st[:, 0:2048], in_=bfc_d[0:1, hf * 2048:(hf + 1) * 2048]),
                waits=[(sV, sV.n)], inc=sDS, inc_by=16)
            em.emit("vector", lambda e, hf=hf: e.tensor_copy(
                out=bfc_rowb[0:1, hf * 2048:(hf + 1) * 2048], in_=brow_st[0:1, 0:2048]),
                waits=[(sDS, sDS.n)] + ([(sT, v_s_done_T)] if hf == 0 else []), inc=sV)
        v_bfcrow = sV.n
        v_wfc = sDW.n
        qsl_free_f = (sT, 0)
        h_free = {0: (sC, v_s_done_C), 1: (sC, v_s_done_C)}
        for i in range(NT):
            em.emit("sync", lambda e, i=i: e.dma_start(
                out=qsl[:], in_=xqt_s[:, i, :, :].rearrange("t p n -> p t n")),
                waits=[(sDW, v_wfc), qsl_free_f]
                + ([(sDP, v_cDP), st_qsl_free[0]] if i == 0 else []),
                inc=sDQ, inc_by=16)
            v_xq = sDQ.n
            for c in range(8):
                bk = c % 2
                pp = pA if bk == 0 else pB
                for k in range(ND):
                    w = [(sDQ, v_xq)] if (c == 0 and k == 0) else []
                    if k == 0:
                        w = w + [h_free[bk]]
                    em.emit("tensor", lambda e, c=c, k=k, pp=pp: e.matmul(
                        out=pp[:], lhsT=qsl[:, k, :], rhs=wfcT[:, k, c * 512:(c + 1) * 512],
                        start=(k == 0), stop=False), waits=w)
                em.emit("tensor", lambda e, c=c, pp=pp: e.matmul(
                    out=pp[:], lhsT=onr[:], rhs=bfc_rowb[0:1, c * 512:(c + 1) * 512],
                    start=False, stop=True),
                    waits=[(sV, v_bfcrow)] if (i == 0 and c == 0) else [], inc=sT)
                em.emit("scalar", lambda e, c=c, pp=pp: e.activation(
                    out=hgb[:, c * 512:(c + 1) * 512], in_=pp[:], func=AF.Gelu),
                    waits=[(sT, sT.n)], inc=sC)
                h_free[bk] = (sC, sC.n)
                em.emit("vector", lambda e, i=i, c=c: e.reduce_max(
                    out=amax_v[:, i, c * 16:(c + 1) * 16],
                    in_=hgb[:, c * 512:(c + 1) * 512].rearrange("p (g k) -> p g k", k=32),
                    axis=AX.X, apply_absolute_value=True),
                    waits=[(sC, sC.n)], inc=sV)
            qsl_free_f = (sT, sT.n)
            em.emit("gpsimd", lambda e, i=i: e.dma_start(
                out=hg_s[i, :, :], in_=hgb[:]), waits=[(sC, sC.n)], inc=sDP, inc_by=16)
            if debug_taps and i == 0:
                em.emit("sync", lambda e: e.dma_start(out=taps["t_hg"], in_=hgb[:]),
                        waits=[(sC, sC.n)], inc=sDS, inc_by=16)
            em.emit("scalar", lambda e: e.nop(), waits=[(sDP, sDP.n), (sV, sV.n)])

        v_f15 = sV.n
        for ch in range(4):
            v_f15 = chainT(
                1024, amax_all[:, ch * 1024:(ch + 1) * 1024],
                T_all[:].rearrange("p t n -> p (t n)")[:, ch * 1024:(ch + 1) * 1024],
                f15[0], f15[1], f15[2], rnd15[:, 0:1024], v_f15)
        v_sch = v_f15
        if debug_taps:
            em.emit("sync", lambda e: e.dma_start(
                out=taps["t_sc"], in_=R3[:, 21504:25600]),
                waits=[(sV, v_sch)], inc=sDS, inc_by=16)

        for t in range(NH):
            em.emit("sync", lambda e, t=t: e.dma_start(
                out=wprT[:, t, :].rearrange("p (i n) -> p i n", n=128),
                in_=wprt_s[t, :, :, :].rearrange("i p n -> p i n")),
                waits=[(sT, sT.n), (sDWp[0], v_wst[0]),
                       (sDWp[1], v_wst[1])] if t == 0 else [],
                inc=sDW, inc_by=16)
        for t in range(ND):
            em.emit("sync", lambda e, t=t: e.dma_start(
                out=wstage[:, 0:1024], in_=Wo_d[t * 128:(t + 1) * 128, :]),
                waits=[(sV, sV.n), (sT, sT.n)], inc=sDS, inc_by=16)
            em.emit("vector", lambda e, t=t: e.tensor_copy(
                out=Wo_sb[:, t, :], in_=wstage[:, 0:1024]),
                waits=[(sDS, sDS.n)], inc=sV)
        em.emit("sync", lambda e: e.dma_start(out=brow_st[:, 0:D], in_=bo_d),
                waits=[(sV, sV.n)], inc=sDS, inc_by=16)
        em.emit("vector", lambda e: e.tensor_copy(out=bo_rowb, in_=brow_st[0:1, 0:D]),
                waits=[(sDS, sDS.n), (sT, sT.n)], inc=sV)
        vv = sV.n
        em.emit("sync", lambda e: e.dma_start(out=brow_st[:, 0:D], in_=bpr_d),
                waits=[(sV, vv)], inc=sDS, inc_by=16)
        em.emit("vector", lambda e: e.tensor_copy(out=bpr_rowb, in_=brow_st[0:1, 0:D]),
                waits=[(sDS, sDS.n)], inc=sV)
        v_f2rows = sV.n
        v_f2w = sDW.n

        x_free = (sV, 0)
        y_free = (sDS, 0)
        asl_free = (sT, 0)
        hgbs = [hgb, r3(25600, 4096)]
        sDHp = [sDH, sDH2]
        v_hq_p = [0, 0]
        v_trend_p = [0, 0]
        st_hg = [0, 0]

        def emit_f2_load(i):
            hgb_p = hgbs[i % 2]
            em.emit("sync", lambda e, i=i, hgb_p=hgb_p: e.dma_start(
                out=hgb_p[:], in_=hg_s[i, :, :]),
                waits=[(sV, v_hq_p[i % 2])]
                + ([(sDW, v_f2w), (sDP, sDP.n)] if i == 0 else []),
                inc=sDHp[i % 2], inc_by=16)
            st_hg[i % 2] = sDHp[i % 2].n

        emit_f2_load(0)
        for i in range(NT):
            if i + 1 < NT:
                emit_f2_load(i + 1)
            hgb_p = hgbs[i % 2]
            hq_p = hq_b
            v_hg = st_hg[i % 2]
            em.emit("sync", lambda e, i=i: e.dma_start(
                out=asl[:], in_=at_s[:, i, :, :].rearrange("t p n -> p t n")),
                waits=[asl_free, (sDA, v_s_atdone)], inc=sDAL, inc_by=16)
            v_asl = sDAL.n
            em.emit("sync", lambda e, i=i: e.dma_start(
                out=xtile2[:], in_=x_d[i * 128:(i + 1) * 128, :]),
                waits=[x_free], inc=sDX, inc_by=16)
            v_x = sDX.n
            for ch in range(4):
                sl = slice(ch * 1024, (ch + 1) * 1024)
                em.emit("vector", lambda e, i=i, ch=ch, sl=sl, hgb_p=hgb_p,
                        hq_p=hq_p: e.tensor_tensor(
                    out=hq_p[:, sl].rearrange("p (g k) -> p g k", k=32),
                    in0=hgb_p[:, sl].rearrange("p (g k) -> p g k", k=32),
                    in1=T_all[:, i, ch * 32:(ch + 1) * 32].unsqueeze(-1)
                        .to_broadcast([128, 32, 32]),
                    op=ALU.add),
                    waits=[(sDHp[i % 2], v_hg)]
                    + ([(sV, v_sch), (sV, v_f2rows), (sDA, v_s_atdone)]
                       if (i == 0 and ch == 0) else [])
                    + ([(sT, v_trend_p[0])] if ch == 0 else []), ser=True)
                em.emit("vector", lambda e, i=i, ch=ch, sl=sl, hq_p=hq_p:
                        e.tensor_tensor(
                    out=hq_p[:, sl].rearrange("p (g k) -> p g k", k=32),
                    in0=hq_p[:, sl].rearrange("p (g k) -> p g k", k=32),
                    in1=T_all[:, i, ch * 32:(ch + 1) * 32].unsqueeze(-1)
                        .to_broadcast([128, 32, 32]),
                    op=ALU.subtract), inc=sV, ser=True)
            v_hq = sV.n
            v_hq_p[i % 2] = v_hq
            for h in range(8):
                _, ev_hqT = tr_batch(
                    4, lambda k, h=h, hq_p=hq_p:
                    hq_p[:, (h * 4 + k) * 128:(h * 4 + k + 1) * 128],
                    lambda k, h=h: hqT[:, h * 4 + k, :],
                    tr_waits=([(sV, v_hq)] if h == 0 else []))
            v_trend_p[0] = sT.n
            for dch in range(2):
                pp = pACC[dch]
                for kt in range(NH):
                    w = [ev_hqT] if kt == 0 else []
                    em.emit("tensor", lambda e, dch=dch, kt=kt, pp=pp: e.matmul(
                        out=pp[:], lhsT=hqT[:, kt, :], rhs=wprT[:, kt, dch * 512:(dch + 1) * 512],
                        start=(kt == 0), stop=False), waits=w)
                for t in range(ND):
                    w = [(sDAL, v_asl)] if t == 0 else []
                    em.emit("tensor", lambda e, dch=dch, t=t, pp=pp: e.matmul(
                        out=pp[:], lhsT=asl[:, t, :], rhs=Wo_sb[:, t, dch * 512:(dch + 1) * 512],
                        start=False, stop=False), waits=w)
                em.emit("tensor", lambda e, dch=dch, pp=pp: e.matmul(
                    out=pp[:], lhsT=onr[:], rhs=bpr_rowb[0:1, dch * 512:(dch + 1) * 512],
                    start=False, stop=False))
                em.emit("tensor", lambda e, dch=dch, pp=pp: e.matmul(
                    out=pp[:], lhsT=onr[:], rhs=bo_rowb[0:1, dch * 512:(dch + 1) * 512],
                    start=False, stop=True), inc=sT)
                em.emit("vector", lambda e, dch=dch, pp=pp: e.tensor_tensor(
                    out=ybuf[:, 0:512] if dch == 0 else ybuf[:, 512:1024],
                    in0=xtile2[:, dch * 512:(dch + 1) * 512],
                    in1=pp[:], op=ALU.add),
                    waits=[(sT, sT.n), (sDX, v_x), y_free], inc=sV)
            asl_free = (sT, sT.n)
            x_free = (sV, sV.n)
            v_y = sV.n
            em.emit("sync", lambda e, i=i: e.dma_start(
                out=y_d[i * 128:(i + 1) * 128, :], in_=ybuf[:]),
                waits=[(sV, v_y)], inc=sDY, inc_by=16)
            y_free = (sDY, sDY.n)

        em.emit("sync", lambda e: e.nop(),
                waits=[(sDY, sDY.n), (sDS, sDS.n), (sDP, sDP.n), (sDA, sDA.n),
                       (sDX, sDX.n), (sDQ, sDQ.n), (sDV, sDV.n), (sDH, sDH.n),
                       (sDAL, sDAL.n), (sDW, sDW.n),
                       (sDWp[0], sDWp[0].n), (sDWp[1], sDWp[1].n),
                       (sVS, sVS.n), (sVC, sVC.n),
                       (sDC[0], sDC[0].n), (sDC[1], sDC[1].n),
                       (sDC2[0], sDC2[0].n), (sDC2[1], sDC2[1].n),
                       (sDQ2, sDQ2.n), (sDH2, sDH2.n)])

        with nc.Block() as block:
            @block.sync
            def _(eng):
                for f in em.ops["sync"]:
                    f(eng)

            @block.vector
            def _(eng):
                for f in em.ops["vector"]:
                    f(eng)

            @block.scalar
            def _(eng):
                for f in em.ops["scalar"]:
                    f(eng)

            @block.tensor
            def _(eng):
                for f in em.ops["tensor"]:
                    f(eng)

            @block.gpsimd
            def _(eng):
                for f in em.ops["gpsimd"]:
                    f(eng)

    return nc


def host_inputs(inp):
    x = np.ascontiguousarray(np.asarray(inp["L0_attn_x"], dtype=np.float32))

    def f32(k):
        return np.ascontiguousarray(np.asarray(inp[k], dtype=np.float32))

    Wq, Wk, Wv, Wo = f32("L0_attn_Wq"), f32("L0_attn_Wk"), f32("L0_attn_Wv"), f32("L0_attn_Wo")
    bq, bk, bv, bo = f32("L0_attn_bq"), f32("L0_attn_bk"), f32("L0_attn_bv"), f32("L0_attn_bo")
    wfc, bfc = f32("L0_ffn_w_fc"), f32("L0_ffn_b_fc")
    wpr, bpr = f32("L0_ffn_w_proj"), f32("L0_ffn_b_proj")

    ident = np.eye(128, dtype=np.float32).astype(BF)
    r = np.arange(128)[:, None]
    cc = np.arange(512)[None, :]
    cmask = np.stack([np.where(cc > (o * 128 + r), np.float32(NEG), np.float32(0.0))
                      for o in range(4)])
    cmask = np.ascontiguousarray(cmask.transpose(1, 0, 2)).astype(BF)
    onc = np.ones((128, 1), dtype=np.float32).astype(BF)
    onr = np.ones((1, 128), dtype=np.float32).astype(BF)

    shared = {
        "Wq": Wq, "Wk": Wk, "Wv": Wv, "Wo": Wo,
        "bqt": np.ascontiguousarray(bq.reshape(ND, 128).T),
        "bkt": np.ascontiguousarray(bk.reshape(ND, 128).T),
        "bvt": np.ascontiguousarray(bv.reshape(ND, 128).T),
        "bo_r": np.ascontiguousarray(bo.reshape(1, D)),
        "w_fc": wfc, "b_fc_r": np.ascontiguousarray(bfc.reshape(1, H)),
        "w_proj": wpr, "b_proj_r": np.ascontiguousarray(bpr.reshape(1, D)),
        "ident": ident, "cmask": cmask, "ones_col": onc, "ones_row": onr,
    }
    return [dict(shared, x=np.ascontiguousarray(x[b])) for b in range(8)]


def kernel(**inputs):
    in_maps = host_inputs(inputs)
    nc = build_nc(debug_taps=False)
    res = run_bass_kernel_spmd(nc, in_maps, core_ids=list(range(8)))
    return np.stack([np.asarray(r["y"], dtype=np.float32) for r in res.results], axis=0)
